# version 07
# baseline (speedup 1.0000x reference)
import numpy as np
import ml_dtypes
from contextlib import ExitStack

import concourse.bass as bass
import concourse.mybir as mybir
from concourse.bass_utils import run_bass_kernel_spmd
from concourse.bass_core_rust import IndirectOffsetOnAxis
from concourse.aot_env import allow_ici_all_to_all

F32 = mybir.dt.float32
BF16 = mybir.dt.bfloat16
I32 = mybir.dt.int32
AF = mybir.ActivationFunctionType
ALU = mybir.AluOpType

BATCH, S, D = 4, 4096, 1024
T = 2048
NCK = 8
NC = 8
NSLICE = 4
TS = 512
LN_EPS = 1e-5

PHASES = ["l0ffn", "l0attn", "l0ffn", "l1ffn", "l1attn", "l1ffn"]
N_SUB = len(PHASES)


class Prog:
    ENGINES = ("sync", "vector", "scalar", "tensor", "gpsimd")

    def __init__(self):
        self.ops = {e: [] for e in self.ENGINES}
        self.semv = {}
        self.sems = {}

    def bump(self, name, k=1):
        self.semv[name] = self.semv.get(name, 0) + k
        return self.semv[name]

    def val(self, name):
        return self.semv.get(name, 0)

    def i1(self, name):
        self.bump(name, 1)
        return [(name, 1)]

    def emit(self, eng, f, wait=(), inc=()):
        wait = tuple(wait)
        inc = tuple(inc)

        def run(e, f=f, wait=wait, inc=inc):
            for s, v in wait:
                e.wait_ge(self.sems[s], v)
            r = f(e)
            for s, a in inc:
                r = r.then_inc(self.sems[s], a)
        self.ops[eng].append(run)

    def dma(self, eng, out, in_, sem, wait=(), **kw):
        v = self.bump(sem, 16)
        self.emit(eng, lambda e, o=out, i=in_, kw=kw: e.dma_start(out=o, in_=i, **kw),
                  wait=wait, inc=[(sem, 16)])
        return v


class V2:

    def __init__(self, t, off, cols):
        self.t, self.off, self.cols = t, off, cols

    def __getitem__(self, idx):
        if isinstance(idx, slice):
            assert idx == slice(None)
            return self.t[:, self.off:self.off + self.cols]
        pr, cs = idx
        a = cs.start if cs.start is not None else 0
        b = cs.stop if cs.stop is not None else self.cols
        return self.t[pr, self.off + a:self.off + b]


class Arena:
    def __init__(self, t, cols):
        self.t, self.cols, self.off = t, cols, 0

    def take(self, n):
        n = (n + 15) // 16 * 16
        assert self.off + n <= self.cols, f"arena overflow: {self.off}+{n} > {self.cols}"
        v = V2(self.t, self.off, n)
        self.off += n
        return v

    def reset(self, off=0):
        self.off = off


def build_program(n_sub=N_SUB):
    nc = bass.Bass("TRN2", target_bir_lowering=False, debug=False)
    p = Prog()

    dt_in = {}

    def din(name, shape, dtype=F32):
        dt_in[name] = nc.dram_tensor(name, list(shape), dtype, kind="ExternalInput")
        return dt_in[name]

    x_in = din("x_in", [D, T])
    x0row = din("x0row", [128, 8])
    rw_t = din("rw_t", [D, 8])
    ldn = din("ldn", [D, 128])
    lup = din("lup", [128, D])
    c_oh8 = din("c_oh8", [8, 128])
    wg_r = din("wg_r", [24, 8, 128, 128], I32)
    wdkv = din("wdkv", [D, 128])
    wdq = din("wdq", [D, 128])
    wuk = din("wuk", [128, 768])
    wuv = din("wuv", [128, 1024])
    wuq = din("wuq", [128, 768])
    wqr = din("wqr", [128, 256])
    wkr = din("wkr", [D, 256])
    wo1 = din("wo1", [D, D])
    bdkv = din("bdkv", [128, 1])
    bdq = din("bdq", [128, 1])
    bkf = din("bkf", [128, 8])
    bqf = din("bqf", [128, 8])
    bkr_c = din("bkr_c", [128, 2])
    bqr_c = din("bqr_c", [128, 2])
    buv_c = din("buv_c", [128, 8])
    bo1_c = din("bo1_c", [128, 8])
    c_invf = din("c_invf", [128, 1])
    c_sign = din("c_sign", [128, 1])
    c_perm = din("c_perm", [128, 128], BF16)
    c_qoff = din("c_qoff", [1, 1])
    wq0 = din("wq0", [D, 512])
    wk0 = din("wk0", [D, 512])
    wv0 = din("wv0", [D, 512])
    wo0 = din("wo0", [D, D])
    rel0 = din("rel0", [2048, 64])
    bq0 = din("bq0", [128, 4])
    bk0 = din("bk0", [128, 4])
    bvf0 = din("bvf0", [128, 8])
    bo0 = din("bo0", [128, 8])
    c_ident = din("c_ident", [128, 128], BF16)
    cc0_in = nc.dram_tensor("cc0_in", [128, 16384], BF16)
    cc0_out = nc.dram_tensor("cc0_out", [2, 128, 16384], BF16)
    cc2_in = nc.dram_tensor("cc2_in", [512, 4096], BF16)
    cc2_out = nc.dram_tensor("cc2_out", [2, 512, 4096], BF16)
    c_half = din("c_half", [128, 2])
    SLABN = 768 + 32 * 128 * 2048 + 768
    slab_all = nc.dram_tensor("slab_all", [SLABN], BF16)
    cc1_in = nc.dram_tensor("cc1_in", [128, 16384], BF16)
    cc1_out = nc.dram_tensor("cc1_out", [2, 128, 16384], BF16)
    krd = nc.dram_tensor("krd", [2, 128, 4096], BF16)
    aud = nc.dram_tensor("aud", [8, 128, 2048], BF16)
    wu_r = din("wu_r", [24, 8, 128, 128], I32)
    wd_r = din("wd_r", [8, 24, 128, 128], I32)

    y_out = nc.dram_tensor("y_out", [D, T], F32, kind="ExternalOutput")
    import os as _os
    DUMP = _os.environ.get("DEV_DUMP", "") == "1"
    if DUMP:
        dbg = {n: nc.dram_tensor(n, sh, BF16, kind="ExternalOutput")
               for n, sh in [("dck", [128, 4096]), ("dcq", [128, 2048]),
                             ("dkw", [128, 4096]), ("dqw", [128, 2048]),
                             ("dv65", [128, 4160]), ("dau", [128, 16384]),
                             ("dwof", [128, 8192]), ("dp0", [128, 512]),
                             ("de0", [128, 512]), ("df0", [128, 512]),
                             ("dm0", [128, 512]), ("deb", [128, 2048]), ("debs", [128, 1024]),
                             ("dauh", [64, 4096]), ("dq0", [128, 4096]), ("dk0", [128, 4096])]}
        dbg["dxpre"] = nc.dram_tensor("dxpre", [128, 16384], F32, kind="ExternalOutput")
        dbg["dxent"] = nc.dram_tensor("dxent", [128, 16384], F32, kind="ExternalOutput")

    with ExitStack() as ctx:
        E = ctx.enter_context

        x_res = E(nc.sbuf_tensor("x_res", [128, NCK * T], F32))
        ybuf = E(nc.sbuf_tensor("ybuf", [128, NCK * T], BF16))
        sq_sb = E(nc.sbuf_tensor("sq_sb", [128, NCK * TS], BF16))
        dn_sb = E(nc.sbuf_tensor("dn_sb", [128, T], BF16))
        ones_bf = E(nc.sbuf_tensor("ones_bf", [128, 2], BF16))
        ones_f = E(nc.sbuf_tensor("ones_f", [128, 2], F32))
        onesr_bf = E(nc.sbuf_tensor("onesr_bf", [1, 128], BF16))
        onesr_f = E(nc.sbuf_tensor("onesr_f", [1, 128], F32))
        eps_sb = E(nc.sbuf_tensor("eps_sb", [128, 1], F32))
        strow = E(nc.sbuf_tensor("strow", [1, 3 * TS], F32))
        x0_sb = E(nc.sbuf_tensor("x0_sb", [128, 8], F32))
        x0sq = E(nc.sbuf_tensor("x0sq", [128, 8], F32))
        ln0f = E(nc.sbuf_tensor("ln0f", [128, 8], F32))
        ln0b = E(nc.sbuf_tensor("ln0b", [128, 8], BF16))
        gsc = E(nc.sbuf_tensor("gsc", [1, 16], F32))
        grow = E(nc.sbuf_tensor("grow", [1, 8], F32))
        grow2 = E(nc.sbuf_tensor("grow2", [1, 8], F32))
        g8_sb = E(nc.sbuf_tensor("g8_sb", [8, 1], F32))
        g8b_sb = E(nc.sbuf_tensor("g8b_sb", [8, 1], F32))
        gb_sb = E(nc.sbuf_tensor("gb_sb", [8, 4], F32))
        gexp_sb = E(nc.sbuf_tensor("gexp_sb", [128, 1], F32))
        rw_sb = E(nc.sbuf_tensor("rw_sb", [128, 64], BF16))
        ldn_sb = E(nc.sbuf_tensor("ldn_sb", [128, 1024], BF16))
        lup_sb = E(nc.sbuf_tensor("lup_sb", [128, 1024], BF16))
        oh8_sb = E(nc.sbuf_tensor("oh8_sb", [8, 128], F32))
        invf_sb = E(nc.sbuf_tensor("invf_sb", [128, 1], F32))
        sign_sb = E(nc.sbuf_tensor("sign_sb", [128, 1], F32))
        perm_sb = E(nc.sbuf_tensor("perm_sb", [128, 128], BF16))
        zrow_sb = E(nc.sbuf_tensor("zrow_sb", [1, 2 * TS], BF16))
        zrowf_sb = E(nc.sbuf_tensor("zrowf_sb", [1, 2 * TS], F32))
        bvec_sb = E(nc.sbuf_tensor("bvec_sb", [128, 32], F32))
        bvec2_sb = E(nc.sbuf_tensor("bvec2_sb", [128, 8], F32))
        iot_sb = E(nc.sbuf_tensor("iot_sb", [128, TS], F32))
        qoffc_sb = E(nc.sbuf_tensor("qoffc_sb", [128, 1], F32))
        qoffr_sb = E(nc.sbuf_tensor("qoffr_sb", [1, 1], F32))
        qoffi_sb = E(nc.sbuf_tensor("qoffi_sb", [1, 1], mybir.dt.int32))
        f0_sb = E(nc.sbuf_tensor("f0_sb", [128, 32], F32))
        zpt_sb = E(nc.sbuf_tensor("zpt_sb", [128, 10], F32))
        zqt_sb = E(nc.sbuf_tensor("zqt_sb", [128, 1], F32))
        bvb_sb = E(nc.sbuf_tensor("bvb_sb", [128, 8], BF16))
        au0_sb = E(nc.sbuf_tensor("au0_sb", [128, 8], BF16))
        d0g_sb = E(nc.sbuf_tensor("d0g_sb", [128, 1], BF16))
        maski_sb = E(nc.sbuf_tensor("maski_sb", [128, 4 * 512], mybir.dt.int8))
        AB = E(nc.sbuf_tensor("AB", [128, 36864], BF16))

        pb = [E(nc.psum_tensor(f"pb{i}", [128, 512], F32)) for i in range(7)]
        pbt = E(nc.psum_tensor("pbt", [128, 1024], BF16))

        for sname in ("dS", "dA", "dG", "t", "v", "s", "g", "c", "st",
                      "w0", "w1", "w2", "u0", "u1", "f0", "f1",
                      "e0", "e1", "e2", "e3", "r0", "a0"):
            p.sems[sname] = E(nc.semaphore(sname))

        p.dma("sync", x_res[:], bass.AP(x_in, 0, [[T, 128], [128 * T, NCK], [1, T]]), "dS")
        p.dma("sync", x0_sb[:], x0row.ap(), "dS")
        p.dma("sync", oh8_sb[:], c_oh8.ap(), "dS")
        p.dma("gpsimd", rw_sb[:], bass.AP(rw_t, 0, [[8, 128], [8 * 128, NCK], [1, 8]]), "dG")
        p.dma("gpsimd", ldn_sb[:], bass.AP(ldn, 0, [[128, 128], [128 * 128, NCK], [1, 128]]), "dG")
        p.dma("gpsimd", lup_sb[:], lup.ap(), "dG")
        dG_w = p.val("dG")
        dS_x = 16
        dS_all = p.val("dS")

        p.emit("vector", lambda e: e.memset(ones_bf[:], 1.0))
        p.emit("vector", lambda e: e.memset(ones_f[:], 1.0))
        p.emit("vector", lambda e: e.memset(onesr_bf[:], 1.0))
        p.emit("vector", lambda e: e.memset(onesr_f[:], 1.0))
        p.emit("vector", lambda e: e.memset(eps_sb[:], LN_EPS), inc=p.i1("v"))
        v_const = p.val("v")

        def VW():
            return ("v", p.val("v"))

        def SW():
            return ("s", p.val("s"))

        def TW():
            return ("t", p.val("t"))

        def emit_layernorm(ready):
            for ts in range(NSLICE):
                for ck in range(NCK):
                    co = ck * T + ts * TS
                    p.emit("scalar",
                           lambda e, co=co: e.copy(ybuf[:, co:co + TS], x_res[:, co:co + TS]),
                           wait=(ready if (ts == 0 and ck == 0) else ()),
                           inc=(p.i1("s") if ck == NCK - 1 else ()))
                s_cast = p.val("s")
                for ck in range(NCK):
                    co = ck * T + ts * TS
                    sl = ck * TS
                    p.emit("scalar",
                           lambda e, co=co, sl=sl: e.activation(
                               sq_sb[:, sl:sl + TS], x_res[:, co:co + TS], AF.Square),
                           wait=([TW()] if ck == 0 and ts > 0 else ()),
                           inc=(p.i1("s") if ck == NCK - 1 else ()))
                s_sq = p.val("s")
                for ck in range(NCK):
                    co = ck * T + ts * TS
                    p.emit("tensor",
                           lambda e, co=co, ck=ck: e.matmul(
                               pb[4][0:1, :], ones_bf[:, 0:1], ybuf[:, co:co + TS],
                               start=(ck == 0), stop=(ck == NCK - 1)),
                           wait=([("s", s_cast), VW()] if ck == 0 else ()))
                for ck in range(NCK):
                    sl = ck * TS
                    p.emit("tensor",
                           lambda e, sl=sl, ck=ck: e.matmul(
                               pb[5][0:1, :], ones_bf[:, 0:1], sq_sb[:, sl:sl + TS],
                               start=(ck == 0), stop=(ck == NCK - 1)),
                           wait=([("s", s_sq)] if ck == 0 else ()),
                           inc=(p.i1("t") if ck == NCK - 1 else ()))
                t_stats = p.val("t")

                mrow = strow[0:1, 0:TS]
                rrow = strow[0:1, TS:2 * TS]
                vrow = strow[0:1, 2 * TS:3 * TS]
                p.emit("vector",
                       lambda e: e.tensor_scalar(mrow, pb[4][0:1, :], 1.0 / D, None, ALU.mult),
                       wait=[("t", t_stats)], inc=p.i1("v"))
                p.emit("vector", lambda e: e.tensor_mul(vrow, mrow, mrow),
                       wait=[VW()], inc=p.i1("v"))
                p.emit("vector",
                       lambda e: e.scalar_tensor_tensor(vrow, pb[5][0:1, :], 1.0 / D, vrow,
                                                        ALU.mult, ALU.subtract),
                       wait=[VW()], inc=p.i1("v"))
                v_var = p.val("v")
                p.emit("scalar",
                       lambda e: e.activation(vrow, vrow, AF.Sqrt, bias=eps_sb[0:1, 0:1]),
                       wait=[("v", v_var)], inc=p.i1("s"))
                s_sd = p.val("s")
                p.emit("vector", lambda e: e.reciprocal(rrow, vrow),
                       wait=[("s", s_sd)], inc=p.i1("v"))
                v_r = p.val("v")
                p.emit("tensor",
                       lambda e: e.matmul(pb[6][:, :], onesr_f[0:1, :], mrow,
                                          start=True, stop=True),
                       wait=[("v", v_r)])
                p.emit("tensor",
                       lambda e: e.matmul(pb[4][:, :], onesr_f[0:1, :], rrow,
                                          start=True, stop=True),
                       inc=p.i1("t"))
                t_bc = p.val("t")
                for ck in range(NCK):
                    co = ck * T + ts * TS
                    p.emit("vector",
                           lambda e, co=co: e.scalar_tensor_tensor(
                               ybuf[:, co:co + TS], x_res[:, co:co + TS], 1.0,
                               pb[6][:, :], ALU.mult, ALU.subtract),
                           wait=([("t", t_bc)] if ck == 0 else ()),
                           inc=(p.i1("v") if ck == NCK - 1 else ()))
                v_p1 = p.val("v")
                for ck in range(NCK):
                    co = ck * T + ts * TS
                    p.emit("vector",
                           lambda e, co=co: e.tensor_mul(
                               ybuf[:, co:co + TS], ybuf[:, co:co + TS], pb[4][:, :]),
                           wait=([("v", v_p1)] if ck == 0 else ()),
                           inc=(p.i1("v") if ck == NCK - 1 else ()))
            return p.val("v")

        def emit_gates(ready):
            def ve(f, wait=()):
                p.emit("vector", f, wait=list(wait) + [VW()], inc=p.i1("v"))

            def se(f, wait=()):
                p.emit("scalar", f, wait=list(wait) + [SW()], inc=p.i1("s"))

            se(lambda e: e.activation(x0sq[:], x0_sb[:], AF.Square), wait=ready)
            s_sq = p.val("s")
            p.emit("tensor",
                   lambda e: e.matmul(pb[0][0:1, 0:8], ones_f[:, 0:1], x0_sb[:],
                                      start=True, stop=True),
                   wait=[("s", s_sq), VW()])
            p.emit("tensor",
                   lambda e: e.matmul(pb[0][0:1, 8:16], ones_f[:, 0:1], x0sq[:],
                                      start=True, stop=True),
                   inc=p.i1("t"))
            t_s = p.val("t")
            ve(lambda e: e.reduce_sum(gsc[0:1, 0:1], pb[0][0:1, 0:8],
                                      axis=mybir.AxisListType.X), wait=[("t", t_s)])
            ve(lambda e: e.reduce_sum(gsc[0:1, 1:2], pb[0][0:1, 8:16],
                                      axis=mybir.AxisListType.X))
            ve(lambda e: e.tensor_scalar(gsc[0:1, 0:1], gsc[0:1, 0:1], 1.0 / D, None, ALU.mult))
            ve(lambda e: e.tensor_mul(gsc[0:1, 2:3], gsc[0:1, 0:1], gsc[0:1, 0:1]))
            ve(lambda e: e.scalar_tensor_tensor(gsc[0:1, 2:3], gsc[0:1, 1:2], 1.0 / D,
                                                gsc[0:1, 2:3], ALU.mult, ALU.subtract))
            se(lambda e: e.activation(gsc[0:1, 2:3], gsc[0:1, 2:3], AF.Sqrt,
                                      bias=eps_sb[0:1, 0:1]), wait=[VW()])
            ve(lambda e: e.reciprocal(gsc[0:1, 3:4], gsc[0:1, 2:3]), wait=[SW()])
            v_r0 = p.val("v")
            p.emit("tensor", lambda e: e.matmul(pb[1][:, 0:1], onesr_f[0:1, :],
                                                gsc[0:1, 0:1], start=True, stop=True),
                   wait=[("v", v_r0)])
            p.emit("tensor", lambda e: e.matmul(pb[1][:, 1:2], onesr_f[0:1, :],
                                                gsc[0:1, 3:4], start=True, stop=True),
                   inc=p.i1("t"))
            t_bc = p.val("t")
            ve(lambda e: e.tensor_scalar(ln0f[:], x0_sb[:], pb[1][:, 0:1],
                                         pb[1][:, 1:2], ALU.subtract, ALU.mult),
               wait=[("t", t_bc)])
            se(lambda e: e.copy(ln0b[:], ln0f[:]), wait=[VW()])
            s_ln0 = p.val("s")
            for ck in range(NCK):
                p.emit("tensor",
                       lambda e, ck=ck: e.matmul(pb[2][0:8, 0:1],
                                                 rw_sb[:, ck * 8:(ck + 1) * 8],
                                                 ln0b[:, ck:ck + 1],
                                                 start=(ck == 0), stop=(ck == NCK - 1)),
                       wait=([("s", s_ln0), ("dG", dG_w)] if ck == 0 else ()))
            for ck in range(NCK):
                p.emit("tensor",
                       lambda e, ck=ck: e.matmul(pb[3][0:1, 0:8],
                                                 ln0b[:, ck:ck + 1],
                                                 rw_sb[:, ck * 8:(ck + 1) * 8],
                                                 start=(ck == 0), stop=(ck == NCK - 1)),
                       inc=(p.i1("t") if ck == NCK - 1 else ()))
            t_log = p.val("t")
            ve(lambda e: e.reduce_max(gsc[0:1, 4:5], pb[3][0:1, 0:8],
                                      axis=mybir.AxisListType.X), wait=[("t", t_log)])
            ve(lambda e: e.tensor_scalar(gsc[0:1, 5:6], gsc[0:1, 4:5], -1.0, None, ALU.mult))
            ve(lambda e: e.tensor_scalar(grow[0:1, :], pb[3][0:1, 0:8],
                                         gsc[0:1, 4:5], None, ALU.is_equal))
            ve(lambda e: e.scalar_tensor_tensor(grow[0:1, :], grow[0:1, :], -1e30,
                                                pb[3][0:1, 0:8], ALU.mult, ALU.add))
            ve(lambda e: e.reduce_max(gsc[0:1, 6:7], grow[0:1, :],
                                      axis=mybir.AxisListType.X))
            se(lambda e: e.activation(grow[0:1, :], pb[3][0:1, 0:8], AF.Exp,
                                      bias=gsc[0:1, 5:6]), wait=[VW()])
            ve(lambda e: e.tensor_scalar(grow2[0:1, :], pb[3][0:1, 0:8],
                                         gsc[0:1, 6:7], None, ALU.is_ge), wait=[SW()])
            ve(lambda e: e.tensor_mul(grow[0:1, :], grow[0:1, :], grow2[0:1, :]))
            ve(lambda e: e.reduce_sum(gsc[0:1, 7:8], grow[0:1, :],
                                      axis=mybir.AxisListType.X))
            ve(lambda e: e.tensor_scalar(gsc[0:1, 9:10], gsc[0:1, 7:8], 2.0, None, ALU.mult))
            ve(lambda e: e.reciprocal(gsc[0:1, 8:9], gsc[0:1, 9:10]))
            v_rz = p.val("v")
            p.emit("tensor", lambda e: e.matmul(pb[1][0:8, 2:3], onesr_f[0:1, 0:8],
                                                gsc[0:1, 5:6], start=True, stop=True),
                   wait=[("v", v_rz)])
            p.emit("tensor", lambda e: e.matmul(pb[1][0:8, 3:4], onesr_f[0:1, 0:8],
                                                gsc[0:1, 6:7], start=True, stop=True))
            p.emit("tensor", lambda e: e.matmul(pb[1][0:8, 4:5], onesr_f[0:1, 0:8],
                                                gsc[0:1, 8:9], start=True, stop=True),
                   inc=p.i1("t"))
            t_bc2 = p.val("t")
            ve(lambda e: e.tensor_copy(gb_sb[:, 0:3], pb[1][0:8, 2:5]), wait=[("t", t_bc2)])
            se(lambda e: e.activation(g8_sb[:], pb[2][0:8, 0:1], AF.Exp,
                                      bias=gb_sb[0:8, 0:1]), wait=[VW()])
            ve(lambda e: e.tensor_scalar(g8b_sb[:], pb[2][0:8, 0:1],
                                         gb_sb[0:8, 1:2], None, ALU.is_ge), wait=[SW()])
            ve(lambda e: e.tensor_mul(g8_sb[:], g8_sb[:], g8b_sb[:]))
            ve(lambda e: e.tensor_scalar(g8_sb[:], g8_sb[:], gb_sb[0:8, 2:3], None, ALU.mult))
            v_g8 = p.val("v")
            p.emit("tensor", lambda e: e.matmul(pb[1][:, 5:6], oh8_sb[:], g8_sb[:],
                                                start=True, stop=True),
                   wait=[("v", v_g8)], inc=p.i1("t"))
            t_ge = p.val("t")
            ve(lambda e: e.tensor_copy(gexp_sb[:], pb[1][:, 5:6]), wait=[("t", t_ge)])
            return p.val("v")

        def emit_x0_track():
            for ck in range(NCK):
                p.emit("tensor",
                       lambda e, ck=ck: e.matmul(
                           pb[0][:, 0:1], ldn_sb[:, ck * 128:(ck + 1) * 128],
                           ln0b[:, ck:ck + 1], start=(ck == 0), stop=(ck == NCK - 1)),
                       wait=([VW(), SW()] if ck == 0 else ()),
                       inc=(p.i1("t") if ck == NCK - 1 else ()))
            t_d0 = p.val("t")
            p.emit("vector",
                   lambda e: e.tensor_scalar(d0g_sb[:], pb[0][:, 0:1], gexp_sb[:, 0:1],
                                             None, ALU.mult),
                   wait=[("t", t_d0)], inc=p.i1("v"))
            v_d0 = p.val("v")
            for f in range(NCK):
                p.emit("tensor",
                       lambda e, f=f: e.matmul(
                           pb[1][:, f:f + 1], lup_sb[:, f * 128:(f + 1) * 128],
                           d0g_sb[:], start=True, stop=True),
                       wait=([("v", v_d0)] if f == 0 else ()),
                       inc=(p.i1("t") if f == NCK - 1 else ()))
            t_w0 = p.val("t")
            p.emit("vector",
                   lambda e: e.scalar_tensor_tensor(x0_sb[:], ln0f[:], 0.5, x0_sb[:],
                                                    ALU.mult, ALU.add),
                   wait=[("t", t_w0)], inc=p.i1("v"))
            p.emit("vector",
                   lambda e: e.tensor_add(x0_sb[:], x0_sb[:], pb[1][:, 0:8]),
                   wait=[VW()], inc=p.i1("v"))

        def emit_l0_ffn():
            for ts in range(NSLICE):
                bank = pb[ts % 2]
                for ck in range(NCK):
                    co = ck * T + ts * TS
                    p.emit("tensor",
                           lambda e, co=co, ck=ck, bank=bank: e.matmul(
                               bank[:, :], ldn_sb[:, ck * 128:(ck + 1) * 128],
                               ybuf[:, co:co + TS],
                               start=(ck == 0), stop=(ck == NCK - 1)),
                           wait=([VW()] if ck == 0 else ()),
                           inc=(p.i1("t") if ck == NCK - 1 else ()))
                t_dn = p.val("t")
                p.emit("vector",
                       lambda e, ts=ts, bank=bank: e.tensor_scalar(
                           dn_sb[:, ts * TS:ts * TS + TS], bank[:, :], gexp_sb[:, 0:1],
                           None, ALU.mult),
                       wait=[("t", t_dn)], inc=p.i1("v"))
            v_dn = p.val("v")
            for ck in range(NCK):
                for ts in range(NSLICE):
                    co = ck * T + ts * TS
                    last = (ck == NCK - 1 and ts == NSLICE - 1)
                    p.emit("vector",
                           lambda e, co=co: e.scalar_tensor_tensor(
                               x_res[:, co:co + TS], ybuf[:, co:co + TS], 0.5,
                               x_res[:, co:co + TS], ALU.mult, ALU.add),
                           inc=(p.i1("v") if last else ()))
            v_pa = p.val("v")
            for ts in range(NSLICE):
                for f in range(NCK):
                    bank = pb[2 + (f % 2)]
                    co = f * T + ts * TS
                    p.emit("tensor",
                           lambda e, f=f, ts=ts, bank=bank: e.matmul(
                               bank[:, :], lup_sb[:, f * 128:(f + 1) * 128],
                               dn_sb[:, ts * TS:ts * TS + TS],
                               start=True, stop=True),
                           wait=[VW()],
                           inc=p.i1("t"))
                    t_up = p.val("t")
                    p.emit("vector",
                           lambda e, co=co, bank=bank: e.scalar_tensor_tensor(
                               x_res[:, co:co + TS], bank[:, :], 1.0,
                               x_res[:, co:co + TS], ALU.mult, ALU.add),
                           wait=[("t", t_up), ("v", v_pa)],
                           inc=p.i1("v"))

        def emit_l1_ffn():
            DSC = 0.5 / (127.0 * 127.0)
            ar = Arena(AB, 36864)
            wgc = [ar.take(1024) for _ in range(3)]
            wuc = [ar.take(1024) for _ in range(3)]
            wdc = [ar.take(12 * 128) for _ in range(2)]
            slu = [ar.take(TS) for _ in range(2)]
            hp_sb = ar.take(24 * 1024)
            last_wt = {0: p.val("t"), 1: p.val("t"), 2: p.val("t")}
            for hb in range(2):
                for m in range(24):
                    r = m % 3
                    wsem = f"w{r}"
                    p.dma("gpsimd", wgc[r][:],
                          bass.AP(wg_r, m * 8 * 128 * 128,
                                  [[128, 128], [128 * 128, 8], [1, 128]]),
                          wsem, wait=[("t", last_wt[r])])
                    p.dma("gpsimd", wuc[r][:],
                          bass.AP(wu_r, m * 8 * 128 * 128,
                                  [[128, 128], [128 * 128, 8], [1, 128]]),
                          wsem)
                    w_tot = p.val(wsem)
                    for tsl in range(2):
                        ts = 2 * hb + tsl
                        gbank = pb[4 + tsl]
                        ubank = pb[2 + tsl]
                        for ck in range(NCK):
                            co = ck * T + ts * TS
                            p.emit("tensor",
                                   lambda e, co=co, ck=ck, gbank=gbank, r=r: e.matmul(
                                       gbank[:, :], wgc[r][:, ck * 128:(ck + 1) * 128],
                                       ybuf[:, co:co + TS],
                                       start=(ck == 0), stop=(ck == NCK - 1)),
                                   wait=([(wsem, w_tot), SW(), VW()] if ck == 0 else ()))
                        for ck in range(NCK):
                            co = ck * T + ts * TS
                            p.emit("tensor",
                                   lambda e, co=co, ck=ck, ubank=ubank, r=r: e.matmul(
                                       ubank[:, :], wuc[r][:, ck * 128:(ck + 1) * 128],
                                       ybuf[:, co:co + TS],
                                       start=(ck == 0), stop=(ck == NCK - 1)),
                                   wait=([VW()] if ck == 0 else ()),
                                   inc=(p.i1("t") if ck == NCK - 1 else ()))
                        t_gu = p.val("t")
                        p.emit("scalar",
                               lambda e, tsl=tsl, gbank=gbank: e.activation(
                                   slu[tsl][:], gbank[:, :], AF.Silu, scale=1.0 / 127.0),
                               wait=[("t", t_gu)], inc=p.i1("s"))
                        s_sl = p.val("s")
                        hco = m * 1024 + tsl * TS
                        p.emit("vector",
                               lambda e, hco=hco, tsl=tsl, ubank=ubank: e.tensor_mul(
                                   hp_sb[:, hco:hco + TS], slu[tsl][:], ubank[:, :]),
                               wait=[("s", s_sl)], inc=p.i1("v"))
                    last_wt[r] = p.val("t")
                for fh in range(2 * NCK):
                    f, half = fh // 2, fh % 2
                    r2 = fh % 2
                    usem = f"u{r2}"
                    p.dma("gpsimd", wdc[r2][:],
                          bass.AP(wd_r, (f * 24 + half * 12) * 128 * 128,
                                  [[128, 128], [128 * 128, 12], [1, 128]]),
                          usem, wait=[TW()])
                    u_tot = p.val(usem)
                    for tsl in range(2):
                        ts = 2 * hb + tsl
                        dbank = pb[2 * (f % 2) + tsl]
                        for kh in range(12):
                            khg = half * 12 + kh
                            hco = khg * 1024 + tsl * TS
                            p.emit("tensor",
                                   lambda e, hco=hco, kh=kh, dbank=dbank, r2=r2, half=half: e.matmul(
                                       dbank[:, :], wdc[r2][:, kh * 128:(kh + 1) * 128],
                                       hp_sb[:, hco:hco + TS],
                                       start=(kh == 0 and half == 0),
                                       stop=(kh == 11 and half == 1)),
                                   wait=([(usem, u_tot)] + ([VW()] if half == 0 else [])
                                         if kh == 0 else ()),
                                   inc=(p.i1("t") if (kh == 11 and half == 1) else ()))
                        if half == 1:
                            t_dn = p.val("t")
                            co = f * T + ts * TS
                            p.emit("vector",
                                   lambda e, co=co, dbank=dbank: e.scalar_tensor_tensor(
                                       x_res[:, co:co + TS], dbank[:, :], DSC,
                                       x_res[:, co:co + TS], ALU.mult, ALU.add),
                                   wait=[("t", t_dn), VW()], inc=p.i1("v"))

        def emit_l1_attn():
            if DUMP:
                p.dma("sync", dbg["dxent"].ap(), x_res[:], "dS",
                      wait=[VW(), SW(), TW()])
            SC8 = 1.0 / 8.0
            TWOPI = 6.283185307179586
            PI = 3.141592653589793
            HPI = 1.5707963267948966
            ar = Arena(AB, 36864)
            buvb = ar.take(8)
            ckv = ar.take(4096)
            cqv = ar.take(2048)
            COSq = ar.take(2048)
            RBq = ar.take(2048)
            COS1 = ar.take(512)
            RB1 = ar.take(512)
            rt1 = ar.take(512)
            rt2 = ar.take(512)
            wuk_sb = ar.take(768)
            wuv_sb = ar.take(1024)
            wuq_sb = ar.take(768)
            wqr_sb = ar.take(256)
            mark_pers = ar.off
            ya = ar.take(4096)
            rraw = ar.take(4096)
            wdkv_sb = ar.take(1024)
            wdq_sb = ar.take(1024)
            wkr_sb = ar.take(1024)

            p.dma("gpsimd", wdkv_sb[:], bass.AP(wdkv, 0, [[128, 128], [128 * 128, 8], [1, 128]]), "dG", wait=[TW(), VW()])
            p.dma("gpsimd", wdq_sb[:], bass.AP(wdq, 0, [[128, 128], [128 * 128, 8], [1, 128]]), "dG")
            p.dma("gpsimd", wuk_sb[:], wuk.ap(), "dG")
            p.dma("gpsimd", wuv_sb[:], wuv.ap(), "dG")
            p.dma("gpsimd", wuq_sb[:], wuq.ap(), "dG")
            p.dma("gpsimd", wqr_sb[:], wqr.ap(), "dG")
            p.dma("gpsimd", perm_sb[:], c_perm.ap(), "dG")
            p.dma("gpsimd", buvb[0:128, 0:8], buv_c.ap(), "dG")
            dG_w1 = p.val("dG")
            p.dma("sync", bvec_sb[:, 0:1], bdkv.ap(), "dS", wait=[VW(), SW()])
            p.dma("sync", bvec_sb[:, 1:2], bdq.ap(), "dS")
            p.dma("sync", bvec_sb[:, 2:10], bkf.ap(), "dS")
            p.dma("sync", bvec_sb[:, 10:18], bqf.ap(), "dS")
            p.dma("sync", bvec_sb[:, 18:20], bkr_c.ap(), "dS")
            p.dma("sync", bvec_sb[:, 20:22], bqr_c.ap(), "dS")
            p.dma("sync", bvec_sb[:, 22:30], bo1_c.ap(), "dS")
            p.dma("sync", invf_sb[:], c_invf.ap(), "dS")
            p.dma("sync", sign_sb[:], c_sign.ap(), "dS")
            p.dma("sync", qoffr_sb[:], c_qoff.ap(), "dS")
            dS_b1 = p.val("dS")
            p.emit("tensor", lambda e: e.matmul(pb[2][:, 0:1], onesr_f[0:1, :],
                                                qoffr_sb[0:1, 0:1], start=True, stop=True),
                   wait=[("dS", dS_b1), VW()], inc=p.i1("t"))
            t_qo = p.val("t")
            p.emit("vector", lambda e: e.tensor_copy(qoffc_sb[:], pb[2][:, 0:1]),
                   wait=[("t", t_qo)], inc=p.i1("v"))

            v_ln = p.val("v")
            p.dma("gpsimd", cc1_in.ap(), ybuf[:], "u0", wait=[("v", v_ln)])
            dG_cc = p.val("u0")
            p.emit("gpsimd",
                   lambda e: e.collective_compute(
                       "AllGather", ALU.bypass,
                       replica_groups=[[0, 1], [2, 3], [4, 5], [6, 7]],
                       ins=[cc1_in.ap().opt()], outs=[cc1_out.ap().opt()]),
                   wait=[("u0", dG_cc)], inc=p.i1("c"))
            c_ag = p.val("c")

            def build_rope_slice(base, add_qoff, cosv, coff, rbv, roff):
                ang = iot_sb[:, 0:TS]
                p.emit("gpsimd",
                       lambda e, base=base: e.iota(ang, [[1, TS]], base=base,
                                                   channel_multiplier=0,
                                                   allow_small_or_imprecise_dtypes=True),
                       wait=[VW(), SW(), ("dS", dS_b1)], inc=p.i1("g"))
                g_i = p.val("g")
                if add_qoff:
                    p.emit("vector", lambda e: e.tensor_scalar(ang, ang, qoffc_sb[:, 0:1], None, ALU.add),
                           wait=[("g", g_i)], inc=p.i1("v"))
                else:
                    p.emit("vector", lambda e: e.tensor_copy(ang, ang),
                           wait=[("g", g_i)], inc=p.i1("v"))
                p.emit("vector", lambda e: e.tensor_scalar(ang, ang, invf_sb[:, 0:1], None, ALU.mult),
                       wait=[VW()], inc=p.i1("v"))
                for KK in (16 * TWOPI, 8 * TWOPI, 4 * TWOPI, 2 * TWOPI, TWOPI):
                    p.emit("vector", lambda e, KK=KK: e.tensor_scalar(rt1[:, 0:TS], ang, KK, None, ALU.is_ge),
                           wait=[VW()], inc=p.i1("v"))
                    p.emit("vector", lambda e, KK=KK: e.scalar_tensor_tensor(ang, rt1[:, 0:TS], -KK, ang,
                                                                             ALU.mult, ALU.add),
                           wait=[VW()], inc=p.i1("v"))
                p.emit("vector", lambda e: e.tensor_scalar(rt1[:, 0:TS], ang, PI, None, ALU.is_ge),
                       wait=[VW()], inc=p.i1("v"))
                p.emit("vector", lambda e: e.scalar_tensor_tensor(ang, rt1[:, 0:TS], -TWOPI, ang,
                                                                  ALU.mult, ALU.add),
                       wait=[VW()], inc=p.i1("v"))
                v_r = p.val("v")
                p.emit("scalar", lambda e: e.activation(rt2[:, 0:TS], ang, AF.Sin),
                       wait=[("v", v_r)], inc=p.i1("s"))
                p.emit("vector",
                       lambda e, rbv=rbv, roff=roff: e.tensor_scalar(
                           rbv[:, roff:roff + TS], rt2[:, 0:TS], sign_sb[:, 0:1], None, ALU.mult),
                       wait=[SW()], inc=p.i1("v"))
                p.emit("vector", lambda e: e.tensor_scalar(ang, ang, HPI, None, ALU.add),
                       wait=[VW()], inc=p.i1("v"))
                p.emit("vector", lambda e: e.tensor_scalar(rt1[:, 0:TS], ang, PI, None, ALU.is_ge),
                       wait=[VW()], inc=p.i1("v"))
                p.emit("vector", lambda e: e.scalar_tensor_tensor(ang, rt1[:, 0:TS], -TWOPI, ang,
                                                                  ALU.mult, ALU.add),
                       wait=[VW()], inc=p.i1("v"))
                v_c = p.val("v")
                p.emit("scalar",
                       lambda e, cosv=cosv, coff=coff: e.activation(
                           cosv[:, coff:coff + TS], ang, AF.Sin),
                       wait=[("v", v_c)], inc=p.i1("s"))

            def apply_rope_slice(buf, boff, nr, cosv, coff, rbv, roff):
                p.emit("tensor",
                       lambda e, boff=boff, nr=nr: e.matmul(
                           pb[3][0:nr, 0:TS], perm_sb[0:nr, 0:nr],
                           buf[0:nr, boff:boff + TS], start=True, stop=True),
                       wait=[VW(), SW(), ("dG", dG_w1)], inc=p.i1("t"))
                t_sh = p.val("t")
                p.emit("vector",
                       lambda e, nr=nr, rbv=rbv, roff=roff: e.tensor_mul(
                           rt1[0:nr, 0:TS], pb[3][0:nr, 0:TS], rbv[0:nr, roff:roff + TS]),
                       wait=[("t", t_sh)], inc=p.i1("v"))
                p.emit("vector",
                       lambda e, boff=boff, nr=nr, cosv=cosv, coff=coff: e.tensor_mul(
                           rt2[0:nr, 0:TS], buf[0:nr, boff:boff + TS], cosv[0:nr, coff:coff + TS]),
                       wait=[VW()], inc=p.i1("v"))
                p.emit("vector",
                       lambda e, boff=boff, nr=nr: e.tensor_add(
                           buf[0:nr, boff:boff + TS], rt1[0:nr, 0:TS], rt2[0:nr, 0:TS]),
                       wait=[VW()], inc=p.i1("v"))

            for ts2 in range(8):
                r, tsl = ts2 // 4, ts2 % 4
                p.dma("scalar", ya[:],
                      bass.AP(cc1_out, r * 128 * 16384 + tsl * TS,
                              [[16384, 128], [T, NCK], [1, TS]]),
                      "dA", wait=[("c", c_ag), TW()])
                dA_y = p.val("dA")
                bank = pb[ts2 % 2]
                for ck in range(NCK):
                    p.emit("tensor",
                           lambda e, ck=ck, bank=bank: e.matmul(
                               bank[:, :], wdkv_sb[:, ck * 128:(ck + 1) * 128],
                               ya[:, ck * TS:(ck + 1) * TS],
                               start=(ck == 0), stop=(ck == NCK - 1)),
                           wait=([("dA", dA_y), ("dG", dG_w1), VW()] if ck == 0 else ()),
                           inc=(p.i1("t") if ck == NCK - 1 else ()))
                t_ck = p.val("t")
                p.emit("vector",
                       lambda e, ts2=ts2, bank=bank: e.tensor_scalar(
                           ckv[:, ts2 * TS:(ts2 + 1) * TS], bank[:, :],
                           bvec_sb[:, 0:1], None, ALU.add),
                       wait=[("t", t_ck), ("dS", dS_b1)], inc=p.i1("v"))
            for tsl in range(4):
                bank = pb[2 + (tsl % 2)]
                for ck in range(NCK):
                    p.emit("tensor",
                           lambda e, ck=ck, tsl=tsl, bank=bank: e.matmul(
                               bank[:, :], wdq_sb[:, ck * 128:(ck + 1) * 128],
                               ybuf[:, ck * T + tsl * TS:ck * T + tsl * TS + TS],
                               start=(ck == 0), stop=(ck == NCK - 1)),
                           wait=([VW()] if ck == 0 else ()),
                           inc=(p.i1("t") if ck == NCK - 1 else ()))
                t_cq = p.val("t")
                p.emit("vector",
                       lambda e, tsl=tsl, bank=bank: e.tensor_scalar(
                           cqv[:, tsl * TS:(tsl + 1) * TS], bank[:, :],
                           bvec_sb[:, 1:2], None, ALU.add),
                       wait=[("t", t_cq)], inc=p.i1("v"))

            for rc in range(2):
                p.dma("gpsimd", wkr_sb[:],
                      bass.AP(wkr, rc * 128, [[256, 128], [256 * 128, 8], [1, 128]]),
                      "dG", wait=[TW()])
                dG_kr = p.val("dG")
                for ts2 in range(8):
                    r, tsl = ts2 // 4, ts2 % 4
                    p.dma("scalar", ya[:],
                          bass.AP(cc1_out, r * 128 * 16384 + tsl * TS,
                                  [[16384, 128], [T, NCK], [1, TS]]),
                          "dA", wait=[TW()])
                    dA_y = p.val("dA")
                    bank = pb[ts2 % 2]
                    for ck in range(NCK):
                        p.emit("tensor",
                               lambda e, ck=ck, bank=bank: e.matmul(
                                   bank[:, :], wkr_sb[:, ck * 128:(ck + 1) * 128],
                                   ya[:, ck * TS:(ck + 1) * TS],
                                   start=(ck == 0), stop=(ck == NCK - 1)),
                               wait=([("dA", dA_y), ("dG", dG_kr), VW()] if ck == 0 else ()),
                               inc=(p.i1("t") if ck == NCK - 1 else ()))
                    t_kr = p.val("t")
                    p.emit("vector",
                           lambda e, ts2=ts2, rc=rc, bank=bank: e.tensor_scalar(
                               rraw[:, ts2 * TS:(ts2 + 1) * TS], bank[:, :],
                               bvec_sb[:, 18 + rc:19 + rc], None, ALU.add),
                           wait=([("t", t_kr)] + ([("dS", p.val("dS"))] if ts2 == 0 else [])),
                           inc=p.i1("v"))
                for sl in range(8):
                    build_rope_slice(sl * TS, False, COS1, 0, RB1, 0)
                    apply_rope_slice(rraw, sl * TS, 128, COS1, 0, RB1, 0)
                v_rope = p.val("v")
                p.dma("sync", bass.AP(krd, rc * 128 * 4096, [[4096, 128], [1, 4096]]),
                      rraw[:], "dS", wait=[("v", v_rope)])
            dS_krd = p.val("dS")

            for sl in range(4):
                build_rope_slice(sl * TS, True, COSq, sl * TS, RBq, sl * TS)

            if DUMP:
                p.dma("sync", dbg["dck"].ap(), ckv[:], "dS", wait=[VW()])
                p.dma("sync", dbg["dcq"].ap(), cqv[:], "dS")
            ar.reset(mark_pers)
            kwv = ar.take(4096)
            qwv = ar.take(2048)
            v65 = ar.take(32 * 130)
            qraw = ar.take(2048)
            auwA = ar.take(2048)
            auwB = ar.take(2048)
            eT = [ar.take(512) for _ in range(4)]
            zbs = ar.take(512)
            ABC = 36864

            for w in range(8):
                hA, hB = 2 * w, 2 * w + 1
                p.emit("vector",
                       lambda e: e.memset(
                           bass.AP(AB, v65.off + 64, [[ABC, 128], [130, 32], [65, 2], [1, 1]]),
                           1.0),
                       wait=([TW()] + ([("dS", p.val("dS")), ("dA", p.val("dA")),
                                        ("dG", p.val("dG")), SW()] if (w == 0 or DUMP) else [])),
                       inc=p.i1("v"))
                for tt in range(32):
                    bank = pb[tt % 2]
                    p.emit("tensor",
                           lambda e, tt=tt, w=w, bank=bank: e.matmul(
                               bank[:, 0:128], ckv[:, tt * 128:(tt + 1) * 128],
                               wuv_sb[:, w * 128:(w + 1) * 128],
                               start=True, stop=True),
                           wait=[VW()], inc=p.i1("t"))
                    t_v = p.val("t")
                    p.emit("vector",
                           lambda e, tt=tt, bank=bank: e.tensor_copy(
                               bass.AP(AB, v65.off + tt * 130, [[ABC, 128], [65, 2], [1, 64]]),
                               bank[:, 0:128]),
                           wait=[("t", t_v)], inc=p.i1("v"))
                for ts2 in range(8):
                    bank = pb[2 + ts2 % 2]
                    for hh in range(2):
                        h = 2 * w + hh
                        p.emit("tensor",
                               lambda e, ts2=ts2, h=h, hh=hh, bank=bank: e.matmul(
                                   bank[64 * hh:64 * hh + 48, :],
                                   wuk_sb[:, h * 48:h * 48 + 48],
                                   ckv[:, ts2 * TS:(ts2 + 1) * TS],
                                   start=True, stop=True,
                                   tile_position=(0, 64 * hh)),
                               wait=([VW()] if hh == 0 else ()),
                               inc=(p.i1("t") if hh == 1 else ()))
                    t_kn = p.val("t")
                    for hh in range(2):
                        p.emit("vector",
                               lambda e, ts2=ts2, hh=hh, w=w, bank=bank: e.tensor_scalar(
                                   kwv[64 * hh:64 * hh + 48, ts2 * TS:(ts2 + 1) * TS],
                                   bank[64 * hh:64 * hh + 48, :],
                                   bvec_sb[64 * hh:64 * hh + 48, 2 + w:3 + w],
                                   None, ALU.add),
                               wait=([("t", t_kn)] if hh == 0 else ()),
                               inc=(p.i1("v") if hh == 1 else ()))
                v_kn = p.val("v")
                for hh in range(2):
                    h = 2 * w + hh
                    rc, rr = h // 8, (h % 8) * 16
                    p.dma("sync", kwv[48 + 64 * hh:64 + 64 * hh, 0:4096],
                          bass.AP(krd, rc * 128 * 4096 + rr * 4096, [[4096, 16], [1, 4096]]),
                          "dS", wait=[("v", v_kn), TW(), ("dS", p.val("dS"))])
                dS_krot = p.val("dS")
                for tsl in range(4):
                    bank = pb[4 + tsl % 2]
                    for hh in range(2):
                        h = 2 * w + hh
                        p.emit("tensor",
                               lambda e, tsl=tsl, h=h, hh=hh, bank=bank: e.matmul(
                                   bank[64 * hh:64 * hh + 48, :],
                                   wuq_sb[:, h * 48:h * 48 + 48],
                                   cqv[:, tsl * TS:(tsl + 1) * TS],
                                   start=True, stop=True,
                                   tile_position=(0, 64 * hh)),
                               wait=([VW()] if hh == 0 else ()),
                               inc=(p.i1("t") if hh == 1 else ()))
                    t_qn = p.val("t")
                    for hh in range(2):
                        p.emit("vector",
                               lambda e, tsl=tsl, hh=hh, w=w, bank=bank: e.tensor_scalar(
                                   qwv[64 * hh:64 * hh + 48, tsl * TS:(tsl + 1) * TS],
                                   bank[64 * hh:64 * hh + 48, :],
                                   bvec_sb[64 * hh:64 * hh + 48, 10 + w:11 + w],
                                   None, ALU.add),
                               wait=([("t", t_qn)] if hh == 0 else ()),
                               inc=(p.i1("v") if hh == 1 else ()))
                for tsl in range(4):
                    bank = pb[4 + tsl % 2]
                    p.emit("tensor",
                           lambda e, tsl=tsl, w=w, bank=bank: e.matmul(
                               bank[0:32, :], wqr_sb[:, w * 32:(w + 1) * 32],
                               cqv[:, tsl * TS:(tsl + 1) * TS],
                               start=True, stop=True),
                           wait=[VW()], inc=p.i1("t"))
                    t_qr = p.val("t")
                    p.emit("vector",
                           lambda e, tsl=tsl, w=w, bank=bank: e.tensor_scalar(
                               qraw[0:32, tsl * TS:(tsl + 1) * TS], bank[0:32, :],
                               bvec_sb[(w % 4) * 32:(w % 4) * 32 + 32, 20 + w // 4:21 + w // 4],
                               None, ALU.add),
                           wait=([("t", t_qr)] + ([("dS", p.val("dS"))] if tsl == 0 else [])),
                           inc=p.i1("v"))
                for sl in range(4):
                    apply_rope_slice(qraw, sl * TS, 32, COSq, sl * TS, RBq, sl * TS)
                v_qrope = p.val("v")
                for hh in range(2):
                    p.dma("sync", qwv[48 + 64 * hh:64 + 64 * hh, 0:2048],
                          qraw[16 * hh:16 * hh + 16, 0:2048],
                          "dS", wait=[("v", v_qrope), TW(), ("dS", p.val("dS"))])
                dS_qrot = p.val("dS")

                if DUMP and w == 0:
                    p.dma("sync", dbg["dkw"].ap(), kwv[:], "dS", wait=[VW(), ("dS", p.val("dS"))])
                    p.dma("sync", dbg["dqw"].ap(), qwv[:], "dS")
                    p.dma("sync", dbg["dv65"].ap(), v65[:], "dS")
                for qp in range(2):
                    avb = [pb[2 + 2 * j + hh] for hh in range(2) for j in range(2)]
                    for kt in range(32):
                        for j in range(2):
                            qs = 2 * qp + j
                            for hh in range(2):
                                sb = pb[hh]
                                p.emit("tensor",
                                       lambda e, kt=kt, hh=hh, qs=qs, sb=sb: e.matmul(
                                           sb[:, :], kwv[64 * hh:64 * hh + 64, kt * 128:(kt + 1) * 128],
                                           qwv[64 * hh:64 * hh + 64, qs * TS:(qs + 1) * TS],
                                           start=True, stop=True),
                                       wait=(([SW(), ("dS", dS_qrot)] + ([VW()] if kt == 0 else []))
                                             if hh == 0 else ()),
                                       inc=(p.i1("t") if hh == 1 else ()))
                            t_sc = p.val("t")
                            for hh in range(2):
                                sb = pb[hh]
                                p.emit("scalar",
                                       lambda e, hh=hh, j=j, sb=sb: e.activation(
                                           eT[2 * hh + j][:], sb[:, :], AF.Exp, scale=SC8),
                                       wait=([("t", t_sc)] if hh == 0 else ()),
                                       inc=(p.i1("s") if hh == 1 else ()))
                            s_e = p.val("s")
                            for hh in range(2):
                                p.emit("tensor",
                                       lambda e, kt=kt, hh=hh, j=j: e.matmul(
                                           avb[2 * hh + j][0:65, :],
                                           bass.AP(AB, v65.off + kt * 130 + 65 * hh,
                                                   [[ABC, 128], [1, 65]]),
                                           eT[2 * hh + j][:],
                                           start=(kt == 0), stop=(kt == 31)),
                                       wait=([("s", s_e)] if hh == 0 else ()),
                                       inc=(p.i1("t") if (kt == 31 and j == 1 and hh == 1) else ()))
                    t_av = p.val("t")
                    for hh in range(2):
                        auw = auwA if hh == 0 else auwB
                        for j in range(2):
                            qs = 2 * qp + j
                            av = avb[2 * hh + j]
                            p.emit("vector",
                                   lambda e, av=av, j=j: e.reciprocal(
                                       zrowf_sb[0:1, j * TS:(j + 1) * TS], av[64:65, :]),
                                   wait=[("t", t_av), VW(), ("dA", p.val("dA"))], inc=p.i1("v"))
                            p.emit("vector",
                                   lambda e, j=j: e.tensor_copy(
                                       zrow_sb[0:1, j * TS:(j + 1) * TS],
                                       zrowf_sb[0:1, j * TS:(j + 1) * TS]),
                                   wait=[VW()], inc=p.i1("v"))
                            v_z = p.val("v")
                            zb = pb[hh]
                            p.emit("tensor",
                                   lambda e, zb=zb, j=j: e.matmul(
                                       zb[0:64, :], onesr_bf[0:1, 0:64],
                                       zrow_sb[0:1, j * TS:(j + 1) * TS],
                                       start=True, stop=True),
                                   wait=[("v", v_z)], inc=p.i1("t"))
                            t_zb = p.val("t")
                            p.emit("vector",
                                   lambda e, zb=zb: e.tensor_copy(zbs[0:64, 0:TS], zb[0:64, :]),
                                   wait=[("t", t_zb)], inc=p.i1("v"))
                            v_zc = p.val("v")
                            p.emit("vector",
                                   lambda e, auw=auw, av=av, qs=qs: e.tensor_mul(
                                       auw[0:64, qs * TS:(qs + 1) * TS],
                                       av[0:64, :], zbs[0:64, 0:TS]),
                                   wait=[("v", v_zc)], inc=p.i1("v"))
                v_au = p.val("v")
                for hh in range(2):
                    auw = auwA if hh == 0 else auwB
                    h = 2 * w + hh
                    p.dma("scalar", bass.AP(aud, 0, [[1, 1]]) if False else
                          bass.AP(aud, h * 64 * 2048, [[2048, 64], [1, 2048]]),
                          auw[0:64, 0:2048],
                          "dA", wait=[("v", v_au), ("dA", p.val("dA"))])
            dA_au = p.val("dA")

            ar.reset(16)
            auf = ar.take(16384)
            wof = ar.take(8192)
            bcorb = ar.take(16)
            p.dma("scalar", auf[:],
                  bass.AP(aud, 0, [[2048, 128], [128 * 2048, 8], [1, 2048]]),
                  "dA", wait=[("dA", dA_au), TW(), VW()])
            dA_auf = p.val("dA")
            if DUMP:
                p.dma("sync", dbg["dau"].ap(), auf[:], "dS", wait=[("dA", dA_auf)])
                p.dma("sync", dbg["dxpre"].ap(), x_res[:], "dS", wait=[VW()])
            p.dma("gpsimd", wof[:],
                  bass.AP(wo1, 0, [[1024, 128], [128 * 1024, 8], [1, 1024]]),
                  "dG", wait=[TW()])
            dG_wo = p.val("dG")
            for oc in range(NCK):
                for ck in range(NCK):
                    p.emit("tensor",
                           lambda e, oc=oc, ck=ck: e.matmul(
                               pb[6][:, oc:oc + 1],
                               wof[:, ck * 1024 + oc * 128:ck * 1024 + oc * 128 + 128],
                               buvb[:, ck:ck + 1],
                               start=(ck == 0), stop=(ck == NCK - 1)),
                           wait=([("dG", dG_wo), ("dG", dG_w1), VW()] if (oc == 0 and ck == 0) else ()),
                           inc=(p.i1("t") if (oc == NCK - 1 and ck == NCK - 1) else ()))
            t_bc = p.val("t")
            if DUMP:
                p.dma("sync", dbg["dwof"].ap(), wof[:], "dS", wait=[("dG", p.val("dG"))])
            p.emit("vector",
                   lambda e: e.tensor_add(bvec2_sb[:, 0:8], pb[6][:, 0:8], bvec_sb[:, 22:30]),
                   wait=[("t", t_bc)], inc=p.i1("v"))
            v_bc = p.val("v")
            for tsl in range(4):
                for oc in range(NCK):
                    bank = pb[oc % 2]
                    for ck in range(NCK):
                        p.emit("tensor",
                               lambda e, tsl=tsl, oc=oc, ck=ck, bank=bank: e.matmul(
                                   bank[:, :],
                                   wof[:, ck * 1024 + oc * 128:ck * 1024 + oc * 128 + 128],
                                   auf[:, ck * 2048 + tsl * TS:ck * 2048 + tsl * TS + TS],
                                   start=(ck == 0), stop=(ck == NCK - 1)),
                               wait=(([("dA", dA_auf), ("v", v_bc), VW()] +
                                      ([("dS", p.val("dS"))] if DUMP else []))
                                     if ck == 0 else ()),
                               inc=(p.i1("t") if ck == NCK - 1 else ()))
                    t_o = p.val("t")
                    co = oc * T + tsl * TS
                    p.emit("vector",
                           lambda e, co=co, oc=oc, bank=bank: e.scalar_tensor_tensor(
                               x_res[:, co:co + TS], bank[:, :],
                               bvec2_sb[:, oc:oc + 1],
                               x_res[:, co:co + TS], ALU.add, ALU.add),
                           wait=[("t", t_o)], inc=p.i1("v"))

        def emit_l0_attn():
            SC8 = 1.0 / 8.0
            ar = Arena(AB, 36864)
            rk_fm = ar.take(2048)
            masks = [ar.take(512) for _ in range(4)]
            zt = ar.take(512)
            idn = ar.take(128)
            mark0 = ar.off
            qwv = ar.take(4096)
            kwv = ar.take(4096)
            v65 = ar.take(32 * 130)
            wqc = ar.take(1024)
            wkc = ar.take(1024)
            wvc = ar.take(1024)
            ebs = ar.take(2048)
            fsl = [ar.take(512) for _ in range(4)]
            esl = [ar.take(512) for _ in range(2)]
            psl = [ar.take(512) for _ in range(2)]
            ptl = [ar.take(512) for _ in range(2)]
            auh = ar.take(4096)
            zbs0 = ar.take(512)
            ya0 = ar.take(4096)


            p.dma("gpsimd", idn[:], c_ident.ap(), "dG", wait=[TW(), VW(), SW()])
            p.dma("sync", bvec_sb[:, 0:4], bq0.ap(), "dS", wait=[VW(), SW()])
            p.dma("sync", bvec_sb[:, 4:8], bk0.ap(), "dS")
            p.dma("sync", bvec_sb[:, 8:16], bvf0.ap(), "dS")
            p.dma("sync", bvec_sb[:, 16:24], bo0.ap(), "dS")
            dS_b0 = p.val("dS")
            p.dma("gpsimd", ya0[0:128, 0:1024],
                  bass.AP(rel0, 0, [[64, 128], [128 * 64, 16], [1, 64]]), "dG")
            dG_rel = p.val("dG")
            p.emit("vector", lambda e: e.memset(zt[:], 0.0), wait=[TW()], inc=p.i1("v"))
            for mi in range(4):
                p.emit("gpsimd",
                       lambda e, mi=mi: e.memset(masks[mi][:], 1.0),
                       inc=p.i1("g"))
                g_m = p.val("g")
                p.emit("gpsimd",
                       lambda e, mi=mi: e.affine_select(
                           masks[mi][:], masks[mi][:], [[-1, 512]], ALU.is_ge, 0.0,
                           base=mi * 128, channel_multiplier=1),
                       wait=[("g", g_m)], inc=p.i1("g"))
                p.emit("gpsimd",
                       lambda e, mi=mi: e.memset(maski_sb[:, mi * 512:(mi + 1) * 512], 1),
                       inc=p.i1("g"))
                g_m3 = p.val("g")
                p.emit("gpsimd",
                       lambda e, mi=mi: e.affine_select(
                           maski_sb[:, mi * 512:(mi + 1) * 512],
                           maski_sb[:, mi * 512:(mi + 1) * 512],
                           [[-1, 512]], ALU.is_ge, 0,
                           base=mi * 128, channel_multiplier=1),
                       wait=[("g", g_m3)], inc=p.i1("g"))
            g_masks = p.val("g")
            for zq in range(12):
                p.dma("scalar",
                      bass.AP(slab_all, 768 + zq * 128 * 2048, [[2048, 128], [1, 512]]),
                      zt[:], "dA", wait=[VW()])
            dA_pad = p.val("dA")
            for mt in range(16):
                p.emit("tensor",
                       lambda e, mt=mt: e.transpose(
                           pbt[0:64, (mt % 8) * 128:(mt % 8) * 128 + 128],
                           ya0[0:128, mt * 64:(mt + 1) * 64], idn[0:128, 0:128]),
                       wait=([("dG", dG_rel), VW()] if mt == 0 else
                             ([TW(), VW()] if mt == 8 else ())),
                       inc=(p.i1("t") if mt % 8 == 7 else ()))
                if mt % 8 == 7:
                    t_rk = p.val("t")
                    p.emit("vector",
                           lambda e, mt=mt: e.tensor_copy(
                               rk_fm[0:64, (mt - 7) * 128:(mt + 1) * 128],
                               pbt[0:64, 0:1024]),
                           wait=[("t", t_rk)], inc=p.i1("v"))
            v_rk = p.val("v")
            p.dma("sync", rk_fm[64:128, 0:2048], rk_fm[0:64, 0:2048], "r0",
                  wait=[("v", v_rk)])
            w1_dup = p.val("r0")

            v_ln = p.val("v")
            p.dma("gpsimd", cc0_in.ap(), ybuf[:], "u0", wait=[("v", v_ln)])
            u_cc = p.val("u0")
            p.emit("gpsimd",
                   lambda e: e.collective_compute(
                       "AllGather", ALU.bypass,
                       replica_groups=[[0, 1], [2, 3], [4, 5], [6, 7]],
                       ins=[cc0_in.ap().opt()], outs=[cc0_out.ap().opt()]),
                   wait=[("u0", u_cc)], inc=p.i1("c"))
            c_ag = p.val("c")

            _l0stop = int(_os.environ.get("DEV_L0STOP", "0"))
            if DUMP and _l0stop >= 1:
                p.dma("sync", dbg["dm0"].ap(), masks[0][:], "st",
                      wait=[("g", p.val("g")), ("dA", p.val("dA")), VW()])
                p.dma("sync", dbg["deb"].ap(),
                      bass.AP(slab_all, 768, [[2048, 128], [1, 2048]]), "st")
            FS = ("f0", "f1", "r0", "st")
            _cv, _cs, _ct = p.val("v"), p.val("s"), p.val("t")
            bankfree = {b: [("v", _cv), ("s", _cs)] for b in (0, 1, 3, 4, 6)}
            avfree = {5: [("v", _cv)], 2: [("v", _cv)]}
            eslfree = [_cv, _cv]
            fslfree = [_cv] * 4
            f0free = _cv
            ebsfree = [0] * 4
            pslfree = [_ct, _ct]
            pbt_free = _cv
            ptlfree = [_ct, _ct]
            for w in range(0 if _l0stop >= 1 else 4):
                p.dma("gpsimd", wqc[:],
                      bass.AP(wq0, w * 128, [[512, 128], [512 * 128, 8], [1, 128]]),
                      "w0", wait=[TW()])
                p.dma("gpsimd", wkc[:],
                      bass.AP(wk0, w * 128, [[512, 128], [512 * 128, 8], [1, 128]]),
                      "w0")
                p.dma("gpsimd", wvc[:],
                      bass.AP(wv0, w * 128, [[512, 128], [512 * 128, 8], [1, 128]]),
                      "w0")
                w_w = p.val("w0")
                for ts2 in range(8):
                    r, tsl = ts2 // 4, ts2 % 4
                    p.dma("scalar", ya0[:],
                          bass.AP(cc0_out, r * 128 * 16384 + tsl * TS,
                                  [[16384, 128], [T, NCK], [1, TS]]),
                          "dA", wait=[("c", c_ag), TW()])
                    dA_y = p.val("dA")
                    for mm_i, (wc, dst, bcol) in enumerate(
                            ((wqc, qwv, 0), (wkc, kwv, 4))):
                        bank = pb[mm_i]
                        for ck in range(NCK):
                            p.emit("tensor",
                                   lambda e, ck=ck, wc=wc, bank=bank: e.matmul(
                                       bank[:, :], wc[:, ck * 128:(ck + 1) * 128],
                                       ya0[:, ck * TS:(ck + 1) * TS],
                                       start=(ck == 0), stop=(ck == NCK - 1)),
                                   wait=([("dA", dA_y), ("w0", w_w), VW(), SW()] if ck == 0 else ()),
                                   inc=(p.i1("t") if ck == NCK - 1 else ()))
                        t_p = p.val("t")
                        p.emit("vector",
                               lambda e, ts2=ts2, dst=dst, bank=bank, bcol=bcol, w=w: e.tensor_scalar(
                                   dst[:, ts2 * TS:(ts2 + 1) * TS], bank[:, :],
                                   bvec_sb[:, bcol + w:bcol + w + 1], None, ALU.add),
                               wait=[("t", t_p), ("dS", dS_b0)], inc=p.i1("v"))
                    for tt4 in range(4):
                        tt = ts2 * 4 + tt4
                        bank = pb[2 + tt4 % 2]
                        for ck in range(NCK):
                            p.emit("tensor",
                                   lambda e, ck=ck, tt4=tt4, bank=bank: e.matmul(
                                       bank[:, 0:128],
                                       ya0[:, ck * TS + tt4 * 128:ck * TS + tt4 * 128 + 128],
                                       wvc[:, ck * 128:(ck + 1) * 128],
                                       start=(ck == 0), stop=(ck == NCK - 1)),
                                   wait=([VW(), SW()] if ck == 0 else ()),
                                   inc=(p.i1("t") if ck == NCK - 1 else ()))
                        t_v = p.val("t")
                        p.emit("vector",
                               lambda e, tt=tt, bank=bank: e.tensor_copy(
                                   bass.AP(AB, v65.off + tt * 130, [[36864, 128], [65, 2], [1, 64]]),
                                   bank[:, 0:128]),
                               wait=[("t", t_v)], inc=p.i1("v"))
                p.emit("vector",
                       lambda e: e.memset(
                           bass.AP(AB, v65.off + 64, [[36864, 128], [130, 32], [65, 2], [1, 1]]),
                           1.0),
                       inc=p.i1("v"))
                v_wave = p.val("v")

                bankfree[0] = [("v", v_wave)]
                bankfree[1] = [("v", v_wave)]
                bankfree[3] = [("v", v_wave)]
                avfree[2] = [("v", v_wave)]
                for hh in range(2):
                    h = 2 * w + hh
                    steps = []
                    for qt in range(32):
                        span = qt * 128 + 128
                        Wn = min((span + 511) // 512, 4) * 512
                        for bs in range(Wn // 512):
                            steps.append((qt, bs, Wn))
                    NBS = len(steps)
                    t_bmm = {}
                    dA_q = {}
                    e0w = {}
                    for n in range(NBS + 1):
                        if n < NBS:
                            qt, bs, Wn = steps[n]
                            bB = 4 if n % 2 == 0 else 0
                            p.emit("tensor",
                                   lambda e, qt=qt, bs=bs, Wn=Wn, hh=hh, bB=bB: e.matmul(
                                       pb[bB][:, :],
                                       qwv[64 * hh:64 * hh + 64, qt * 128:(qt + 1) * 128],
                                       rk_fm[64 * hh:64 * hh + 64,
                                             (2048 - Wn) + bs * 512:(2048 - Wn) + bs * 512 + 512],
                                       start=True, stop=True),
                                   wait=(list(bankfree[bB]) +
                                         ([("r0", w1_dup)] if n == 0 else [])),
                                   inc=p.i1("t"))
                            t_bmm[n] = p.val("t")
                        m = n - 1
                        if m >= 0:
                            qt, bs, Wn = steps[m]
                            bB = 4 if m % 2 == 0 else 0
                            slot = bs
                            p.emit("scalar",
                                   lambda e, slot=slot, bB=bB: e.activation(
                                       ebs[:, slot * TS:slot * TS + TS], pb[bB][:, :],
                                       AF.Exp, scale=SC8),
                                   wait=[("t", t_bmm[m]), (f"e{slot}", ebsfree[slot])] +
                                        ([("v", f0free)] if slot == 0 else []),
                                   inc=p.i1("s"))
                            s_exp = p.val("s")
                            bankfree[bB] = [("s", s_exp)]
                            if bs == 0 and Wn == 2048:
                                p.emit("vector",
                                       lambda e, qt=qt: e.tensor_copy(
                                           f0_sb[:, qt:qt + 1], ebs[:, 0:1]),
                                       wait=[("s", s_exp)], inc=p.i1("v"))
                                f0free = p.val("v")
                            p.dma("gpsimd",
                                  bass.AP(slab_all,
                                          768 + qt * 262144 + (2048 - Wn) + bs * 512,
                                          [[2048, 128], [1, 512]]),
                                  ebs[:, slot * TS:slot * TS + TS],
                                  f"e{slot}",
                                  wait=([("s", s_exp)] +
                                        ([(fs, p.val(fs)) for fs in FS] if m == 0 else [])))
                            ebsfree[slot] = p.val(f"e{slot}")
                            if bs == 0:
                                e0w[qt] = p.val("e0")
                            if bs == Wn // 512 - 1:
                                dA_q[qt] = [(f"e{sl}", p.val(f"e{sl}"))
                                            for sl in range(Wn // 512)]
                    for qt in range(32):
                        I0 = qt * 128
                        span = I0 + 128
                        nkt = (span + 511) // 512
                        Wn = min(nkt, 4) * 512
                        qpar = qt % 2
                        bSs = (6, 3) if qpar == 0 else (1, 3)
                        bAV = 5 if qpar == 0 else 2
                        dA_slab = dA_q[qt]
                        kinfo = []
                        for kt in range(nkt):
                            ktw = min(512, span - kt * 512)
                            c2 = I0 - kt * 512 - 2048
                            kinfo.append(dict(
                                ktw=ktw, c2=c2,
                                fullsat=(c2 >= 511), nosat=(c2 <= -128),
                                diag=(kt == nkt - 1), ntp=(ktw + 127) // 128))
                        st_exp = {}
                        st_p = {}
                        st_tp = {}
                        st_ev = {}
                        st_av = {}
                        st_f = {}
                        LAG1, LAG2 = 1, 2
                        for i in range(nkt + LAG2):
                            kt = i
                            if kt < nkt:
                                ki = kinfo[kt]
                                if not ki["fullsat"]:
                                    fs = FS[kt % 4]
                                    xw = (list(dA_slab) +
                                          [("v", fslfree[kt % 4])] +
                                          ([("e0", e0w[qt + 1])]
                                           if (ki["diag"] and qt + 1 <= 31
                                               and qt + 1 >= 12) else []))
                                    p.dma("sync", fsl[kt % 4][:],
                                          bass.AP(slab_all,
                                                  768 + qt * 262144 + 2047 - I0 + kt * 512,
                                                  [[2047, 128], [1, 512]]),
                                          fs, wait=xw)
                                    st_f[kt] = (fs, p.val(fs))
                                bk = bSs[kt % 2]
                                p.emit("tensor",
                                       lambda e, hh=hh, qt=qt, kt=kt, bk=bk: e.matmul(
                                           pb[bk][:, :], qwv[64 * hh:64 * hh + 64, qt * 128:(qt + 1) * 128],
                                           kwv[64 * hh:64 * hh + 64, kt * 512:kt * 512 + 512],
                                           start=True, stop=True),
                                       wait=list(bankfree[bk]), inc=p.i1("t"))
                                t_s = p.val("t")
                                p.emit("scalar",
                                       lambda e, sl2=kt % 2, bk=bk: e.activation(
                                           esl[sl2][:], pb[bk][:, :], AF.Exp, scale=SC8),
                                       wait=[("t", t_s), ("v", eslfree[kt % 2])],
                                       inc=p.i1("s"))
                                st_exp[kt] = p.val("s")
                                bankfree[bk] = [("s", st_exp[kt])]
                            k2 = i - LAG2
                            if 0 <= k2 < nkt:
                                ki = kinfo[k2]
                                po = (k2 % 2) * 512
                                p.emit("vector",
                                       lambda e, sl2=k2 % 2, po=po, ntp=ki["ntp"]: e.tensor_copy(
                                           ptl[sl2][0:128, 0:ntp * 128],
                                           pbt[:, po:po + ntp * 128]),
                                       wait=[("t", max(st_tp[k2], ptlfree[k2 % 2]))],
                                       inc=p.i1("v"))
                                st_ev[k2] = p.val("v")
                                pbt_free = st_ev[k2]
                                for tp in range(ki["ntp"]):
                                    ktt = k2 * 4 + tp
                                    p.emit("tensor",
                                           lambda e, ktt=ktt, hh=hh, sl2=k2 % 2, tp=tp,
                                                  span=span, bAV=bAV: e.matmul(
                                               pb[bAV][0:65, 0:128],
                                               bass.AP(AB, v65.off + ktt * 130 + 65 * hh,
                                                       [[36864, 128], [1, 65]]),
                                               ptl[sl2][0:128, tp * 128:(tp + 1) * 128],
                                               start=(ktt == 0), stop=(ktt == (span // 128) - 1)),
                                           wait=([("v", st_ev[k2])] +
                                                 (list(avfree[bAV]) if k2 == 0 else [])
                                                 if tp == 0 else ()),
                                           inc=(p.i1("t") if tp == ki["ntp"] - 1 else ()))
                                st_av[k2] = p.val("t")
                                ptlfree[k2 % 2] = st_av[k2]
                            k1 = i - LAG1
                            if 0 <= k1 < nkt:
                                ki = kinfo[k1]
                                ps = psl[k1 % 2]
                                pw = [("s", st_exp[k1]), ("t", pslfree[k1 % 2])]
                                if k1 in st_f:
                                    pw.append(st_f[k1])
                                if ki["fullsat"]:
                                    p.emit("vector",
                                           lambda e, ps=ps, sl2=k1 % 2, qt=qt: e.tensor_scalar(
                                               ps[:], esl[sl2][:], f0_sb[:, qt:qt + 1],
                                               None, ALU.mult),
                                           wait=pw, inc=p.i1("v"))
                                else:
                                    p.emit("vector",
                                           lambda e, ps=ps, sl2=k1 % 2, sl4=k1 % 4: e.tensor_mul(
                                               ps[:], esl[sl2][:], fsl[sl4][:]),
                                           wait=pw, inc=p.i1("v"))
                                    if not ki["nosat"]:
                                        mi = ki["c2"] // 128
                                        assert 0 <= mi <= 3
                                        p.emit("vector",
                                               lambda e, sl2=k1 % 2, qt=qt: e.tensor_scalar(
                                                   esl[sl2][:], esl[sl2][:],
                                                   f0_sb[:, qt:qt + 1], None, ALU.mult),
                                               wait=[VW()], inc=p.i1("v"))
                                        p.emit("vector",
                                               lambda e, ps=ps, sl2=k1 % 2, mi=mi: e.copy_predicated(
                                                   ps[:], maski_sb[:, mi * 512:(mi + 1) * 512],
                                                   esl[sl2][:]),
                                               wait=[VW()], inc=p.i1("v"))
                                    if ki["diag"]:
                                        mi2 = (I0 - (nkt - 1) * 512) // 128
                                        assert 0 <= mi2 <= 3
                                        p.emit("vector",
                                               lambda e, ps=ps, mi2=mi2: e.tensor_mul(
                                                   ps[:], ps[:], masks[mi2][:]),
                                               wait=[VW()], inc=p.i1("v"))
                                st_p[k1] = p.val("v")
                                eslfree[k1 % 2] = st_p[k1]
                                if k1 in st_f:
                                    fslfree[k1 % 4] = st_p[k1]
                                po = (k1 % 2) * 512
                                for tp in range(ki["ntp"]):
                                    p.emit("tensor",
                                           lambda e, ps=ps, tp=tp, po=po: e.transpose(
                                               pbt[:, po + tp * 128:po + (tp + 1) * 128],
                                               ps[0:128, tp * 128:tp * 128 + 128],
                                               idn[0:128, 0:128]),
                                           wait=([("v", max(st_p[k1], pbt_free))]
                                                 if tp == 0 else ()),
                                           inc=(p.i1("t") if tp == ki["ntp"] - 1 else ()))
                                st_tp[k1] = p.val("t")
                                pslfree[k1 % 2] = st_tp[k1]
                        t_av = st_av[nkt - 1]
                        zro = qpar * 128
                        p.emit("vector",
                               lambda e, zro=zro, bAV=bAV: e.reciprocal(
                                   zrowf_sb[0:1, zro:zro + 128], pb[bAV][64:65, 0:128]),
                               wait=[("t", t_av)], inc=p.i1("v"))
                        p.emit("vector",
                               lambda e, zro=zro: e.tensor_copy(
                                   zrow_sb[0:1, zro:zro + 128], zrowf_sb[0:1, zro:zro + 128]),
                               wait=[VW()], inc=p.i1("v"))
                        v_z = p.val("v")
                        bZ = bSs[0]
                        p.emit("tensor",
                               lambda e, zro=zro, bZ=bZ: e.matmul(
                                   pb[bZ][0:64, 0:128], onesr_bf[0:1, 0:64],
                                   zrow_sb[0:1, zro:zro + 128], start=True, stop=True),
                               wait=[("v", v_z)] + list(bankfree[bZ]), inc=p.i1("t"))
                        t_zb = p.val("t")
                        p.emit("vector",
                               lambda e, zro=zro, bZ=bZ: e.tensor_copy(
                                   zbs0[0:64, zro:zro + 128], pb[bZ][0:64, 0:128]),
                               wait=[("t", t_zb)], inc=p.i1("v"))
                        v_zc = p.val("v")
                        bankfree[bZ] = [("v", v_zc)]
                        p.emit("vector",
                               lambda e, qt=qt, zro=zro, bAV=bAV: e.tensor_mul(
                                   auh[0:64, qt * 128:(qt + 1) * 128],
                                   pb[bAV][0:64, 0:128], zbs0[0:64, zro:zro + 128]),
                               wait=[("v", v_zc)] +
                                    ([("a0", p.val("a0"))] if qt <= 1 else []),
                               inc=p.i1("v"))
                        avfree[bAV] = [("v", p.val("v"))]
                    v_au = p.val("v")
                    p.dma("scalar",
                          bass.AP(cc2_in, (h % 8) * 64 * 4096, [[4096, 64], [1, 4096]]),
                          auh[0:64, 0:4096], "a0",
                          wait=[("v", v_au), ("a0", p.val("a0"))])
            u_au = p.val("a0")

            p.emit("gpsimd",
                   lambda e: e.collective_compute(
                       "AllGather", ALU.bypass,
                       replica_groups=[[0, 1], [2, 3], [4, 5], [6, 7]],
                       ins=[cc2_in.ap().opt()], outs=[cc2_out.ap().opt()]),
                   wait=[("a0", u_au)], inc=p.i1("c"))
            c_au = p.val("c")

            ar.reset(mark0)
            auo = ar.take(4096)
            auoB = ar.take(4096)
            wvv = ar.take(8192)
            p.dma("gpsimd", wvv[0:128, 0:8192],
                  bass.AP(wo0, 0, [[1024, 128], [128 * 1024, 8], [1, 1024]]), "dG",
                  wait=[TW(), VW(), SW()])
            dG_w0 = p.val("dG")
            p.emit("scalar", lambda e: e.copy(bvb_sb[:, 0:8], bvec_sb[:, 8:16]),
                   wait=[("dS", dS_b0), SW()], inc=p.i1("s"))
            s_bv = p.val("s")
            for oc in range(NCK):
                for ck in range(NCK):
                    p.emit("tensor",
                           lambda e, oc=oc, ck=ck: e.matmul(
                               pb[6][:, oc:oc + 1],
                               wvv[0:128, ck * 1024 + oc * 128:ck * 1024 + oc * 128 + 128],
                               bvb_sb[:, ck:ck + 1],
                               start=(ck == 0), stop=(ck == NCK - 1)),
                           wait=([("dG", dG_w0), ("s", s_bv), VW()] if (oc == 0 and ck == 0) else ()),
                           inc=(p.i1("t") if (oc == NCK - 1 and ck == NCK - 1) else ()))
            t_bc0 = p.val("t")
            p.emit("vector",
                   lambda e: e.tensor_add(bvec2_sb[:, 0:8], pb[6][:, 0:8], bvec_sb[:, 16:24]),
                   wait=[("t", t_bc0)], inc=p.i1("v"))
            v_bc0 = p.val("v")
            p.dma("sync", bvec_sb[:, 30:32], c_half.ap(), "dS",
                  wait=[("dS", p.val("dS"))])
            dS_hm = p.val("dS")
            for tsl in range(4):
                p.dma("scalar", auoB[:],
                      bass.AP(cc2_out, tsl * TS, [[4096, 128], [128 * 4096, 8], [1, TS]]),
                      "dA", wait=[("c", c_au), TW(), VW(), ("dA", p.val("dA"))])
                p.dma("scalar", auo[:],
                      bass.AP(cc2_out, 2048 + tsl * TS, [[4096, 128], [128 * 4096, 8], [1, TS]]),
                      "dA", wait=[("dA", p.val("dA"))])
                dA_both = p.val("dA")
                p.emit("vector",
                       lambda e: e.tensor_scalar(auo[:], auo[:], bvec_sb[:, 31:32],
                                                 None, ALU.mult),
                       wait=[("dA", dA_both), ("dS", dS_hm), VW()], inc=p.i1("v"))
                p.emit("vector",
                       lambda e: e.scalar_tensor_tensor(auo[:], auoB[:],
                                                        bvec_sb[:, 30:31], auo[:],
                                                        ALU.mult, ALU.add),
                       wait=[VW()], inc=p.i1("v"))
                g_au = ("v", p.val("v"))
                for oc in range(NCK):
                    bank = pb[oc % 2]
                    for ck in range(NCK):
                        p.emit("tensor",
                               lambda e, tsl=tsl, oc=oc, ck=ck, bank=bank: e.matmul(
                                   bank[:, :],
                                   wvv[0:128, ck * 1024 + oc * 128:ck * 1024 + oc * 128 + 128],
                                   auo[:, ck * TS:(ck + 1) * TS],
                                   start=(ck == 0), stop=(ck == NCK - 1)),
                               wait=([g_au, ("v", v_bc0), VW()] if ck == 0 else ()),
                               inc=(p.i1("t") if ck == NCK - 1 else ()))
                    t_o = p.val("t")
                    co = oc * T + tsl * TS
                    p.emit("vector",
                           lambda e, co=co, oc=oc, bank=bank: e.scalar_tensor_tensor(
                               x_res[:, co:co + TS], bank[:, :],
                               bvec2_sb[:, oc:oc + 1],
                               x_res[:, co:co + TS], ALU.add, ALU.add),
                           wait=[("t", t_o)], inc=p.i1("v"))
            p.dma("scalar", au0_sb[0:128, 0:8],
                  bass.AP(cc2_out, 0, [[4096, 128], [128 * 4096, 8], [1, 1]]),
                  "dA", wait=[("c", c_au), ("dA", p.val("dA"))],
                  allow_slow_non_contiguous=True)
            dA_au0 = p.val("dA")
            for oc in range(NCK):
                for ck in range(NCK):
                    p.emit("tensor",
                           lambda e, oc=oc, ck=ck: e.matmul(
                               pb[6][:, 8 + oc:9 + oc],
                               wvv[0:128, ck * 1024 + oc * 128:ck * 1024 + oc * 128 + 128],
                               au0_sb[:, ck:ck + 1],
                               start=(ck == 0), stop=(ck == NCK - 1)),
                           wait=([("dA", dA_au0), VW()] if (oc == 0 and ck == 0) else ()),
                           inc=(p.i1("t") if (oc == NCK - 1 and ck == NCK - 1) else ()))
            t_x0 = p.val("t")
            p.emit("vector",
                   lambda e: e.tensor_add(x0_sb[:], x0_sb[:], pb[6][:, 8:16]),
                   wait=[("t", t_x0), VW()], inc=p.i1("v"))
            p.emit("vector",
                   lambda e: e.tensor_add(x0_sb[:], x0_sb[:], bvec2_sb[:, 0:8]),
                   wait=[VW()], inc=p.i1("v"))

        def phase_l0ffn(track_x0=False):
            emit_gates(ready=[("dS", dS_all), ("v", p.val("v")), ("s", p.val("s")),
                              ("t", p.val("t"))])
            emit_layernorm(ready=[("dS", dS_all), ("v", p.val("v"))])
            emit_l0_ffn()
            if track_x0:
                emit_x0_track()

        def phase_l0attn():
            emit_layernorm(ready=[("v", p.val("v")), ("s", p.val("s")),
                                  ("t", p.val("t")), ("dS", dS_all)])
            emit_l0_attn()

        def phase_l1ffn():
            emit_layernorm(ready=[("v", p.val("v")), ("s", p.val("s")),
                                  ("t", p.val("t")), ("dS", dS_all)])
            emit_l1_ffn()

        def phase_l1attn():
            emit_layernorm(ready=[("v", p.val("v")), ("s", p.val("s")),
                                  ("t", p.val("t")), ("dS", dS_all)])
            emit_l1_attn()

        phase_fns = {"l0ffn": phase_l0ffn, "l1ffn": phase_l1ffn,
                     "l1attn": phase_l1attn, "l0attn": phase_l0attn}
        phases_used = list(PHASES[:n_sub])
        for pi, ph in enumerate(phases_used):
            if ph == "l0ffn" and "l0attn" in phases_used[pi + 1:]:
                phase_fns[ph](track_x0=True)
            else:
                phase_fns[ph]()

        v_done = p.val("v")
        s_done = p.val("s")
        t_done = p.val("t")
        for ck in range(NCK):
            p.dma("sync", bass.AP(y_out, ck * 128 * T, [[T, 128], [1, T]]),
                  x_res[:, ck * T:(ck + 1) * T], "st",
                  wait=[("v", v_done), ("s", s_done), ("t", t_done)] if ck == 0 else ())
        st_all = p.val("st")
        p.emit("sync", lambda e: e.nop(), wait=[("st", st_all)])

        with nc.Block() as block:
            @block.sync
            def _(e):
                for f in p.ops["sync"]:
                    f(e)

            @block.vector
            def _(e):
                for f in p.ops["vector"]:
                    f(e)

            @block.scalar
            def _(e):
                for f in p.ops["scalar"]:
                    f(e)

            @block.tensor
            def _(e):
                for f in p.ops["tensor"]:
                    f(e)

            @block.gpsimd
            def _(e):
                for f in p.ops["gpsimd"]:
                    f(e)

    return nc


def _prep_inputs(ins):
    x = np.asarray(ins["L0_attn_x"], dtype=np.float32)
    maps = []
    oh8 = np.zeros((8, 128), dtype=np.float32)
    for e in range(8):
        oh8[e, e * 16:(e + 1) * 16] = 1.0
    wg = np.asarray(ins["L1_ffn_w_gate"])
    wu = np.asarray(ins["L1_ffn_w_up"])
    wd = np.asarray(ins["L1_ffn_w_down"])
    wg_r = np.ascontiguousarray(wg.reshape(24, 128, 8, 128).transpose(0, 2, 3, 1))
    wu_r = np.ascontiguousarray(wu.reshape(24, 128, 8, 128).transpose(0, 2, 3, 1))
    wd_r = np.ascontiguousarray(wd.reshape(8, 128, 24, 128).transpose(0, 2, 3, 1))
    rw_t = np.ascontiguousarray(np.asarray(ins["L0_ffn_router_w"]).T)
    ldn = np.ascontiguousarray(
        np.asarray(ins["L0_ffn_lora_down"]).transpose(2, 0, 1).reshape(D, 128))
    lup = np.ascontiguousarray(
        np.asarray(ins["L0_ffn_lora_up"]).transpose(0, 2, 1).reshape(128, D))
    g = {k: np.asarray(v) for k, v in ins.items()}
    H, DH, DR, SD = 16, 64, 16, 48
    bkf_flat = np.zeros(1024, np.float32)
    bqf_flat = np.zeros(1024, np.float32)
    for h in range(H):
        bkf_flat[h * 64:h * 64 + 48] = g["L1_attn_b_uk"][h * 48:(h + 1) * 48]
        bqf_flat[h * 64:h * 64 + 48] = g["L1_attn_b_uq"][h * 48:(h + 1) * 48]
    invf = (1.0 / 10000.0 ** (np.arange(0, 4, dtype=np.float64) / 4.0))
    invf_row = np.zeros(128, np.float32)
    sign_row = np.zeros(128, np.float32)
    for p_ in range(128):
        d = p_ % 16
        if d < 8:
            invf_row[p_] = np.float32(invf[d % 4] / 40.0)
            sign_row[p_] = -1.0 if d < 4 else 1.0
    perm = np.zeros((128, 128), np.float32)
    for r in range(128):
        d = r % 16
        if d < 4:
            perm[r, r + 4] = 1.0
        elif d < 8:
            perm[r, r - 4] = 1.0
    perm_t = np.ascontiguousarray(perm.T).astype(ml_dtypes.bfloat16)
    l0c = {
        "wo0": g["L0_attn_Wo"],
        "rel0": g["L0_attn_rel_key"],
        "bvf0": np.ascontiguousarray(g["L0_attn_bv"].reshape(8, 128).T),
        "bo0": np.ascontiguousarray(g["L0_attn_bo"].reshape(8, 128).T),
        "c_ident": np.eye(128, dtype=np.float32).astype(ml_dtypes.bfloat16),
    }
    l1c = {
        "wdkv": g["L1_attn_W_dkv"], "wdq": g["L1_attn_W_dq"],
        "wuk": g["L1_attn_W_uk"], "wuv": g["L1_attn_W_uv"],
        "wuq": g["L1_attn_W_uq"], "wqr": g["L1_attn_W_qr"],
        "wkr": g["L1_attn_W_kr"], "wo1": g["L1_attn_W_o"],
        "bdkv": np.ascontiguousarray(g["L1_attn_b_dkv"].reshape(128, 1)),
        "bdq": np.ascontiguousarray(g["L1_attn_b_dq"].reshape(128, 1)),
        "bkf": np.ascontiguousarray(bkf_flat.reshape(8, 128).T),
        "bqf": np.ascontiguousarray(bqf_flat.reshape(8, 128).T),
        "bkr_c": np.ascontiguousarray(g["L1_attn_b_kr"].reshape(2, 128).T),
        "bqr_c": np.ascontiguousarray(g["L1_attn_b_qr"].reshape(2, 128).T),
        "buv_c": np.ascontiguousarray(g["L1_attn_b_uv"].reshape(8, 128).T),
        "bo1_c": np.ascontiguousarray(g["L1_attn_b_o"].reshape(8, 128).T),
        "c_invf": invf_row.reshape(128, 1),
        "c_sign": sign_row.reshape(128, 1),
        "c_perm": perm_t,
    }
    for c in range(NC):
        b, s = c // 2, c % 2
        im = {
            "x_in": np.ascontiguousarray(x[b, s * T:(s + 1) * T, :].T),
            "x0row": np.ascontiguousarray(x[b, 0, :].reshape(8, 128).T),
            "rw_t": rw_t,
            "ldn": ldn,
            "lup": lup,
            "c_oh8": oh8,
            "wg_r": wg_r,
            "wu_r": wu_r,
            "wd_r": wd_r,
            "c_qoff": np.array([[s * 2048]], dtype=np.float32),
            "c_half": np.ascontiguousarray(
                np.stack([np.full(128, 1.0 - s, np.float32),
                          np.full(128, float(s), np.float32)], 1)),
            "wq0": np.ascontiguousarray(g["L0_attn_Wq"][:, s * 512:(s + 1) * 512]),
            "wk0": np.ascontiguousarray(g["L0_attn_Wk"][:, s * 512:(s + 1) * 512]),
            "wv0": np.ascontiguousarray(g["L0_attn_Wv"][:, s * 512:(s + 1) * 512]),
            "bq0": np.ascontiguousarray(g["L0_attn_bq"][s * 512:(s + 1) * 512].reshape(4, 128).T),
            "bk0": np.ascontiguousarray(g["L0_attn_bk"][s * 512:(s + 1) * 512].reshape(4, 128).T),
        }
        im.update(l1c)
        im.update(l0c)
        maps.append(im)
    return maps


def kernel(**ins):
    maps = _prep_inputs(ins)
    nc = build_program(N_SUB)
    res = run_bass_kernel_spmd(nc, maps, core_ids=list(range(NC)))
    out = np.zeros((BATCH, S, D), dtype=np.float32)
    g = {k: np.asarray(v) for k, v in ins.items()}
    H, DH, DR, SD = 16, 64, 16, 48
    bkf_flat = np.zeros(1024, np.float32)
    bqf_flat = np.zeros(1024, np.float32)
    for h in range(H):
        bkf_flat[h * 64:h * 64 + 48] = g["L1_attn_b_uk"][h * 48:(h + 1) * 48]
        bqf_flat[h * 64:h * 64 + 48] = g["L1_attn_b_uq"][h * 48:(h + 1) * 48]
    invf = (1.0 / 10000.0 ** (np.arange(0, 4, dtype=np.float64) / 4.0))
    invf_row = np.zeros(128, np.float32)
    sign_row = np.zeros(128, np.float32)
    for p_ in range(128):
        d = p_ % 16
        if d < 8:
            invf_row[p_] = np.float32(invf[d % 4] / 40.0)
            sign_row[p_] = -1.0 if d < 4 else 1.0
    perm = np.zeros((128, 128), np.float32)
    for r in range(128):
        d = r % 16
        if d < 4:
            perm[r, r + 4] = 1.0
        elif d < 8:
            perm[r, r - 4] = 1.0
    perm_t = np.ascontiguousarray(perm.T).astype(ml_dtypes.bfloat16)
    l0c = {
        "wo0": g["L0_attn_Wo"],
        "rel0": g["L0_attn_rel_key"],
        "bvf0": np.ascontiguousarray(g["L0_attn_bv"].reshape(8, 128).T),
        "bo0": np.ascontiguousarray(g["L0_attn_bo"].reshape(8, 128).T),
        "c_ident": np.eye(128, dtype=np.float32).astype(ml_dtypes.bfloat16),
    }
    l1c = {
        "wdkv": g["L1_attn_W_dkv"], "wdq": g["L1_attn_W_dq"],
        "wuk": g["L1_attn_W_uk"], "wuv": g["L1_attn_W_uv"],
        "wuq": g["L1_attn_W_uq"], "wqr": g["L1_attn_W_qr"],
        "wkr": g["L1_attn_W_kr"], "wo1": g["L1_attn_W_o"],
        "bdkv": np.ascontiguousarray(g["L1_attn_b_dkv"].reshape(128, 1)),
        "bdq": np.ascontiguousarray(g["L1_attn_b_dq"].reshape(128, 1)),
        "bkf": np.ascontiguousarray(bkf_flat.reshape(8, 128).T),
        "bqf": np.ascontiguousarray(bqf_flat.reshape(8, 128).T),
        "bkr_c": np.ascontiguousarray(g["L1_attn_b_kr"].reshape(2, 128).T),
        "bqr_c": np.ascontiguousarray(g["L1_attn_b_qr"].reshape(2, 128).T),
        "buv_c": np.ascontiguousarray(g["L1_attn_b_uv"].reshape(8, 128).T),
        "bo1_c": np.ascontiguousarray(g["L1_attn_b_o"].reshape(8, 128).T),
        "c_invf": invf_row.reshape(128, 1),
        "c_sign": sign_row.reshape(128, 1),
        "c_perm": perm_t,
    }
    for c in range(NC):
        b, s = c // 2, c % 2
        out[b, s * T:(s + 1) * T, :] = res.results[c]["y_out"].T
    return out


# version 08
# speedup vs baseline: 1.0469x; 1.0469x over previous
import numpy as np
import ml_dtypes
from contextlib import ExitStack

import concourse.bass as bass
import concourse.mybir as mybir
from concourse.bass_utils import run_bass_kernel_spmd
from concourse.bass_core_rust import IndirectOffsetOnAxis
from concourse.aot_env import allow_ici_all_to_all

F32 = mybir.dt.float32
BF16 = mybir.dt.bfloat16
I32 = mybir.dt.int32
AF = mybir.ActivationFunctionType
ALU = mybir.AluOpType

BATCH, S, D = 4, 4096, 1024
T = 2048
NCK = 8
NC = 8
NSLICE = 4
TS = 512
LN_EPS = 1e-5

PHASES = ["l0ffn", "l0attn", "l0ffn", "l1ffn", "l1attn", "l1ffn"]
N_SUB = len(PHASES)


class Prog:
    ENGINES = ("sync", "vector", "scalar", "tensor", "gpsimd")

    def __init__(self):
        self.ops = {e: [] for e in self.ENGINES}
        self.semv = {}
        self.sems = {}

    def bump(self, name, k=1):
        self.semv[name] = self.semv.get(name, 0) + k
        return self.semv[name]

    def val(self, name):
        return self.semv.get(name, 0)

    def i1(self, name):
        self.bump(name, 1)
        return [(name, 1)]

    def emit(self, eng, f, wait=(), inc=()):
        wait = tuple(wait)
        inc = tuple(inc)

        def run(e, f=f, wait=wait, inc=inc):
            for s, v in wait:
                e.wait_ge(self.sems[s], v)
            r = f(e)
            for s, a in inc:
                r = r.then_inc(self.sems[s], a)
        self.ops[eng].append(run)

    def dma(self, eng, out, in_, sem, wait=(), **kw):
        v = self.bump(sem, 16)
        self.emit(eng, lambda e, o=out, i=in_, kw=kw: e.dma_start(out=o, in_=i, **kw),
                  wait=wait, inc=[(sem, 16)])
        return v


class V2:

    def __init__(self, t, off, cols):
        self.t, self.off, self.cols = t, off, cols

    def __getitem__(self, idx):
        if isinstance(idx, slice):
            assert idx == slice(None)
            return self.t[:, self.off:self.off + self.cols]
        pr, cs = idx
        a = cs.start if cs.start is not None else 0
        b = cs.stop if cs.stop is not None else self.cols
        return self.t[pr, self.off + a:self.off + b]


class Arena:
    def __init__(self, t, cols):
        self.t, self.cols, self.off = t, cols, 0

    def take(self, n):
        n = (n + 15) // 16 * 16
        assert self.off + n <= self.cols, f"arena overflow: {self.off}+{n} > {self.cols}"
        v = V2(self.t, self.off, n)
        self.off += n
        return v

    def reset(self, off=0):
        self.off = off


def build_program(n_sub=N_SUB):
    nc = bass.Bass("TRN2", target_bir_lowering=False, debug=False)
    p = Prog()

    dt_in = {}

    def din(name, shape, dtype=F32):
        dt_in[name] = nc.dram_tensor(name, list(shape), dtype, kind="ExternalInput")
        return dt_in[name]

    x_in = din("x_in", [D, T])
    x0row = din("x0row", [128, 8])
    rw_t = din("rw_t", [D, 8])
    ldn = din("ldn", [D, 128])
    lup = din("lup", [128, D])
    c_oh8 = din("c_oh8", [8, 128])
    wg_r = din("wg_r", [24, 8, 128, 128], I32)
    wdkv = din("wdkv", [D, 128])
    wdq = din("wdq", [D, 128])
    wuk = din("wuk", [128, 768])
    wuv = din("wuv", [128, 1024])
    wuq = din("wuq", [128, 768])
    wqr = din("wqr", [128, 256])
    wkr = din("wkr", [D, 256])
    wo1 = din("wo1", [D, D])
    bdkv = din("bdkv", [128, 1])
    bdq = din("bdq", [128, 1])
    bkf = din("bkf", [128, 8])
    bqf = din("bqf", [128, 8])
    bkr_c = din("bkr_c", [128, 2])
    bqr_c = din("bqr_c", [128, 2])
    buv_c = din("buv_c", [128, 8])
    bo1_c = din("bo1_c", [128, 8])
    c_invf = din("c_invf", [128, 1])
    c_sign = din("c_sign", [128, 1])
    c_perm = din("c_perm", [128, 128], BF16)
    c_qoff = din("c_qoff", [1, 1])
    wq0 = din("wq0", [D, 512])
    wk0 = din("wk0", [D, 512])
    wv0 = din("wv0", [D, 512])
    wo0 = din("wo0", [D, D])
    rel0 = din("rel0", [2048, 64])
    bq0 = din("bq0", [128, 4])
    bk0 = din("bk0", [128, 4])
    bvf0 = din("bvf0", [128, 8])
    bo0 = din("bo0", [128, 8])
    c_ident = din("c_ident", [128, 128], BF16)
    cc0_in = nc.dram_tensor("cc0_in", [128, 16384], BF16)
    cc0_out = nc.dram_tensor("cc0_out", [2, 128, 16384], BF16)
    cc2_in = nc.dram_tensor("cc2_in", [512, 4096], BF16)
    cc2_out = nc.dram_tensor("cc2_out", [2, 512, 4096], BF16)
    c_half = din("c_half", [128, 2])
    SLABN = 768 + 32 * 128 * 2048 + 768
    slab_all = nc.dram_tensor("slab_all", [SLABN], BF16)
    cc1_in = nc.dram_tensor("cc1_in", [128, 16384], BF16)
    cc1_out = nc.dram_tensor("cc1_out", [2, 128, 16384], BF16)
    krd = nc.dram_tensor("krd", [2, 128, 4096], BF16)
    aud = nc.dram_tensor("aud", [8, 128, 2048], BF16)
    wu_r = din("wu_r", [24, 8, 128, 128], I32)
    wd_r = din("wd_r", [8, 24, 128, 128], I32)

    y_out = nc.dram_tensor("y_out", [D, T], F32, kind="ExternalOutput")
    import os as _os
    DUMP = _os.environ.get("DEV_DUMP", "") == "1"
    if DUMP:
        dbg = {n: nc.dram_tensor(n, sh, BF16, kind="ExternalOutput")
               for n, sh in [("dck", [128, 4096]), ("dcq", [128, 2048]),
                             ("dkw", [128, 4096]), ("dqw", [128, 2048]),
                             ("dv65", [128, 4160]), ("dau", [128, 16384]),
                             ("dwof", [128, 8192]), ("dp0", [128, 512]),
                             ("de0", [128, 512]), ("df0", [128, 512]),
                             ("dm0", [128, 512]), ("deb", [128, 2048]), ("debs", [128, 1024]),
                             ("dauh", [64, 4096]), ("dq0", [128, 4096]), ("dk0", [128, 4096])]}
        dbg["dxpre"] = nc.dram_tensor("dxpre", [128, 16384], F32, kind="ExternalOutput")
        dbg["dxent"] = nc.dram_tensor("dxent", [128, 16384], F32, kind="ExternalOutput")

    with ExitStack() as ctx:
        E = ctx.enter_context

        x_res = E(nc.sbuf_tensor("x_res", [128, NCK * T], F32))
        ybuf = E(nc.sbuf_tensor("ybuf", [128, NCK * T], BF16))
        sq_sb = E(nc.sbuf_tensor("sq_sb", [128, NCK * TS], BF16))
        dn_sb = E(nc.sbuf_tensor("dn_sb", [128, T], BF16))
        ones_bf = E(nc.sbuf_tensor("ones_bf", [128, 2], BF16))
        ones_f = E(nc.sbuf_tensor("ones_f", [128, 2], F32))
        onesr_bf = E(nc.sbuf_tensor("onesr_bf", [1, 128], BF16))
        onesr_f = E(nc.sbuf_tensor("onesr_f", [1, 128], F32))
        eps_sb = E(nc.sbuf_tensor("eps_sb", [128, 1], F32))
        strow = E(nc.sbuf_tensor("strow", [1, 3 * TS], F32))
        x0_sb = E(nc.sbuf_tensor("x0_sb", [128, 8], F32))
        x0sq = E(nc.sbuf_tensor("x0sq", [128, 8], F32))
        ln0f = E(nc.sbuf_tensor("ln0f", [128, 8], F32))
        ln0b = E(nc.sbuf_tensor("ln0b", [128, 8], BF16))
        gsc = E(nc.sbuf_tensor("gsc", [1, 16], F32))
        grow = E(nc.sbuf_tensor("grow", [1, 8], F32))
        grow2 = E(nc.sbuf_tensor("grow2", [1, 8], F32))
        g8_sb = E(nc.sbuf_tensor("g8_sb", [8, 1], F32))
        g8b_sb = E(nc.sbuf_tensor("g8b_sb", [8, 1], F32))
        gb_sb = E(nc.sbuf_tensor("gb_sb", [8, 4], F32))
        gexp_sb = E(nc.sbuf_tensor("gexp_sb", [128, 1], F32))
        rw_sb = E(nc.sbuf_tensor("rw_sb", [128, 64], BF16))
        ldn_sb = E(nc.sbuf_tensor("ldn_sb", [128, 1024], BF16))
        lup_sb = E(nc.sbuf_tensor("lup_sb", [128, 1024], BF16))
        oh8_sb = E(nc.sbuf_tensor("oh8_sb", [8, 128], F32))
        invf_sb = E(nc.sbuf_tensor("invf_sb", [128, 1], F32))
        sign_sb = E(nc.sbuf_tensor("sign_sb", [128, 1], F32))
        perm_sb = E(nc.sbuf_tensor("perm_sb", [128, 128], BF16))
        zrow_sb = E(nc.sbuf_tensor("zrow_sb", [1, 2 * TS], BF16))
        zrowf_sb = E(nc.sbuf_tensor("zrowf_sb", [1, 2 * TS], F32))
        bvec_sb = E(nc.sbuf_tensor("bvec_sb", [128, 32], F32))
        bvec2_sb = E(nc.sbuf_tensor("bvec2_sb", [128, 8], F32))
        iot_sb = E(nc.sbuf_tensor("iot_sb", [128, TS], F32))
        qoffc_sb = E(nc.sbuf_tensor("qoffc_sb", [128, 1], F32))
        qoffr_sb = E(nc.sbuf_tensor("qoffr_sb", [1, 1], F32))
        qoffi_sb = E(nc.sbuf_tensor("qoffi_sb", [1, 1], mybir.dt.int32))
        f0_sb = E(nc.sbuf_tensor("f0_sb", [128, 32], F32))
        zpt_sb = E(nc.sbuf_tensor("zpt_sb", [128, 10], F32))
        zqt_sb = E(nc.sbuf_tensor("zqt_sb", [128, 1], F32))
        bvb_sb = E(nc.sbuf_tensor("bvb_sb", [128, 8], BF16))
        au0_sb = E(nc.sbuf_tensor("au0_sb", [128, 8], BF16))
        d0g_sb = E(nc.sbuf_tensor("d0g_sb", [128, 1], BF16))
        maski_sb = E(nc.sbuf_tensor("maski_sb", [128, 4 * 512], mybir.dt.int8))
        AB = E(nc.sbuf_tensor("AB", [128, 36864], BF16))

        pb = [E(nc.psum_tensor(f"pb{i}", [128, 512], F32)) for i in range(7)]
        pbt = E(nc.psum_tensor("pbt", [128, 1024], BF16))

        for sname in ("dS", "dA", "dG", "t", "v", "s", "g", "c", "st",
                      "w0", "w1", "w2", "u0", "u1", "f0", "f1",
                      "e0", "e1", "e2", "e3", "r0", "a0"):
            p.sems[sname] = E(nc.semaphore(sname))

        p.dma("sync", x_res[:], bass.AP(x_in, 0, [[T, 128], [128 * T, NCK], [1, T]]), "dS")
        p.dma("sync", x0_sb[:], x0row.ap(), "dS")
        p.dma("sync", oh8_sb[:], c_oh8.ap(), "dS")
        p.dma("gpsimd", rw_sb[:], bass.AP(rw_t, 0, [[8, 128], [8 * 128, NCK], [1, 8]]), "dG")
        p.dma("gpsimd", ldn_sb[:], bass.AP(ldn, 0, [[128, 128], [128 * 128, NCK], [1, 128]]), "dG")
        p.dma("gpsimd", lup_sb[:], lup.ap(), "dG")
        dG_w = p.val("dG")
        dS_x = 16
        dS_all = p.val("dS")

        p.emit("vector", lambda e: e.memset(ones_bf[:], 1.0))
        p.emit("vector", lambda e: e.memset(ones_f[:], 1.0))
        p.emit("vector", lambda e: e.memset(onesr_bf[:], 1.0))
        p.emit("vector", lambda e: e.memset(onesr_f[:], 1.0))
        p.emit("vector", lambda e: e.memset(eps_sb[:], LN_EPS), inc=p.i1("v"))
        v_const = p.val("v")

        def VW():
            return ("v", p.val("v"))

        def SW():
            return ("s", p.val("s"))

        def TW():
            return ("t", p.val("t"))

        def emit_layernorm(ready):
            for ts in range(NSLICE):
                for ck in range(NCK):
                    co = ck * T + ts * TS
                    p.emit("scalar",
                           lambda e, co=co: e.copy(ybuf[:, co:co + TS], x_res[:, co:co + TS]),
                           wait=(ready if (ts == 0 and ck == 0) else ()),
                           inc=(p.i1("s") if ck == NCK - 1 else ()))
                s_cast = p.val("s")
                for ck in range(NCK):
                    co = ck * T + ts * TS
                    sl = ck * TS
                    p.emit("scalar",
                           lambda e, co=co, sl=sl: e.activation(
                               sq_sb[:, sl:sl + TS], x_res[:, co:co + TS], AF.Square),
                           wait=([TW()] if ck == 0 and ts > 0 else ()),
                           inc=(p.i1("s") if ck == NCK - 1 else ()))
                s_sq = p.val("s")
                for ck in range(NCK):
                    co = ck * T + ts * TS
                    p.emit("tensor",
                           lambda e, co=co, ck=ck: e.matmul(
                               pb[4][0:1, :], ones_bf[:, 0:1], ybuf[:, co:co + TS],
                               start=(ck == 0), stop=(ck == NCK - 1)),
                           wait=([("s", s_cast), VW()] if ck == 0 else ()))
                for ck in range(NCK):
                    sl = ck * TS
                    p.emit("tensor",
                           lambda e, sl=sl, ck=ck: e.matmul(
                               pb[5][0:1, :], ones_bf[:, 0:1], sq_sb[:, sl:sl + TS],
                               start=(ck == 0), stop=(ck == NCK - 1)),
                           wait=([("s", s_sq)] if ck == 0 else ()),
                           inc=(p.i1("t") if ck == NCK - 1 else ()))
                t_stats = p.val("t")

                mrow = strow[0:1, 0:TS]
                rrow = strow[0:1, TS:2 * TS]
                vrow = strow[0:1, 2 * TS:3 * TS]
                p.emit("vector",
                       lambda e: e.tensor_scalar(mrow, pb[4][0:1, :], 1.0 / D, None, ALU.mult),
                       wait=[("t", t_stats)], inc=p.i1("v"))
                p.emit("vector", lambda e: e.tensor_mul(vrow, mrow, mrow),
                       wait=[VW()], inc=p.i1("v"))
                p.emit("vector",
                       lambda e: e.scalar_tensor_tensor(vrow, pb[5][0:1, :], 1.0 / D, vrow,
                                                        ALU.mult, ALU.subtract),
                       wait=[VW()], inc=p.i1("v"))
                v_var = p.val("v")
                p.emit("scalar",
                       lambda e: e.activation(vrow, vrow, AF.Sqrt, bias=eps_sb[0:1, 0:1]),
                       wait=[("v", v_var)], inc=p.i1("s"))
                s_sd = p.val("s")
                p.emit("vector", lambda e: e.reciprocal(rrow, vrow),
                       wait=[("s", s_sd)], inc=p.i1("v"))
                v_r = p.val("v")
                p.emit("tensor",
                       lambda e: e.matmul(pb[6][:, :], onesr_f[0:1, :], mrow,
                                          start=True, stop=True),
                       wait=[("v", v_r)])
                p.emit("tensor",
                       lambda e: e.matmul(pb[4][:, :], onesr_f[0:1, :], rrow,
                                          start=True, stop=True),
                       inc=p.i1("t"))
                t_bc = p.val("t")
                for ck in range(NCK):
                    co = ck * T + ts * TS
                    p.emit("vector",
                           lambda e, co=co: e.scalar_tensor_tensor(
                               ybuf[:, co:co + TS], x_res[:, co:co + TS], 1.0,
                               pb[6][:, :], ALU.mult, ALU.subtract),
                           wait=([("t", t_bc)] if ck == 0 else ()),
                           inc=(p.i1("v") if ck == NCK - 1 else ()))
                v_p1 = p.val("v")
                for ck in range(NCK):
                    co = ck * T + ts * TS
                    p.emit("vector",
                           lambda e, co=co: e.tensor_mul(
                               ybuf[:, co:co + TS], ybuf[:, co:co + TS], pb[4][:, :]),
                           wait=([("v", v_p1)] if ck == 0 else ()),
                           inc=(p.i1("v") if ck == NCK - 1 else ()))
            return p.val("v")

        def emit_gates(ready):
            def ve(f, wait=()):
                p.emit("vector", f, wait=list(wait) + [VW()], inc=p.i1("v"))

            def se(f, wait=()):
                p.emit("scalar", f, wait=list(wait) + [SW()], inc=p.i1("s"))

            se(lambda e: e.activation(x0sq[:], x0_sb[:], AF.Square), wait=ready)
            s_sq = p.val("s")
            p.emit("tensor",
                   lambda e: e.matmul(pb[0][0:1, 0:8], ones_f[:, 0:1], x0_sb[:],
                                      start=True, stop=True),
                   wait=[("s", s_sq), VW()])
            p.emit("tensor",
                   lambda e: e.matmul(pb[0][0:1, 8:16], ones_f[:, 0:1], x0sq[:],
                                      start=True, stop=True),
                   inc=p.i1("t"))
            t_s = p.val("t")
            ve(lambda e: e.reduce_sum(gsc[0:1, 0:1], pb[0][0:1, 0:8],
                                      axis=mybir.AxisListType.X), wait=[("t", t_s)])
            ve(lambda e: e.reduce_sum(gsc[0:1, 1:2], pb[0][0:1, 8:16],
                                      axis=mybir.AxisListType.X))
            ve(lambda e: e.tensor_scalar(gsc[0:1, 0:1], gsc[0:1, 0:1], 1.0 / D, None, ALU.mult))
            ve(lambda e: e.tensor_mul(gsc[0:1, 2:3], gsc[0:1, 0:1], gsc[0:1, 0:1]))
            ve(lambda e: e.scalar_tensor_tensor(gsc[0:1, 2:3], gsc[0:1, 1:2], 1.0 / D,
                                                gsc[0:1, 2:3], ALU.mult, ALU.subtract))
            se(lambda e: e.activation(gsc[0:1, 2:3], gsc[0:1, 2:3], AF.Sqrt,
                                      bias=eps_sb[0:1, 0:1]), wait=[VW()])
            ve(lambda e: e.reciprocal(gsc[0:1, 3:4], gsc[0:1, 2:3]), wait=[SW()])
            v_r0 = p.val("v")
            p.emit("tensor", lambda e: e.matmul(pb[1][:, 0:1], onesr_f[0:1, :],
                                                gsc[0:1, 0:1], start=True, stop=True),
                   wait=[("v", v_r0)])
            p.emit("tensor", lambda e: e.matmul(pb[1][:, 1:2], onesr_f[0:1, :],
                                                gsc[0:1, 3:4], start=True, stop=True),
                   inc=p.i1("t"))
            t_bc = p.val("t")
            ve(lambda e: e.tensor_scalar(ln0f[:], x0_sb[:], pb[1][:, 0:1],
                                         pb[1][:, 1:2], ALU.subtract, ALU.mult),
               wait=[("t", t_bc)])
            se(lambda e: e.copy(ln0b[:], ln0f[:]), wait=[VW()])
            s_ln0 = p.val("s")
            for ck in range(NCK):
                p.emit("tensor",
                       lambda e, ck=ck: e.matmul(pb[2][0:8, 0:1],
                                                 rw_sb[:, ck * 8:(ck + 1) * 8],
                                                 ln0b[:, ck:ck + 1],
                                                 start=(ck == 0), stop=(ck == NCK - 1)),
                       wait=([("s", s_ln0), ("dG", dG_w)] if ck == 0 else ()))
            for ck in range(NCK):
                p.emit("tensor",
                       lambda e, ck=ck: e.matmul(pb[3][0:1, 0:8],
                                                 ln0b[:, ck:ck + 1],
                                                 rw_sb[:, ck * 8:(ck + 1) * 8],
                                                 start=(ck == 0), stop=(ck == NCK - 1)),
                       inc=(p.i1("t") if ck == NCK - 1 else ()))
            t_log = p.val("t")
            ve(lambda e: e.reduce_max(gsc[0:1, 4:5], pb[3][0:1, 0:8],
                                      axis=mybir.AxisListType.X), wait=[("t", t_log)])
            ve(lambda e: e.tensor_scalar(gsc[0:1, 5:6], gsc[0:1, 4:5], -1.0, None, ALU.mult))
            ve(lambda e: e.tensor_scalar(grow[0:1, :], pb[3][0:1, 0:8],
                                         gsc[0:1, 4:5], None, ALU.is_equal))
            ve(lambda e: e.scalar_tensor_tensor(grow[0:1, :], grow[0:1, :], -1e30,
                                                pb[3][0:1, 0:8], ALU.mult, ALU.add))
            ve(lambda e: e.reduce_max(gsc[0:1, 6:7], grow[0:1, :],
                                      axis=mybir.AxisListType.X))
            se(lambda e: e.activation(grow[0:1, :], pb[3][0:1, 0:8], AF.Exp,
                                      bias=gsc[0:1, 5:6]), wait=[VW()])
            ve(lambda e: e.tensor_scalar(grow2[0:1, :], pb[3][0:1, 0:8],
                                         gsc[0:1, 6:7], None, ALU.is_ge), wait=[SW()])
            ve(lambda e: e.tensor_mul(grow[0:1, :], grow[0:1, :], grow2[0:1, :]))
            ve(lambda e: e.reduce_sum(gsc[0:1, 7:8], grow[0:1, :],
                                      axis=mybir.AxisListType.X))
            ve(lambda e: e.tensor_scalar(gsc[0:1, 9:10], gsc[0:1, 7:8], 2.0, None, ALU.mult))
            ve(lambda e: e.reciprocal(gsc[0:1, 8:9], gsc[0:1, 9:10]))
            v_rz = p.val("v")
            p.emit("tensor", lambda e: e.matmul(pb[1][0:8, 2:3], onesr_f[0:1, 0:8],
                                                gsc[0:1, 5:6], start=True, stop=True),
                   wait=[("v", v_rz)])
            p.emit("tensor", lambda e: e.matmul(pb[1][0:8, 3:4], onesr_f[0:1, 0:8],
                                                gsc[0:1, 6:7], start=True, stop=True))
            p.emit("tensor", lambda e: e.matmul(pb[1][0:8, 4:5], onesr_f[0:1, 0:8],
                                                gsc[0:1, 8:9], start=True, stop=True),
                   inc=p.i1("t"))
            t_bc2 = p.val("t")
            ve(lambda e: e.tensor_copy(gb_sb[:, 0:3], pb[1][0:8, 2:5]), wait=[("t", t_bc2)])
            se(lambda e: e.activation(g8_sb[:], pb[2][0:8, 0:1], AF.Exp,
                                      bias=gb_sb[0:8, 0:1]), wait=[VW()])
            ve(lambda e: e.tensor_scalar(g8b_sb[:], pb[2][0:8, 0:1],
                                         gb_sb[0:8, 1:2], None, ALU.is_ge), wait=[SW()])
            ve(lambda e: e.tensor_mul(g8_sb[:], g8_sb[:], g8b_sb[:]))
            ve(lambda e: e.tensor_scalar(g8_sb[:], g8_sb[:], gb_sb[0:8, 2:3], None, ALU.mult))
            v_g8 = p.val("v")
            p.emit("tensor", lambda e: e.matmul(pb[1][:, 5:6], oh8_sb[:], g8_sb[:],
                                                start=True, stop=True),
                   wait=[("v", v_g8)], inc=p.i1("t"))
            t_ge = p.val("t")
            ve(lambda e: e.tensor_copy(gexp_sb[:], pb[1][:, 5:6]), wait=[("t", t_ge)])
            return p.val("v")

        def emit_x0_track():
            for ck in range(NCK):
                p.emit("tensor",
                       lambda e, ck=ck: e.matmul(
                           pb[0][:, 0:1], ldn_sb[:, ck * 128:(ck + 1) * 128],
                           ln0b[:, ck:ck + 1], start=(ck == 0), stop=(ck == NCK - 1)),
                       wait=([VW(), SW()] if ck == 0 else ()),
                       inc=(p.i1("t") if ck == NCK - 1 else ()))
            t_d0 = p.val("t")
            p.emit("vector",
                   lambda e: e.tensor_scalar(d0g_sb[:], pb[0][:, 0:1], gexp_sb[:, 0:1],
                                             None, ALU.mult),
                   wait=[("t", t_d0)], inc=p.i1("v"))
            v_d0 = p.val("v")
            for f in range(NCK):
                p.emit("tensor",
                       lambda e, f=f: e.matmul(
                           pb[1][:, f:f + 1], lup_sb[:, f * 128:(f + 1) * 128],
                           d0g_sb[:], start=True, stop=True),
                       wait=([("v", v_d0)] if f == 0 else ()),
                       inc=(p.i1("t") if f == NCK - 1 else ()))
            t_w0 = p.val("t")
            p.emit("vector",
                   lambda e: e.scalar_tensor_tensor(x0_sb[:], ln0f[:], 0.5, x0_sb[:],
                                                    ALU.mult, ALU.add),
                   wait=[("t", t_w0)], inc=p.i1("v"))
            p.emit("vector",
                   lambda e: e.tensor_add(x0_sb[:], x0_sb[:], pb[1][:, 0:8]),
                   wait=[VW()], inc=p.i1("v"))

        def emit_l0_ffn():
            for ts in range(NSLICE):
                bank = pb[ts % 2]
                for ck in range(NCK):
                    co = ck * T + ts * TS
                    p.emit("tensor",
                           lambda e, co=co, ck=ck, bank=bank: e.matmul(
                               bank[:, :], ldn_sb[:, ck * 128:(ck + 1) * 128],
                               ybuf[:, co:co + TS],
                               start=(ck == 0), stop=(ck == NCK - 1)),
                           wait=([VW()] if ck == 0 else ()),
                           inc=(p.i1("t") if ck == NCK - 1 else ()))
                t_dn = p.val("t")
                p.emit("vector",
                       lambda e, ts=ts, bank=bank: e.tensor_scalar(
                           dn_sb[:, ts * TS:ts * TS + TS], bank[:, :], gexp_sb[:, 0:1],
                           None, ALU.mult),
                       wait=[("t", t_dn)], inc=p.i1("v"))
            v_dn = p.val("v")
            for ck in range(NCK):
                for ts in range(NSLICE):
                    co = ck * T + ts * TS
                    last = (ck == NCK - 1 and ts == NSLICE - 1)
                    p.emit("vector",
                           lambda e, co=co: e.scalar_tensor_tensor(
                               x_res[:, co:co + TS], ybuf[:, co:co + TS], 0.5,
                               x_res[:, co:co + TS], ALU.mult, ALU.add),
                           inc=(p.i1("v") if last else ()))
            v_pa = p.val("v")
            for ts in range(NSLICE):
                for f in range(NCK):
                    bank = pb[2 + (f % 2)]
                    co = f * T + ts * TS
                    p.emit("tensor",
                           lambda e, f=f, ts=ts, bank=bank: e.matmul(
                               bank[:, :], lup_sb[:, f * 128:(f + 1) * 128],
                               dn_sb[:, ts * TS:ts * TS + TS],
                               start=True, stop=True),
                           wait=[VW()],
                           inc=p.i1("t"))
                    t_up = p.val("t")
                    p.emit("vector",
                           lambda e, co=co, bank=bank: e.scalar_tensor_tensor(
                               x_res[:, co:co + TS], bank[:, :], 1.0,
                               x_res[:, co:co + TS], ALU.mult, ALU.add),
                           wait=[("t", t_up), ("v", v_pa)],
                           inc=p.i1("v"))

        def emit_l1_ffn():
            DSC = 0.5 / (127.0 * 127.0)
            ar = Arena(AB, 36864)
            wgc = [ar.take(1024) for _ in range(3)]
            wuc = [ar.take(1024) for _ in range(3)]
            wdc = [ar.take(12 * 128) for _ in range(2)]
            slu = [ar.take(TS) for _ in range(2)]
            hp_sb = ar.take(24 * 1024)
            last_wt = {0: p.val("t"), 1: p.val("t"), 2: p.val("t")}
            for hb in range(2):
                _iw = [("s", p.val("s")), ("v", p.val("v"))]
                gfree = {0: _iw, 1: _iw, 4: _iw, 5: _iw}
                ufree = {2: _iw, 3: _iw, 6: _iw}
                slufree = [p.val("v")] * 2
                for m in range(24):
                    r = m % 3
                    wsem = f"w{r}"
                    p.dma("gpsimd", wgc[r][:],
                          bass.AP(wg_r, m * 8 * 128 * 128,
                                  [[128, 128], [128 * 128, 8], [1, 128]]),
                          wsem, wait=[("t", last_wt[r])])
                    p.dma("gpsimd", wuc[r][:],
                          bass.AP(wu_r, m * 8 * 128 * 128,
                                  [[128, 128], [128 * 128, 8], [1, 128]]),
                          wsem)
                    w_tot = p.val(wsem)
                    for tsl in range(2):
                        ts = 2 * hb + tsl
                        gb = (4 + tsl) if m % 2 == 0 else (0 + tsl)
                        ub = (2 + tsl) if m % 2 == 0 else (6 if tsl == 0 else 2)
                        gbank = pb[gb]
                        ubank = pb[ub]
                        for ck in range(NCK):
                            co = ck * T + ts * TS
                            p.emit("tensor",
                                   lambda e, co=co, ck=ck, gbank=gbank, r=r: e.matmul(
                                       gbank[:, :], wgc[r][:, ck * 128:(ck + 1) * 128],
                                       ybuf[:, co:co + TS],
                                       start=(ck == 0), stop=(ck == NCK - 1)),
                                   wait=(([(wsem, w_tot)] + list(gfree[gb]))
                                         if ck == 0 else ()))
                        for ck in range(NCK):
                            co = ck * T + ts * TS
                            p.emit("tensor",
                                   lambda e, co=co, ck=ck, ubank=ubank, r=r: e.matmul(
                                       ubank[:, :], wuc[r][:, ck * 128:(ck + 1) * 128],
                                       ybuf[:, co:co + TS],
                                       start=(ck == 0), stop=(ck == NCK - 1)),
                                   wait=(list(ufree[ub]) if ck == 0 else ()),
                                   inc=(p.i1("t") if ck == NCK - 1 else ()))
                        t_gu = p.val("t")
                        p.emit("scalar",
                               lambda e, tsl=tsl, gbank=gbank: e.activation(
                                   slu[tsl][:], gbank[:, :], AF.Silu, scale=1.0 / 127.0),
                               wait=[("t", t_gu), ("v", slufree[tsl])], inc=p.i1("s"))
                        s_sl = p.val("s")
                        gfree[gb] = [("s", s_sl)]
                        hco = m * 1024 + tsl * TS
                        p.emit("vector",
                               lambda e, hco=hco, tsl=tsl, ubank=ubank: e.tensor_mul(
                                   hp_sb[:, hco:hco + TS], slu[tsl][:], ubank[:, :]),
                               wait=[("s", s_sl)], inc=p.i1("v"))
                        ufree[ub] = [("v", p.val("v"))]
                        slufree[tsl] = p.val("v")
                    last_wt[r] = p.val("t")
                for fh in range(2 * NCK):
                    f, half = fh // 2, fh % 2
                    r2 = fh % 2
                    usem = f"u{r2}"
                    p.dma("gpsimd", wdc[r2][:],
                          bass.AP(wd_r, (f * 24 + half * 12) * 128 * 128,
                                  [[128, 128], [128 * 128, 12], [1, 128]]),
                          usem, wait=[TW()])
                    u_tot = p.val(usem)
                    for tsl in range(2):
                        ts = 2 * hb + tsl
                        dbank = pb[2 * (f % 2) + tsl]
                        for kh in range(12):
                            khg = half * 12 + kh
                            hco = khg * 1024 + tsl * TS
                            p.emit("tensor",
                                   lambda e, hco=hco, kh=kh, dbank=dbank, r2=r2, half=half: e.matmul(
                                       dbank[:, :], wdc[r2][:, kh * 128:(kh + 1) * 128],
                                       hp_sb[:, hco:hco + TS],
                                       start=(kh == 0 and half == 0),
                                       stop=(kh == 11 and half == 1)),
                                   wait=([(usem, u_tot)] + ([VW(), SW()] if half == 0 else [])
                                         if kh == 0 else ()),
                                   inc=(p.i1("t") if (kh == 11 and half == 1) else ()))
                        if half == 1:
                            t_dn = p.val("t")
                            co = f * T + ts * TS
                            p.emit("vector",
                                   lambda e, co=co, dbank=dbank: e.scalar_tensor_tensor(
                                       x_res[:, co:co + TS], dbank[:, :], DSC,
                                       x_res[:, co:co + TS], ALU.mult, ALU.add),
                                   wait=[("t", t_dn), VW()], inc=p.i1("v"))

        def emit_l1_attn():
            if DUMP:
                p.dma("sync", dbg["dxent"].ap(), x_res[:], "dS",
                      wait=[VW(), SW(), TW()])
            SC8 = 1.0 / 8.0
            TWOPI = 6.283185307179586
            PI = 3.141592653589793
            HPI = 1.5707963267948966
            ar = Arena(AB, 36864)
            buvb = ar.take(8)
            ckv = ar.take(4096)
            cqv = ar.take(2048)
            COSq = ar.take(2048)
            RBq = ar.take(2048)
            COS1 = ar.take(512)
            RB1 = ar.take(512)
            rt1 = ar.take(512)
            rt2 = ar.take(512)
            wuk_sb = ar.take(768)
            wuv_sb = ar.take(1024)
            wuq_sb = ar.take(768)
            wqr_sb = ar.take(256)
            mark_pers = ar.off
            ya = ar.take(4096)
            rraw = ar.take(4096)
            wdkv_sb = ar.take(1024)
            wdq_sb = ar.take(1024)
            wkr_sb = ar.take(1024)

            p.dma("gpsimd", wdkv_sb[:], bass.AP(wdkv, 0, [[128, 128], [128 * 128, 8], [1, 128]]), "dG", wait=[TW(), VW()])
            p.dma("gpsimd", wdq_sb[:], bass.AP(wdq, 0, [[128, 128], [128 * 128, 8], [1, 128]]), "dG")
            p.dma("gpsimd", wuk_sb[:], wuk.ap(), "dG")
            p.dma("gpsimd", wuv_sb[:], wuv.ap(), "dG")
            p.dma("gpsimd", wuq_sb[:], wuq.ap(), "dG")
            p.dma("gpsimd", wqr_sb[:], wqr.ap(), "dG")
            p.dma("gpsimd", perm_sb[:], c_perm.ap(), "dG")
            p.dma("gpsimd", buvb[0:128, 0:8], buv_c.ap(), "dG")
            dG_w1 = p.val("dG")
            p.dma("sync", bvec_sb[:, 0:1], bdkv.ap(), "dS", wait=[VW(), SW()])
            p.dma("sync", bvec_sb[:, 1:2], bdq.ap(), "dS")
            p.dma("sync", bvec_sb[:, 2:10], bkf.ap(), "dS")
            p.dma("sync", bvec_sb[:, 10:18], bqf.ap(), "dS")
            p.dma("sync", bvec_sb[:, 18:20], bkr_c.ap(), "dS")
            p.dma("sync", bvec_sb[:, 20:22], bqr_c.ap(), "dS")
            p.dma("sync", bvec_sb[:, 22:30], bo1_c.ap(), "dS")
            p.dma("sync", invf_sb[:], c_invf.ap(), "dS")
            p.dma("sync", sign_sb[:], c_sign.ap(), "dS")
            p.dma("sync", qoffr_sb[:], c_qoff.ap(), "dS")
            dS_b1 = p.val("dS")
            p.emit("tensor", lambda e: e.matmul(pb[2][:, 0:1], onesr_f[0:1, :],
                                                qoffr_sb[0:1, 0:1], start=True, stop=True),
                   wait=[("dS", dS_b1), VW()], inc=p.i1("t"))
            t_qo = p.val("t")
            p.emit("vector", lambda e: e.tensor_copy(qoffc_sb[:], pb[2][:, 0:1]),
                   wait=[("t", t_qo)], inc=p.i1("v"))

            v_ln = p.val("v")
            p.dma("gpsimd", cc1_in.ap(), ybuf[:], "u0", wait=[("v", v_ln)])
            dG_cc = p.val("u0")
            p.emit("gpsimd",
                   lambda e: e.collective_compute(
                       "AllGather", ALU.bypass,
                       replica_groups=[[0, 1], [2, 3], [4, 5], [6, 7]],
                       ins=[cc1_in.ap().opt()], outs=[cc1_out.ap().opt()]),
                   wait=[("u0", dG_cc)], inc=p.i1("c"))
            c_ag = p.val("c")

            def build_rope_slice(base, add_qoff, cosv, coff, rbv, roff):
                ang = iot_sb[:, 0:TS]
                p.emit("gpsimd",
                       lambda e, base=base: e.iota(ang, [[1, TS]], base=base,
                                                   channel_multiplier=0,
                                                   allow_small_or_imprecise_dtypes=True),
                       wait=[VW(), SW(), ("dS", dS_b1)], inc=p.i1("g"))
                g_i = p.val("g")
                if add_qoff:
                    p.emit("vector", lambda e: e.tensor_scalar(ang, ang, qoffc_sb[:, 0:1], None, ALU.add),
                           wait=[("g", g_i)], inc=p.i1("v"))
                else:
                    p.emit("vector", lambda e: e.tensor_copy(ang, ang),
                           wait=[("g", g_i)], inc=p.i1("v"))
                p.emit("vector", lambda e: e.tensor_scalar(ang, ang, invf_sb[:, 0:1], None, ALU.mult),
                       wait=[VW()], inc=p.i1("v"))
                for KK in (16 * TWOPI, 8 * TWOPI, 4 * TWOPI, 2 * TWOPI, TWOPI):
                    p.emit("vector", lambda e, KK=KK: e.tensor_scalar(rt1[:, 0:TS], ang, KK, None, ALU.is_ge),
                           wait=[VW()], inc=p.i1("v"))
                    p.emit("vector", lambda e, KK=KK: e.scalar_tensor_tensor(ang, rt1[:, 0:TS], -KK, ang,
                                                                             ALU.mult, ALU.add),
                           wait=[VW()], inc=p.i1("v"))
                p.emit("vector", lambda e: e.tensor_scalar(rt1[:, 0:TS], ang, PI, None, ALU.is_ge),
                       wait=[VW()], inc=p.i1("v"))
                p.emit("vector", lambda e: e.scalar_tensor_tensor(ang, rt1[:, 0:TS], -TWOPI, ang,
                                                                  ALU.mult, ALU.add),
                       wait=[VW()], inc=p.i1("v"))
                v_r = p.val("v")
                p.emit("scalar", lambda e: e.activation(rt2[:, 0:TS], ang, AF.Sin),
                       wait=[("v", v_r)], inc=p.i1("s"))
                p.emit("vector",
                       lambda e, rbv=rbv, roff=roff: e.tensor_scalar(
                           rbv[:, roff:roff + TS], rt2[:, 0:TS], sign_sb[:, 0:1], None, ALU.mult),
                       wait=[SW()], inc=p.i1("v"))
                p.emit("vector", lambda e: e.tensor_scalar(ang, ang, HPI, None, ALU.add),
                       wait=[VW()], inc=p.i1("v"))
                p.emit("vector", lambda e: e.tensor_scalar(rt1[:, 0:TS], ang, PI, None, ALU.is_ge),
                       wait=[VW()], inc=p.i1("v"))
                p.emit("vector", lambda e: e.scalar_tensor_tensor(ang, rt1[:, 0:TS], -TWOPI, ang,
                                                                  ALU.mult, ALU.add),
                       wait=[VW()], inc=p.i1("v"))
                v_c = p.val("v")
                p.emit("scalar",
                       lambda e, cosv=cosv, coff=coff: e.activation(
                           cosv[:, coff:coff + TS], ang, AF.Sin),
                       wait=[("v", v_c)], inc=p.i1("s"))

            def apply_rope_slice(buf, boff, nr, cosv, coff, rbv, roff):
                p.emit("tensor",
                       lambda e, boff=boff, nr=nr: e.matmul(
                           pb[3][0:nr, 0:TS], perm_sb[0:nr, 0:nr],
                           buf[0:nr, boff:boff + TS], start=True, stop=True),
                       wait=[VW(), SW(), ("dG", dG_w1)], inc=p.i1("t"))
                t_sh = p.val("t")
                p.emit("vector",
                       lambda e, nr=nr, rbv=rbv, roff=roff: e.tensor_mul(
                           rt1[0:nr, 0:TS], pb[3][0:nr, 0:TS], rbv[0:nr, roff:roff + TS]),
                       wait=[("t", t_sh)], inc=p.i1("v"))
                p.emit("vector",
                       lambda e, boff=boff, nr=nr, cosv=cosv, coff=coff: e.tensor_mul(
                           rt2[0:nr, 0:TS], buf[0:nr, boff:boff + TS], cosv[0:nr, coff:coff + TS]),
                       wait=[VW()], inc=p.i1("v"))
                p.emit("vector",
                       lambda e, boff=boff, nr=nr: e.tensor_add(
                           buf[0:nr, boff:boff + TS], rt1[0:nr, 0:TS], rt2[0:nr, 0:TS]),
                       wait=[VW()], inc=p.i1("v"))

            for ts2 in range(8):
                r, tsl = ts2 // 4, ts2 % 4
                p.dma("scalar", ya[:],
                      bass.AP(cc1_out, r * 128 * 16384 + tsl * TS,
                              [[16384, 128], [T, NCK], [1, TS]]),
                      "dA", wait=[("c", c_ag), TW()])
                dA_y = p.val("dA")
                bank = pb[ts2 % 2]
                for ck in range(NCK):
                    p.emit("tensor",
                           lambda e, ck=ck, bank=bank: e.matmul(
                               bank[:, :], wdkv_sb[:, ck * 128:(ck + 1) * 128],
                               ya[:, ck * TS:(ck + 1) * TS],
                               start=(ck == 0), stop=(ck == NCK - 1)),
                           wait=([("dA", dA_y), ("dG", dG_w1), VW()] if ck == 0 else ()),
                           inc=(p.i1("t") if ck == NCK - 1 else ()))
                t_ck = p.val("t")
                p.emit("vector",
                       lambda e, ts2=ts2, bank=bank: e.tensor_scalar(
                           ckv[:, ts2 * TS:(ts2 + 1) * TS], bank[:, :],
                           bvec_sb[:, 0:1], None, ALU.add),
                       wait=[("t", t_ck), ("dS", dS_b1)], inc=p.i1("v"))
            for tsl in range(4):
                bank = pb[2 + (tsl % 2)]
                for ck in range(NCK):
                    p.emit("tensor",
                           lambda e, ck=ck, tsl=tsl, bank=bank: e.matmul(
                               bank[:, :], wdq_sb[:, ck * 128:(ck + 1) * 128],
                               ybuf[:, ck * T + tsl * TS:ck * T + tsl * TS + TS],
                               start=(ck == 0), stop=(ck == NCK - 1)),
                           wait=([VW()] if ck == 0 else ()),
                           inc=(p.i1("t") if ck == NCK - 1 else ()))
                t_cq = p.val("t")
                p.emit("vector",
                       lambda e, tsl=tsl, bank=bank: e.tensor_scalar(
                           cqv[:, tsl * TS:(tsl + 1) * TS], bank[:, :],
                           bvec_sb[:, 1:2], None, ALU.add),
                       wait=[("t", t_cq)], inc=p.i1("v"))

            for rc in range(2):
                p.dma("gpsimd", wkr_sb[:],
                      bass.AP(wkr, rc * 128, [[256, 128], [256 * 128, 8], [1, 128]]),
                      "dG", wait=[TW()])
                dG_kr = p.val("dG")
                for ts2 in range(8):
                    r, tsl = ts2 // 4, ts2 % 4
                    p.dma("scalar", ya[:],
                          bass.AP(cc1_out, r * 128 * 16384 + tsl * TS,
                                  [[16384, 128], [T, NCK], [1, TS]]),
                          "dA", wait=[TW()])
                    dA_y = p.val("dA")
                    bank = pb[ts2 % 2]
                    for ck in range(NCK):
                        p.emit("tensor",
                               lambda e, ck=ck, bank=bank: e.matmul(
                                   bank[:, :], wkr_sb[:, ck * 128:(ck + 1) * 128],
                                   ya[:, ck * TS:(ck + 1) * TS],
                                   start=(ck == 0), stop=(ck == NCK - 1)),
                               wait=([("dA", dA_y), ("dG", dG_kr), VW()] if ck == 0 else ()),
                               inc=(p.i1("t") if ck == NCK - 1 else ()))
                    t_kr = p.val("t")
                    p.emit("vector",
                           lambda e, ts2=ts2, rc=rc, bank=bank: e.tensor_scalar(
                               rraw[:, ts2 * TS:(ts2 + 1) * TS], bank[:, :],
                               bvec_sb[:, 18 + rc:19 + rc], None, ALU.add),
                           wait=([("t", t_kr)] + ([("dS", p.val("dS"))] if ts2 == 0 else [])),
                           inc=p.i1("v"))
                for sl in range(8):
                    build_rope_slice(sl * TS, False, COS1, 0, RB1, 0)
                    apply_rope_slice(rraw, sl * TS, 128, COS1, 0, RB1, 0)
                v_rope = p.val("v")
                p.dma("sync", bass.AP(krd, rc * 128 * 4096, [[4096, 128], [1, 4096]]),
                      rraw[:], "dS", wait=[("v", v_rope)])
            dS_krd = p.val("dS")

            for sl in range(4):
                build_rope_slice(sl * TS, True, COSq, sl * TS, RBq, sl * TS)

            if DUMP:
                p.dma("sync", dbg["dck"].ap(), ckv[:], "dS", wait=[VW()])
                p.dma("sync", dbg["dcq"].ap(), cqv[:], "dS")
            ar.reset(mark_pers)
            kwv = ar.take(4096)
            qwv = ar.take(2048)
            v65 = ar.take(32 * 130)
            qraw = ar.take(2048)
            auwA = ar.take(2048)
            auwB = ar.take(2048)
            eT = [ar.take(512) for _ in range(4)]
            zbs = ar.take(512)
            ABC = 36864

            for w in range(8):
                hA, hB = 2 * w, 2 * w + 1
                p.emit("vector",
                       lambda e: e.memset(
                           bass.AP(AB, v65.off + 64, [[ABC, 128], [130, 32], [65, 2], [1, 1]]),
                           1.0),
                       wait=([TW()] + ([("dS", p.val("dS")), ("dA", p.val("dA")),
                                        ("dG", p.val("dG")), SW()] if (w == 0 or DUMP) else [])),
                       inc=p.i1("v"))
                for tt in range(32):
                    bank = pb[tt % 2]
                    p.emit("tensor",
                           lambda e, tt=tt, w=w, bank=bank: e.matmul(
                               bank[:, 0:128], ckv[:, tt * 128:(tt + 1) * 128],
                               wuv_sb[:, w * 128:(w + 1) * 128],
                               start=True, stop=True),
                           wait=[VW()], inc=p.i1("t"))
                    t_v = p.val("t")
                    p.emit("vector",
                           lambda e, tt=tt, bank=bank: e.tensor_copy(
                               bass.AP(AB, v65.off + tt * 130, [[ABC, 128], [65, 2], [1, 64]]),
                               bank[:, 0:128]),
                           wait=[("t", t_v)], inc=p.i1("v"))
                for ts2 in range(8):
                    bank = pb[2 + ts2 % 2]
                    for hh in range(2):
                        h = 2 * w + hh
                        p.emit("tensor",
                               lambda e, ts2=ts2, h=h, hh=hh, bank=bank: e.matmul(
                                   bank[64 * hh:64 * hh + 48, :],
                                   wuk_sb[:, h * 48:h * 48 + 48],
                                   ckv[:, ts2 * TS:(ts2 + 1) * TS],
                                   start=True, stop=True,
                                   tile_position=(0, 64 * hh)),
                               wait=([VW()] if hh == 0 else ()),
                               inc=(p.i1("t") if hh == 1 else ()))
                    t_kn = p.val("t")
                    for hh in range(2):
                        p.emit("vector",
                               lambda e, ts2=ts2, hh=hh, w=w, bank=bank: e.tensor_scalar(
                                   kwv[64 * hh:64 * hh + 48, ts2 * TS:(ts2 + 1) * TS],
                                   bank[64 * hh:64 * hh + 48, :],
                                   bvec_sb[64 * hh:64 * hh + 48, 2 + w:3 + w],
                                   None, ALU.add),
                               wait=([("t", t_kn)] if hh == 0 else ()),
                               inc=(p.i1("v") if hh == 1 else ()))
                v_kn = p.val("v")
                for hh in range(2):
                    h = 2 * w + hh
                    rc, rr = h // 8, (h % 8) * 16
                    p.dma("sync", kwv[48 + 64 * hh:64 + 64 * hh, 0:4096],
                          bass.AP(krd, rc * 128 * 4096 + rr * 4096, [[4096, 16], [1, 4096]]),
                          "dS", wait=[("v", v_kn), TW(), ("dS", p.val("dS"))])
                dS_krot = p.val("dS")
                for tsl in range(4):
                    bank = pb[4 + tsl % 2]
                    for hh in range(2):
                        h = 2 * w + hh
                        p.emit("tensor",
                               lambda e, tsl=tsl, h=h, hh=hh, bank=bank: e.matmul(
                                   bank[64 * hh:64 * hh + 48, :],
                                   wuq_sb[:, h * 48:h * 48 + 48],
                                   cqv[:, tsl * TS:(tsl + 1) * TS],
                                   start=True, stop=True,
                                   tile_position=(0, 64 * hh)),
                               wait=([VW()] if hh == 0 else ()),
                               inc=(p.i1("t") if hh == 1 else ()))
                    t_qn = p.val("t")
                    for hh in range(2):
                        p.emit("vector",
                               lambda e, tsl=tsl, hh=hh, w=w, bank=bank: e.tensor_scalar(
                                   qwv[64 * hh:64 * hh + 48, tsl * TS:(tsl + 1) * TS],
                                   bank[64 * hh:64 * hh + 48, :],
                                   bvec_sb[64 * hh:64 * hh + 48, 10 + w:11 + w],
                                   None, ALU.add),
                               wait=([("t", t_qn)] if hh == 0 else ()),
                               inc=(p.i1("v") if hh == 1 else ()))
                for tsl in range(4):
                    bank = pb[4 + tsl % 2]
                    p.emit("tensor",
                           lambda e, tsl=tsl, w=w, bank=bank: e.matmul(
                               bank[0:32, :], wqr_sb[:, w * 32:(w + 1) * 32],
                               cqv[:, tsl * TS:(tsl + 1) * TS],
                               start=True, stop=True),
                           wait=[VW()], inc=p.i1("t"))
                    t_qr = p.val("t")
                    p.emit("vector",
                           lambda e, tsl=tsl, w=w, bank=bank: e.tensor_scalar(
                               qraw[0:32, tsl * TS:(tsl + 1) * TS], bank[0:32, :],
                               bvec_sb[(w % 4) * 32:(w % 4) * 32 + 32, 20 + w // 4:21 + w // 4],
                               None, ALU.add),
                           wait=([("t", t_qr)] + ([("dS", p.val("dS"))] if tsl == 0 else [])),
                           inc=p.i1("v"))
                for sl in range(4):
                    apply_rope_slice(qraw, sl * TS, 32, COSq, sl * TS, RBq, sl * TS)
                v_qrope = p.val("v")
                for hh in range(2):
                    p.dma("sync", qwv[48 + 64 * hh:64 + 64 * hh, 0:2048],
                          qraw[16 * hh:16 * hh + 16, 0:2048],
                          "dS", wait=[("v", v_qrope), TW(), ("dS", p.val("dS"))])
                dS_qrot = p.val("dS")

                if DUMP and w == 0:
                    p.dma("sync", dbg["dkw"].ap(), kwv[:], "dS", wait=[VW(), ("dS", p.val("dS"))])
                    p.dma("sync", dbg["dqw"].ap(), qwv[:], "dS")
                    p.dma("sync", dbg["dv65"].ap(), v65[:], "dS")
                for qp in range(2):
                    avb = [pb[2 + 2 * j + hh] for hh in range(2) for j in range(2)]
                    for kt in range(32):
                        for j in range(2):
                            qs = 2 * qp + j
                            for hh in range(2):
                                sb = pb[hh]
                                p.emit("tensor",
                                       lambda e, kt=kt, hh=hh, qs=qs, sb=sb: e.matmul(
                                           sb[:, :], kwv[64 * hh:64 * hh + 64, kt * 128:(kt + 1) * 128],
                                           qwv[64 * hh:64 * hh + 64, qs * TS:(qs + 1) * TS],
                                           start=True, stop=True),
                                       wait=(([SW(), ("dS", dS_qrot)] + ([VW()] if kt == 0 else []))
                                             if hh == 0 else ()),
                                       inc=(p.i1("t") if hh == 1 else ()))
                            t_sc = p.val("t")
                            for hh in range(2):
                                sb = pb[hh]
                                p.emit("scalar",
                                       lambda e, hh=hh, j=j, sb=sb: e.activation(
                                           eT[2 * hh + j][:], sb[:, :], AF.Exp, scale=SC8),
                                       wait=([("t", t_sc)] if hh == 0 else ()),
                                       inc=(p.i1("s") if hh == 1 else ()))
                            s_e = p.val("s")
                            for hh in range(2):
                                p.emit("tensor",
                                       lambda e, kt=kt, hh=hh, j=j: e.matmul(
                                           avb[2 * hh + j][0:65, :],
                                           bass.AP(AB, v65.off + kt * 130 + 65 * hh,
                                                   [[ABC, 128], [1, 65]]),
                                           eT[2 * hh + j][:],
                                           start=(kt == 0), stop=(kt == 31)),
                                       wait=([("s", s_e)] if hh == 0 else ()),
                                       inc=(p.i1("t") if (kt == 31 and j == 1 and hh == 1) else ()))
                    t_av = p.val("t")
                    for hh in range(2):
                        auw = auwA if hh == 0 else auwB
                        for j in range(2):
                            qs = 2 * qp + j
                            av = avb[2 * hh + j]
                            p.emit("vector",
                                   lambda e, av=av, j=j: e.reciprocal(
                                       zrowf_sb[0:1, j * TS:(j + 1) * TS], av[64:65, :]),
                                   wait=[("t", t_av), VW(), ("dA", p.val("dA"))], inc=p.i1("v"))
                            p.emit("vector",
                                   lambda e, j=j: e.tensor_copy(
                                       zrow_sb[0:1, j * TS:(j + 1) * TS],
                                       zrowf_sb[0:1, j * TS:(j + 1) * TS]),
                                   wait=[VW()], inc=p.i1("v"))
                            v_z = p.val("v")
                            zb = pb[hh]
                            p.emit("tensor",
                                   lambda e, zb=zb, j=j: e.matmul(
                                       zb[0:64, :], onesr_bf[0:1, 0:64],
                                       zrow_sb[0:1, j * TS:(j + 1) * TS],
                                       start=True, stop=True),
                                   wait=[("v", v_z)], inc=p.i1("t"))
                            t_zb = p.val("t")
                            p.emit("vector",
                                   lambda e, zb=zb: e.tensor_copy(zbs[0:64, 0:TS], zb[0:64, :]),
                                   wait=[("t", t_zb)], inc=p.i1("v"))
                            v_zc = p.val("v")
                            p.emit("vector",
                                   lambda e, auw=auw, av=av, qs=qs: e.tensor_mul(
                                       auw[0:64, qs * TS:(qs + 1) * TS],
                                       av[0:64, :], zbs[0:64, 0:TS]),
                                   wait=[("v", v_zc)], inc=p.i1("v"))
                v_au = p.val("v")
                for hh in range(2):
                    auw = auwA if hh == 0 else auwB
                    h = 2 * w + hh
                    p.dma("scalar", bass.AP(aud, 0, [[1, 1]]) if False else
                          bass.AP(aud, h * 64 * 2048, [[2048, 64], [1, 2048]]),
                          auw[0:64, 0:2048],
                          "dA", wait=[("v", v_au), ("dA", p.val("dA"))])
            dA_au = p.val("dA")

            ar.reset(16)
            auf = ar.take(16384)
            wof = ar.take(8192)
            bcorb = ar.take(16)
            p.dma("scalar", auf[:],
                  bass.AP(aud, 0, [[2048, 128], [128 * 2048, 8], [1, 2048]]),
                  "dA", wait=[("dA", dA_au), TW(), VW()])
            dA_auf = p.val("dA")
            if DUMP:
                p.dma("sync", dbg["dau"].ap(), auf[:], "dS", wait=[("dA", dA_auf)])
                p.dma("sync", dbg["dxpre"].ap(), x_res[:], "dS", wait=[VW()])
            p.dma("gpsimd", wof[:],
                  bass.AP(wo1, 0, [[1024, 128], [128 * 1024, 8], [1, 1024]]),
                  "dG", wait=[TW()])
            dG_wo = p.val("dG")
            for oc in range(NCK):
                for ck in range(NCK):
                    p.emit("tensor",
                           lambda e, oc=oc, ck=ck: e.matmul(
                               pb[6][:, oc:oc + 1],
                               wof[:, ck * 1024 + oc * 128:ck * 1024 + oc * 128 + 128],
                               buvb[:, ck:ck + 1],
                               start=(ck == 0), stop=(ck == NCK - 1)),
                           wait=([("dG", dG_wo), ("dG", dG_w1), VW()] if (oc == 0 and ck == 0) else ()),
                           inc=(p.i1("t") if (oc == NCK - 1 and ck == NCK - 1) else ()))
            t_bc = p.val("t")
            if DUMP:
                p.dma("sync", dbg["dwof"].ap(), wof[:], "dS", wait=[("dG", p.val("dG"))])
            p.emit("vector",
                   lambda e: e.tensor_add(bvec2_sb[:, 0:8], pb[6][:, 0:8], bvec_sb[:, 22:30]),
                   wait=[("t", t_bc)], inc=p.i1("v"))
            v_bc = p.val("v")
            for tsl in range(4):
                for oc in range(NCK):
                    bank = pb[oc % 2]
                    for ck in range(NCK):
                        p.emit("tensor",
                               lambda e, tsl=tsl, oc=oc, ck=ck, bank=bank: e.matmul(
                                   bank[:, :],
                                   wof[:, ck * 1024 + oc * 128:ck * 1024 + oc * 128 + 128],
                                   auf[:, ck * 2048 + tsl * TS:ck * 2048 + tsl * TS + TS],
                                   start=(ck == 0), stop=(ck == NCK - 1)),
                               wait=(([("dA", dA_auf), ("v", v_bc), VW()] +
                                      ([("dS", p.val("dS"))] if DUMP else []))
                                     if ck == 0 else ()),
                               inc=(p.i1("t") if ck == NCK - 1 else ()))
                    t_o = p.val("t")
                    co = oc * T + tsl * TS
                    p.emit("vector",
                           lambda e, co=co, oc=oc, bank=bank: e.scalar_tensor_tensor(
                               x_res[:, co:co + TS], bank[:, :],
                               bvec2_sb[:, oc:oc + 1],
                               x_res[:, co:co + TS], ALU.add, ALU.add),
                           wait=[("t", t_o)], inc=p.i1("v"))

        def emit_l0_attn():
            SC8 = 1.0 / 8.0
            ar = Arena(AB, 36864)
            rk_fm = ar.take(2048)
            masks = [ar.take(512) for _ in range(4)]
            zt = ar.take(512)
            idn = ar.take(128)
            mark0 = ar.off
            qwv = ar.take(4096)
            kwv = ar.take(4096)
            v65 = ar.take(32 * 130)
            wqc = ar.take(1024)
            wkc = ar.take(1024)
            wvc = ar.take(1024)
            ebs = ar.take(2048)
            fsl = [ar.take(512) for _ in range(4)]
            esl = [ar.take(512) for _ in range(2)]
            psl = [ar.take(512) for _ in range(2)]
            ptl = [ar.take(512) for _ in range(2)]
            auh = ar.take(4096)
            zbs0 = ar.take(512)
            ya0 = ar.take(4096)


            p.dma("gpsimd", idn[:], c_ident.ap(), "dG", wait=[TW(), VW(), SW()])
            p.dma("sync", bvec_sb[:, 0:4], bq0.ap(), "dS", wait=[VW(), SW()])
            p.dma("sync", bvec_sb[:, 4:8], bk0.ap(), "dS")
            p.dma("sync", bvec_sb[:, 8:16], bvf0.ap(), "dS")
            p.dma("sync", bvec_sb[:, 16:24], bo0.ap(), "dS")
            dS_b0 = p.val("dS")
            p.dma("gpsimd", ya0[0:128, 0:1024],
                  bass.AP(rel0, 0, [[64, 128], [128 * 64, 16], [1, 64]]), "dG")
            dG_rel = p.val("dG")
            p.emit("vector", lambda e: e.memset(zt[:], 0.0), wait=[TW()], inc=p.i1("v"))
            for mi in range(4):
                p.emit("gpsimd",
                       lambda e, mi=mi: e.memset(masks[mi][:], 1.0),
                       inc=p.i1("g"))
                g_m = p.val("g")
                p.emit("gpsimd",
                       lambda e, mi=mi: e.affine_select(
                           masks[mi][:], masks[mi][:], [[-1, 512]], ALU.is_ge, 0.0,
                           base=mi * 128, channel_multiplier=1),
                       wait=[("g", g_m)], inc=p.i1("g"))
                p.emit("gpsimd",
                       lambda e, mi=mi: e.memset(maski_sb[:, mi * 512:(mi + 1) * 512], 1),
                       inc=p.i1("g"))
                g_m3 = p.val("g")
                p.emit("gpsimd",
                       lambda e, mi=mi: e.affine_select(
                           maski_sb[:, mi * 512:(mi + 1) * 512],
                           maski_sb[:, mi * 512:(mi + 1) * 512],
                           [[-1, 512]], ALU.is_ge, 0,
                           base=mi * 128, channel_multiplier=1),
                       wait=[("g", g_m3)], inc=p.i1("g"))
            g_masks = p.val("g")
            for zq in range(12):
                p.dma("scalar",
                      bass.AP(slab_all, 768 + zq * 128 * 2048, [[2048, 128], [1, 512]]),
                      zt[:], "dA", wait=[VW()])
            dA_pad = p.val("dA")
            for mt in range(16):
                p.emit("tensor",
                       lambda e, mt=mt: e.transpose(
                           pbt[0:64, (mt % 8) * 128:(mt % 8) * 128 + 128],
                           ya0[0:128, mt * 64:(mt + 1) * 64], idn[0:128, 0:128]),
                       wait=([("dG", dG_rel), VW()] if mt == 0 else
                             ([TW(), VW()] if mt == 8 else ())),
                       inc=(p.i1("t") if mt % 8 == 7 else ()))
                if mt % 8 == 7:
                    t_rk = p.val("t")
                    p.emit("vector",
                           lambda e, mt=mt: e.tensor_copy(
                               rk_fm[0:64, (mt - 7) * 128:(mt + 1) * 128],
                               pbt[0:64, 0:1024]),
                           wait=[("t", t_rk)], inc=p.i1("v"))
            v_rk = p.val("v")
            p.dma("sync", rk_fm[64:128, 0:2048], rk_fm[0:64, 0:2048], "r0",
                  wait=[("v", v_rk)])
            w1_dup = p.val("r0")

            v_ln = p.val("v")
            p.dma("gpsimd", cc0_in.ap(), ybuf[:], "u0", wait=[("v", v_ln)])
            u_cc = p.val("u0")
            p.emit("gpsimd",
                   lambda e: e.collective_compute(
                       "AllGather", ALU.bypass,
                       replica_groups=[[0, 1], [2, 3], [4, 5], [6, 7]],
                       ins=[cc0_in.ap().opt()], outs=[cc0_out.ap().opt()]),
                   wait=[("u0", u_cc)], inc=p.i1("c"))
            c_ag = p.val("c")

            _l0stop = int(_os.environ.get("DEV_L0STOP", "0"))
            if DUMP and _l0stop >= 1:
                p.dma("sync", dbg["dm0"].ap(), masks[0][:], "st",
                      wait=[("g", p.val("g")), ("dA", p.val("dA")), VW()])
                p.dma("sync", dbg["deb"].ap(),
                      bass.AP(slab_all, 768, [[2048, 128], [1, 2048]]), "st")
            FS = ("f0", "f1", "r0", "st")
            _cv, _cs, _ct = p.val("v"), p.val("s"), p.val("t")
            bankfree = {b: [("v", _cv), ("s", _cs)] for b in (0, 1, 3, 4, 6)}
            avfree = {5: [("v", _cv)], 2: [("v", _cv)]}
            eslfree = [_cv, _cv]
            fslfree = [_cv] * 4
            f0free = _cv
            ebsfree = [0] * 4
            pslfree = [_ct, _ct]
            pbt_free = _cv
            ptlfree = [_ct, _ct]
            for w in range(0 if _l0stop >= 1 else 4):
                p.dma("gpsimd", wqc[:],
                      bass.AP(wq0, w * 128, [[512, 128], [512 * 128, 8], [1, 128]]),
                      "w0", wait=[TW()])
                p.dma("gpsimd", wkc[:],
                      bass.AP(wk0, w * 128, [[512, 128], [512 * 128, 8], [1, 128]]),
                      "w0")
                p.dma("gpsimd", wvc[:],
                      bass.AP(wv0, w * 128, [[512, 128], [512 * 128, 8], [1, 128]]),
                      "w0")
                w_w = p.val("w0")
                for ts2 in range(8):
                    r, tsl = ts2 // 4, ts2 % 4
                    p.dma("scalar", ya0[:],
                          bass.AP(cc0_out, r * 128 * 16384 + tsl * TS,
                                  [[16384, 128], [T, NCK], [1, TS]]),
                          "dA", wait=[("c", c_ag), TW()])
                    dA_y = p.val("dA")
                    for mm_i, (wc, dst, bcol) in enumerate(
                            ((wqc, qwv, 0), (wkc, kwv, 4))):
                        bank = pb[mm_i]
                        for ck in range(NCK):
                            p.emit("tensor",
                                   lambda e, ck=ck, wc=wc, bank=bank: e.matmul(
                                       bank[:, :], wc[:, ck * 128:(ck + 1) * 128],
                                       ya0[:, ck * TS:(ck + 1) * TS],
                                       start=(ck == 0), stop=(ck == NCK - 1)),
                                   wait=([("dA", dA_y), ("w0", w_w), VW(), SW()] if ck == 0 else ()),
                                   inc=(p.i1("t") if ck == NCK - 1 else ()))
                        t_p = p.val("t")
                        p.emit("vector",
                               lambda e, ts2=ts2, dst=dst, bank=bank, bcol=bcol, w=w: e.tensor_scalar(
                                   dst[:, ts2 * TS:(ts2 + 1) * TS], bank[:, :],
                                   bvec_sb[:, bcol + w:bcol + w + 1], None, ALU.add),
                               wait=[("t", t_p), ("dS", dS_b0)], inc=p.i1("v"))
                    for tt4 in range(4):
                        tt = ts2 * 4 + tt4
                        bank = pb[2 + tt4 % 2]
                        for ck in range(NCK):
                            p.emit("tensor",
                                   lambda e, ck=ck, tt4=tt4, bank=bank: e.matmul(
                                       bank[:, 0:128],
                                       ya0[:, ck * TS + tt4 * 128:ck * TS + tt4 * 128 + 128],
                                       wvc[:, ck * 128:(ck + 1) * 128],
                                       start=(ck == 0), stop=(ck == NCK - 1)),
                                   wait=([VW(), SW()] if ck == 0 else ()),
                                   inc=(p.i1("t") if ck == NCK - 1 else ()))
                        t_v = p.val("t")
                        p.emit("vector",
                               lambda e, tt=tt, bank=bank: e.tensor_copy(
                                   bass.AP(AB, v65.off + tt * 130, [[36864, 128], [65, 2], [1, 64]]),
                                   bank[:, 0:128]),
                               wait=[("t", t_v)], inc=p.i1("v"))
                p.emit("vector",
                       lambda e: e.memset(
                           bass.AP(AB, v65.off + 64, [[36864, 128], [130, 32], [65, 2], [1, 1]]),
                           1.0),
                       inc=p.i1("v"))
                v_wave = p.val("v")

                bankfree[0] = [("v", v_wave)]
                bankfree[1] = [("v", v_wave)]
                bankfree[3] = [("v", v_wave)]
                avfree[2] = [("v", v_wave)]
                for hh in range(2):
                    h = 2 * w + hh
                    steps = []
                    for qt in range(32):
                        span = qt * 128 + 128
                        Wn = min((span + 511) // 512, 4) * 512
                        for bs in range(Wn // 512):
                            steps.append((qt, bs, Wn))
                    NBS = len(steps)
                    t_bmm = {}
                    dA_q = {}
                    e0w = {}
                    for n in range(NBS + 1):
                        if n < NBS:
                            qt, bs, Wn = steps[n]
                            bB = 4 if n % 2 == 0 else 0
                            p.emit("tensor",
                                   lambda e, qt=qt, bs=bs, Wn=Wn, hh=hh, bB=bB: e.matmul(
                                       pb[bB][:, :],
                                       qwv[64 * hh:64 * hh + 64, qt * 128:(qt + 1) * 128],
                                       rk_fm[64 * hh:64 * hh + 64,
                                             (2048 - Wn) + bs * 512:(2048 - Wn) + bs * 512 + 512],
                                       start=True, stop=True),
                                   wait=(list(bankfree[bB]) +
                                         ([("r0", w1_dup)] if n == 0 else [])),
                                   inc=p.i1("t"))
                            t_bmm[n] = p.val("t")
                        m = n - 1
                        if m >= 0:
                            qt, bs, Wn = steps[m]
                            bB = 4 if m % 2 == 0 else 0
                            slot = bs
                            p.emit("scalar",
                                   lambda e, slot=slot, bB=bB: e.activation(
                                       ebs[:, slot * TS:slot * TS + TS], pb[bB][:, :],
                                       AF.Exp, scale=SC8),
                                   wait=[("t", t_bmm[m]), (f"e{slot}", ebsfree[slot])] +
                                        ([("v", f0free)] if slot == 0 else []),
                                   inc=p.i1("s"))
                            s_exp = p.val("s")
                            bankfree[bB] = [("s", s_exp)]
                            if bs == 0 and Wn == 2048:
                                p.emit("vector",
                                       lambda e, qt=qt: e.tensor_copy(
                                           f0_sb[:, qt:qt + 1], ebs[:, 0:1]),
                                       wait=[("s", s_exp)], inc=p.i1("v"))
                                f0free = p.val("v")
                            p.dma("sync",
                                  bass.AP(slab_all,
                                          768 + qt * 262144 + (2048 - Wn) + bs * 512,
                                          [[2048, 128], [1, 512]]),
                                  ebs[:, slot * TS:slot * TS + TS],
                                  f"e{slot}",
                                  wait=([("s", s_exp)] +
                                        ([(fs, p.val(fs)) for fs in FS] if m == 0 else [])))
                            ebsfree[slot] = p.val(f"e{slot}")
                            if bs == 0:
                                e0w[qt] = p.val("e0")
                            if bs == Wn // 512 - 1:
                                dA_q[qt] = [(f"e{sl}", p.val(f"e{sl}"))
                                            for sl in range(Wn // 512)]
                    for qt in range(32):
                        I0 = qt * 128
                        span = I0 + 128
                        nkt = (span + 511) // 512
                        Wn = min(nkt, 4) * 512
                        qpar = qt % 2
                        bSs = (6, 3) if qpar == 0 else (1, 3)
                        bAV = 5 if qpar == 0 else 2
                        dA_slab = dA_q[qt]
                        kinfo = []
                        for kt in range(nkt):
                            ktw = min(512, span - kt * 512)
                            c2 = I0 - kt * 512 - 2048
                            kinfo.append(dict(
                                ktw=ktw, c2=c2,
                                fullsat=(c2 >= 511), nosat=(c2 <= -128),
                                diag=(kt == nkt - 1), ntp=(ktw + 127) // 128))
                        st_exp = {}
                        st_p = {}
                        st_tp = {}
                        st_ev = {}
                        st_av = {}
                        st_f = {}
                        LAG1, LAG2 = 1, 2
                        for i in range(nkt + LAG2):
                            kt = i
                            if kt < nkt:
                                ki = kinfo[kt]
                                if not ki["fullsat"]:
                                    fs = FS[kt % 4]
                                    xw = (list(dA_slab) +
                                          [("v", fslfree[kt % 4])] +
                                          ([("e0", e0w[qt + 1])]
                                           if (ki["diag"] and qt + 1 <= 31
                                               and qt + 1 >= 12) else []))
                                    p.dma("sync", fsl[kt % 4][:],
                                          bass.AP(slab_all,
                                                  768 + qt * 262144 + 2047 - I0 + kt * 512,
                                                  [[2047, 128], [1, 512]]),
                                          fs, wait=xw)
                                    st_f[kt] = (fs, p.val(fs))
                                bk = bSs[kt % 2]
                                p.emit("tensor",
                                       lambda e, hh=hh, qt=qt, kt=kt, bk=bk: e.matmul(
                                           pb[bk][:, :], qwv[64 * hh:64 * hh + 64, qt * 128:(qt + 1) * 128],
                                           kwv[64 * hh:64 * hh + 64, kt * 512:kt * 512 + 512],
                                           start=True, stop=True),
                                       wait=list(bankfree[bk]), inc=p.i1("t"))
                                t_s = p.val("t")
                                p.emit("scalar",
                                       lambda e, sl2=kt % 2, bk=bk: e.activation(
                                           esl[sl2][:], pb[bk][:, :], AF.Exp, scale=SC8),
                                       wait=[("t", t_s), ("v", eslfree[kt % 2])],
                                       inc=p.i1("s"))
                                st_exp[kt] = p.val("s")
                                bankfree[bk] = [("s", st_exp[kt])]
                            k2 = i - LAG2
                            if 0 <= k2 < nkt:
                                ki = kinfo[k2]
                                po = (k2 % 2) * 512
                                p.emit("vector",
                                       lambda e, sl2=k2 % 2, po=po, ntp=ki["ntp"]: e.tensor_copy(
                                           ptl[sl2][0:128, 0:ntp * 128],
                                           pbt[:, po:po + ntp * 128]),
                                       wait=[("t", max(st_tp[k2], ptlfree[k2 % 2]))],
                                       inc=p.i1("v"))
                                st_ev[k2] = p.val("v")
                                pbt_free = st_ev[k2]
                                for tp in range(ki["ntp"]):
                                    ktt = k2 * 4 + tp
                                    p.emit("tensor",
                                           lambda e, ktt=ktt, hh=hh, sl2=k2 % 2, tp=tp,
                                                  span=span, bAV=bAV: e.matmul(
                                               pb[bAV][0:65, 0:128],
                                               bass.AP(AB, v65.off + ktt * 130 + 65 * hh,
                                                       [[36864, 128], [1, 65]]),
                                               ptl[sl2][0:128, tp * 128:(tp + 1) * 128],
                                               start=(ktt == 0), stop=(ktt == (span // 128) - 1)),
                                           wait=([("v", st_ev[k2])] +
                                                 (list(avfree[bAV]) if k2 == 0 else [])
                                                 if tp == 0 else ()),
                                           inc=(p.i1("t") if tp == ki["ntp"] - 1 else ()))
                                st_av[k2] = p.val("t")
                                ptlfree[k2 % 2] = st_av[k2]
                            k1 = i - LAG1
                            if 0 <= k1 < nkt:
                                ki = kinfo[k1]
                                ps = psl[k1 % 2]
                                pw = [("s", st_exp[k1]), ("t", pslfree[k1 % 2])]
                                if k1 in st_f:
                                    pw.append(st_f[k1])
                                if ki["fullsat"]:
                                    p.emit("vector",
                                           lambda e, ps=ps, sl2=k1 % 2, qt=qt: e.tensor_scalar(
                                               ps[:], esl[sl2][:], f0_sb[:, qt:qt + 1],
                                               None, ALU.mult),
                                           wait=pw, inc=p.i1("v"))
                                else:
                                    p.emit("vector",
                                           lambda e, ps=ps, sl2=k1 % 2, sl4=k1 % 4: e.tensor_mul(
                                               ps[:], esl[sl2][:], fsl[sl4][:]),
                                           wait=pw, inc=p.i1("v"))
                                    if not ki["nosat"]:
                                        mi = ki["c2"] // 128
                                        assert 0 <= mi <= 3
                                        p.emit("vector",
                                               lambda e, sl2=k1 % 2, qt=qt: e.tensor_scalar(
                                                   esl[sl2][:], esl[sl2][:],
                                                   f0_sb[:, qt:qt + 1], None, ALU.mult),
                                               wait=[VW()], inc=p.i1("v"))
                                        p.emit("vector",
                                               lambda e, ps=ps, sl2=k1 % 2, mi=mi: e.copy_predicated(
                                                   ps[:], maski_sb[:, mi * 512:(mi + 1) * 512],
                                                   esl[sl2][:]),
                                               wait=[VW()], inc=p.i1("v"))
                                    if ki["diag"]:
                                        mi2 = (I0 - (nkt - 1) * 512) // 128
                                        assert 0 <= mi2 <= 3
                                        p.emit("vector",
                                               lambda e, ps=ps, mi2=mi2: e.tensor_mul(
                                                   ps[:], ps[:], masks[mi2][:]),
                                               wait=[VW()], inc=p.i1("v"))
                                st_p[k1] = p.val("v")
                                eslfree[k1 % 2] = st_p[k1]
                                if k1 in st_f:
                                    fslfree[k1 % 4] = st_p[k1]
                                po = (k1 % 2) * 512
                                for tp in range(ki["ntp"]):
                                    p.emit("tensor",
                                           lambda e, ps=ps, tp=tp, po=po: e.transpose(
                                               pbt[:, po + tp * 128:po + (tp + 1) * 128],
                                               ps[0:128, tp * 128:tp * 128 + 128],
                                               idn[0:128, 0:128]),
                                           wait=([("v", max(st_p[k1], pbt_free))]
                                                 if tp == 0 else ()),
                                           inc=(p.i1("t") if tp == ki["ntp"] - 1 else ()))
                                st_tp[k1] = p.val("t")
                                pslfree[k1 % 2] = st_tp[k1]
                        t_av = st_av[nkt - 1]
                        zro = qpar * 128
                        p.emit("vector",
                               lambda e, zro=zro, bAV=bAV: e.reciprocal(
                                   zrowf_sb[0:1, zro:zro + 128], pb[bAV][64:65, 0:128]),
                               wait=[("t", t_av)], inc=p.i1("v"))
                        p.emit("vector",
                               lambda e, zro=zro: e.tensor_copy(
                                   zrow_sb[0:1, zro:zro + 128], zrowf_sb[0:1, zro:zro + 128]),
                               wait=[VW()], inc=p.i1("v"))
                        v_z = p.val("v")
                        bZ = bSs[0]
                        p.emit("tensor",
                               lambda e, zro=zro, bZ=bZ: e.matmul(
                                   pb[bZ][0:64, 0:128], onesr_bf[0:1, 0:64],
                                   zrow_sb[0:1, zro:zro + 128], start=True, stop=True),
                               wait=[("v", v_z)] + list(bankfree[bZ]), inc=p.i1("t"))
                        t_zb = p.val("t")
                        p.emit("vector",
                               lambda e, zro=zro, bZ=bZ: e.tensor_copy(
                                   zbs0[0:64, zro:zro + 128], pb[bZ][0:64, 0:128]),
                               wait=[("t", t_zb)], inc=p.i1("v"))
                        v_zc = p.val("v")
                        bankfree[bZ] = [("v", v_zc)]
                        p.emit("vector",
                               lambda e, qt=qt, zro=zro, bAV=bAV: e.tensor_mul(
                                   auh[0:64, qt * 128:(qt + 1) * 128],
                                   pb[bAV][0:64, 0:128], zbs0[0:64, zro:zro + 128]),
                               wait=[("v", v_zc)] +
                                    ([("a0", p.val("a0"))] if qt <= 1 else []),
                               inc=p.i1("v"))
                        avfree[bAV] = [("v", p.val("v"))]
                    v_au = p.val("v")
                    p.dma("scalar",
                          bass.AP(cc2_in, (h % 8) * 64 * 4096, [[4096, 64], [1, 4096]]),
                          auh[0:64, 0:4096], "a0",
                          wait=[("v", v_au), ("a0", p.val("a0"))])
            u_au = p.val("a0")

            p.emit("gpsimd",
                   lambda e: e.collective_compute(
                       "AllGather", ALU.bypass,
                       replica_groups=[[0, 1], [2, 3], [4, 5], [6, 7]],
                       ins=[cc2_in.ap().opt()], outs=[cc2_out.ap().opt()]),
                   wait=[("a0", u_au)], inc=p.i1("c"))
            c_au = p.val("c")

            ar.reset(mark0)
            auo = ar.take(4096)
            auoB = ar.take(4096)
            wvv = ar.take(8192)
            p.dma("gpsimd", wvv[0:128, 0:8192],
                  bass.AP(wo0, 0, [[1024, 128], [128 * 1024, 8], [1, 1024]]), "dG",
                  wait=[TW(), VW(), SW()])
            dG_w0 = p.val("dG")
            p.emit("scalar", lambda e: e.copy(bvb_sb[:, 0:8], bvec_sb[:, 8:16]),
                   wait=[("dS", dS_b0), SW()], inc=p.i1("s"))
            s_bv = p.val("s")
            for oc in range(NCK):
                for ck in range(NCK):
                    p.emit("tensor",
                           lambda e, oc=oc, ck=ck: e.matmul(
                               pb[6][:, oc:oc + 1],
                               wvv[0:128, ck * 1024 + oc * 128:ck * 1024 + oc * 128 + 128],
                               bvb_sb[:, ck:ck + 1],
                               start=(ck == 0), stop=(ck == NCK - 1)),
                           wait=([("dG", dG_w0), ("s", s_bv), VW()] if (oc == 0 and ck == 0) else ()),
                           inc=(p.i1("t") if (oc == NCK - 1 and ck == NCK - 1) else ()))
            t_bc0 = p.val("t")
            p.emit("vector",
                   lambda e: e.tensor_add(bvec2_sb[:, 0:8], pb[6][:, 0:8], bvec_sb[:, 16:24]),
                   wait=[("t", t_bc0)], inc=p.i1("v"))
            v_bc0 = p.val("v")
            p.dma("sync", bvec_sb[:, 30:32], c_half.ap(), "dS",
                  wait=[("dS", p.val("dS"))])
            dS_hm = p.val("dS")
            for tsl in range(4):
                p.dma("scalar", auoB[:],
                      bass.AP(cc2_out, tsl * TS, [[4096, 128], [128 * 4096, 8], [1, TS]]),
                      "dA", wait=[("c", c_au), TW(), VW(), ("dA", p.val("dA"))])
                p.dma("scalar", auo[:],
                      bass.AP(cc2_out, 2048 + tsl * TS, [[4096, 128], [128 * 4096, 8], [1, TS]]),
                      "dA", wait=[("dA", p.val("dA"))])
                dA_both = p.val("dA")
                p.emit("vector",
                       lambda e: e.tensor_scalar(auo[:], auo[:], bvec_sb[:, 31:32],
                                                 None, ALU.mult),
                       wait=[("dA", dA_both), ("dS", dS_hm), VW()], inc=p.i1("v"))
                p.emit("vector",
                       lambda e: e.scalar_tensor_tensor(auo[:], auoB[:],
                                                        bvec_sb[:, 30:31], auo[:],
                                                        ALU.mult, ALU.add),
                       wait=[VW()], inc=p.i1("v"))
                g_au = ("v", p.val("v"))
                for oc in range(NCK):
                    bank = pb[oc % 2]
                    for ck in range(NCK):
                        p.emit("tensor",
                               lambda e, tsl=tsl, oc=oc, ck=ck, bank=bank: e.matmul(
                                   bank[:, :],
                                   wvv[0:128, ck * 1024 + oc * 128:ck * 1024 + oc * 128 + 128],
                                   auo[:, ck * TS:(ck + 1) * TS],
                                   start=(ck == 0), stop=(ck == NCK - 1)),
                               wait=([g_au, ("v", v_bc0), VW()] if ck == 0 else ()),
                               inc=(p.i1("t") if ck == NCK - 1 else ()))
                    t_o = p.val("t")
                    co = oc * T + tsl * TS
                    p.emit("vector",
                           lambda e, co=co, oc=oc, bank=bank: e.scalar_tensor_tensor(
                               x_res[:, co:co + TS], bank[:, :],
                               bvec2_sb[:, oc:oc + 1],
                               x_res[:, co:co + TS], ALU.add, ALU.add),
                           wait=[("t", t_o)], inc=p.i1("v"))
            p.dma("scalar", au0_sb[0:128, 0:8],
                  bass.AP(cc2_out, 0, [[4096, 128], [128 * 4096, 8], [1, 1]]),
                  "dA", wait=[("c", c_au), ("dA", p.val("dA"))],
                  allow_slow_non_contiguous=True)
            dA_au0 = p.val("dA")
            for oc in range(NCK):
                for ck in range(NCK):
                    p.emit("tensor",
                           lambda e, oc=oc, ck=ck: e.matmul(
                               pb[6][:, 8 + oc:9 + oc],
                               wvv[0:128, ck * 1024 + oc * 128:ck * 1024 + oc * 128 + 128],
                               au0_sb[:, ck:ck + 1],
                               start=(ck == 0), stop=(ck == NCK - 1)),
                           wait=([("dA", dA_au0), VW()] if (oc == 0 and ck == 0) else ()),
                           inc=(p.i1("t") if (oc == NCK - 1 and ck == NCK - 1) else ()))
            t_x0 = p.val("t")
            p.emit("vector",
                   lambda e: e.tensor_add(x0_sb[:], x0_sb[:], pb[6][:, 8:16]),
                   wait=[("t", t_x0), VW()], inc=p.i1("v"))
            p.emit("vector",
                   lambda e: e.tensor_add(x0_sb[:], x0_sb[:], bvec2_sb[:, 0:8]),
                   wait=[VW()], inc=p.i1("v"))

        def phase_l0ffn(track_x0=False):
            emit_gates(ready=[("dS", dS_all), ("v", p.val("v")), ("s", p.val("s")),
                              ("t", p.val("t"))])
            emit_layernorm(ready=[("dS", dS_all), ("v", p.val("v"))])
            emit_l0_ffn()
            if track_x0:
                emit_x0_track()

        def phase_l0attn():
            emit_layernorm(ready=[("v", p.val("v")), ("s", p.val("s")),
                                  ("t", p.val("t")), ("dS", dS_all)])
            emit_l0_attn()

        def phase_l1ffn():
            emit_layernorm(ready=[("v", p.val("v")), ("s", p.val("s")),
                                  ("t", p.val("t")), ("dS", dS_all)])
            emit_l1_ffn()

        def phase_l1attn():
            emit_layernorm(ready=[("v", p.val("v")), ("s", p.val("s")),
                                  ("t", p.val("t")), ("dS", dS_all)])
            emit_l1_attn()

        phase_fns = {"l0ffn": phase_l0ffn, "l1ffn": phase_l1ffn,
                     "l1attn": phase_l1attn, "l0attn": phase_l0attn}
        phases_used = list(PHASES[:n_sub])
        for pi, ph in enumerate(phases_used):
            if ph == "l0ffn" and "l0attn" in phases_used[pi + 1:]:
                phase_fns[ph](track_x0=True)
            else:
                phase_fns[ph]()

        v_done = p.val("v")
        s_done = p.val("s")
        t_done = p.val("t")
        for ck in range(NCK):
            p.dma("sync", bass.AP(y_out, ck * 128 * T, [[T, 128], [1, T]]),
                  x_res[:, ck * T:(ck + 1) * T], "st",
                  wait=[("v", v_done), ("s", s_done), ("t", t_done)] if ck == 0 else ())
        st_all = p.val("st")
        p.emit("sync", lambda e: e.nop(), wait=[("st", st_all)])

        with nc.Block() as block:
            @block.sync
            def _(e):
                for f in p.ops["sync"]:
                    f(e)

            @block.vector
            def _(e):
                for f in p.ops["vector"]:
                    f(e)

            @block.scalar
            def _(e):
                for f in p.ops["scalar"]:
                    f(e)

            @block.tensor
            def _(e):
                for f in p.ops["tensor"]:
                    f(e)

            @block.gpsimd
            def _(e):
                for f in p.ops["gpsimd"]:
                    f(e)

    return nc


def _prep_inputs(ins):
    x = np.asarray(ins["L0_attn_x"], dtype=np.float32)
    maps = []
    oh8 = np.zeros((8, 128), dtype=np.float32)
    for e in range(8):
        oh8[e, e * 16:(e + 1) * 16] = 1.0
    wg = np.asarray(ins["L1_ffn_w_gate"])
    wu = np.asarray(ins["L1_ffn_w_up"])
    wd = np.asarray(ins["L1_ffn_w_down"])
    wg_r = np.ascontiguousarray(wg.reshape(24, 128, 8, 128).transpose(0, 2, 3, 1))
    wu_r = np.ascontiguousarray(wu.reshape(24, 128, 8, 128).transpose(0, 2, 3, 1))
    wd_r = np.ascontiguousarray(wd.reshape(8, 128, 24, 128).transpose(0, 2, 3, 1))
    rw_t = np.ascontiguousarray(np.asarray(ins["L0_ffn_router_w"]).T)
    ldn = np.ascontiguousarray(
        np.asarray(ins["L0_ffn_lora_down"]).transpose(2, 0, 1).reshape(D, 128))
    lup = np.ascontiguousarray(
        np.asarray(ins["L0_ffn_lora_up"]).transpose(0, 2, 1).reshape(128, D))
    g = {k: np.asarray(v) for k, v in ins.items()}
    H, DH, DR, SD = 16, 64, 16, 48
    bkf_flat = np.zeros(1024, np.float32)
    bqf_flat = np.zeros(1024, np.float32)
    for h in range(H):
        bkf_flat[h * 64:h * 64 + 48] = g["L1_attn_b_uk"][h * 48:(h + 1) * 48]
        bqf_flat[h * 64:h * 64 + 48] = g["L1_attn_b_uq"][h * 48:(h + 1) * 48]
    invf = (1.0 / 10000.0 ** (np.arange(0, 4, dtype=np.float64) / 4.0))
    invf_row = np.zeros(128, np.float32)
    sign_row = np.zeros(128, np.float32)
    for p_ in range(128):
        d = p_ % 16
        if d < 8:
            invf_row[p_] = np.float32(invf[d % 4] / 40.0)
            sign_row[p_] = -1.0 if d < 4 else 1.0
    perm = np.zeros((128, 128), np.float32)
    for r in range(128):
        d = r % 16
        if d < 4:
            perm[r, r + 4] = 1.0
        elif d < 8:
            perm[r, r - 4] = 1.0
    perm_t = np.ascontiguousarray(perm.T).astype(ml_dtypes.bfloat16)
    l0c = {
        "wo0": g["L0_attn_Wo"],
        "rel0": g["L0_attn_rel_key"],
        "bvf0": np.ascontiguousarray(g["L0_attn_bv"].reshape(8, 128).T),
        "bo0": np.ascontiguousarray(g["L0_attn_bo"].reshape(8, 128).T),
        "c_ident": np.eye(128, dtype=np.float32).astype(ml_dtypes.bfloat16),
    }
    l1c = {
        "wdkv": g["L1_attn_W_dkv"], "wdq": g["L1_attn_W_dq"],
        "wuk": g["L1_attn_W_uk"], "wuv": g["L1_attn_W_uv"],
        "wuq": g["L1_attn_W_uq"], "wqr": g["L1_attn_W_qr"],
        "wkr": g["L1_attn_W_kr"], "wo1": g["L1_attn_W_o"],
        "bdkv": np.ascontiguousarray(g["L1_attn_b_dkv"].reshape(128, 1)),
        "bdq": np.ascontiguousarray(g["L1_attn_b_dq"].reshape(128, 1)),
        "bkf": np.ascontiguousarray(bkf_flat.reshape(8, 128).T),
        "bqf": np.ascontiguousarray(bqf_flat.reshape(8, 128).T),
        "bkr_c": np.ascontiguousarray(g["L1_attn_b_kr"].reshape(2, 128).T),
        "bqr_c": np.ascontiguousarray(g["L1_attn_b_qr"].reshape(2, 128).T),
        "buv_c": np.ascontiguousarray(g["L1_attn_b_uv"].reshape(8, 128).T),
        "bo1_c": np.ascontiguousarray(g["L1_attn_b_o"].reshape(8, 128).T),
        "c_invf": invf_row.reshape(128, 1),
        "c_sign": sign_row.reshape(128, 1),
        "c_perm": perm_t,
    }
    for c in range(NC):
        b, s = c // 2, c % 2
        im = {
            "x_in": np.ascontiguousarray(x[b, s * T:(s + 1) * T, :].T),
            "x0row": np.ascontiguousarray(x[b, 0, :].reshape(8, 128).T),
            "rw_t": rw_t,
            "ldn": ldn,
            "lup": lup,
            "c_oh8": oh8,
            "wg_r": wg_r,
            "wu_r": wu_r,
            "wd_r": wd_r,
            "c_qoff": np.array([[s * 2048]], dtype=np.float32),
            "c_half": np.ascontiguousarray(
                np.stack([np.full(128, 1.0 - s, np.float32),
                          np.full(128, float(s), np.float32)], 1)),
            "wq0": np.ascontiguousarray(g["L0_attn_Wq"][:, s * 512:(s + 1) * 512]),
            "wk0": np.ascontiguousarray(g["L0_attn_Wk"][:, s * 512:(s + 1) * 512]),
            "wv0": np.ascontiguousarray(g["L0_attn_Wv"][:, s * 512:(s + 1) * 512]),
            "bq0": np.ascontiguousarray(g["L0_attn_bq"][s * 512:(s + 1) * 512].reshape(4, 128).T),
            "bk0": np.ascontiguousarray(g["L0_attn_bk"][s * 512:(s + 1) * 512].reshape(4, 128).T),
        }
        im.update(l1c)
        im.update(l0c)
        maps.append(im)
    return maps


def kernel(**ins):
    maps = _prep_inputs(ins)
    nc = build_program(N_SUB)
    res = run_bass_kernel_spmd(nc, maps, core_ids=list(range(NC)))
    out = np.zeros((BATCH, S, D), dtype=np.float32)
    g = {k: np.asarray(v) for k, v in ins.items()}
    H, DH, DR, SD = 16, 64, 16, 48
    bkf_flat = np.zeros(1024, np.float32)
    bqf_flat = np.zeros(1024, np.float32)
    for h in range(H):
        bkf_flat[h * 64:h * 64 + 48] = g["L1_attn_b_uk"][h * 48:(h + 1) * 48]
        bqf_flat[h * 64:h * 64 + 48] = g["L1_attn_b_uq"][h * 48:(h + 1) * 48]
    invf = (1.0 / 10000.0 ** (np.arange(0, 4, dtype=np.float64) / 4.0))
    invf_row = np.zeros(128, np.float32)
    sign_row = np.zeros(128, np.float32)
    for p_ in range(128):
        d = p_ % 16
        if d < 8:
            invf_row[p_] = np.float32(invf[d % 4] / 40.0)
            sign_row[p_] = -1.0 if d < 4 else 1.0
    perm = np.zeros((128, 128), np.float32)
    for r in range(128):
        d = r % 16
        if d < 4:
            perm[r, r + 4] = 1.0
        elif d < 8:
            perm[r, r - 4] = 1.0
    perm_t = np.ascontiguousarray(perm.T).astype(ml_dtypes.bfloat16)
    l0c = {
        "wo0": g["L0_attn_Wo"],
        "rel0": g["L0_attn_rel_key"],
        "bvf0": np.ascontiguousarray(g["L0_attn_bv"].reshape(8, 128).T),
        "bo0": np.ascontiguousarray(g["L0_attn_bo"].reshape(8, 128).T),
        "c_ident": np.eye(128, dtype=np.float32).astype(ml_dtypes.bfloat16),
    }
    l1c = {
        "wdkv": g["L1_attn_W_dkv"], "wdq": g["L1_attn_W_dq"],
        "wuk": g["L1_attn_W_uk"], "wuv": g["L1_attn_W_uv"],
        "wuq": g["L1_attn_W_uq"], "wqr": g["L1_attn_W_qr"],
        "wkr": g["L1_attn_W_kr"], "wo1": g["L1_attn_W_o"],
        "bdkv": np.ascontiguousarray(g["L1_attn_b_dkv"].reshape(128, 1)),
        "bdq": np.ascontiguousarray(g["L1_attn_b_dq"].reshape(128, 1)),
        "bkf": np.ascontiguousarray(bkf_flat.reshape(8, 128).T),
        "bqf": np.ascontiguousarray(bqf_flat.reshape(8, 128).T),
        "bkr_c": np.ascontiguousarray(g["L1_attn_b_kr"].reshape(2, 128).T),
        "bqr_c": np.ascontiguousarray(g["L1_attn_b_qr"].reshape(2, 128).T),
        "buv_c": np.ascontiguousarray(g["L1_attn_b_uv"].reshape(8, 128).T),
        "bo1_c": np.ascontiguousarray(g["L1_attn_b_o"].reshape(8, 128).T),
        "c_invf": invf_row.reshape(128, 1),
        "c_sign": sign_row.reshape(128, 1),
        "c_perm": perm_t,
    }
    for c in range(NC):
        b, s = c // 2, c % 2
        out[b, s * T:(s + 1) * T, :] = res.results[c]["y_out"].T
    return out


# version 09
# speedup vs baseline: 1.0491x; 1.0021x over previous
import numpy as np
import ml_dtypes
from contextlib import ExitStack

import concourse.bass as bass
import concourse.mybir as mybir
from concourse.bass_utils import run_bass_kernel_spmd
from concourse.bass_core_rust import IndirectOffsetOnAxis
from concourse.aot_env import allow_ici_all_to_all

F32 = mybir.dt.float32
BF16 = mybir.dt.bfloat16
I32 = mybir.dt.int32
AF = mybir.ActivationFunctionType
ALU = mybir.AluOpType

BATCH, S, D = 4, 4096, 1024
T = 2048
NCK = 8
NC = 8
NSLICE = 4
TS = 512
LN_EPS = 1e-5

PHASES = ["l0ffn", "l0attn", "l0ffn", "l1ffn", "l1attn", "l1ffn"]
N_SUB = len(PHASES)


class Prog:
    ENGINES = ("sync", "vector", "scalar", "tensor", "gpsimd")

    def __init__(self):
        self.ops = {e: [] for e in self.ENGINES}
        self.semv = {}
        self.sems = {}

    def bump(self, name, k=1):
        self.semv[name] = self.semv.get(name, 0) + k
        return self.semv[name]

    def val(self, name):
        return self.semv.get(name, 0)

    def i1(self, name):
        self.bump(name, 1)
        return [(name, 1)]

    def emit(self, eng, f, wait=(), inc=()):
        wait = tuple(wait)
        inc = tuple(inc)

        def run(e, f=f, wait=wait, inc=inc):
            for s, v in wait:
                e.wait_ge(self.sems[s], v)
            r = f(e)
            for s, a in inc:
                r = r.then_inc(self.sems[s], a)
        self.ops[eng].append(run)

    def dma(self, eng, out, in_, sem, wait=(), **kw):
        v = self.bump(sem, 16)
        self.emit(eng, lambda e, o=out, i=in_, kw=kw: e.dma_start(out=o, in_=i, **kw),
                  wait=wait, inc=[(sem, 16)])
        return v


class V2:

    def __init__(self, t, off, cols):
        self.t, self.off, self.cols = t, off, cols

    def __getitem__(self, idx):
        if isinstance(idx, slice):
            assert idx == slice(None)
            return self.t[:, self.off:self.off + self.cols]
        pr, cs = idx
        a = cs.start if cs.start is not None else 0
        b = cs.stop if cs.stop is not None else self.cols
        return self.t[pr, self.off + a:self.off + b]


class Arena:
    def __init__(self, t, cols):
        self.t, self.cols, self.off = t, cols, 0

    def take(self, n):
        n = (n + 15) // 16 * 16
        assert self.off + n <= self.cols, f"arena overflow: {self.off}+{n} > {self.cols}"
        v = V2(self.t, self.off, n)
        self.off += n
        return v

    def reset(self, off=0):
        self.off = off


def build_program(n_sub=N_SUB):
    nc = bass.Bass("TRN2", target_bir_lowering=False, debug=False)
    p = Prog()

    dt_in = {}

    def din(name, shape, dtype=F32):
        dt_in[name] = nc.dram_tensor(name, list(shape), dtype, kind="ExternalInput")
        return dt_in[name]

    x_in = din("x_in", [D, T])
    x0row = din("x0row", [128, 8])
    rw_t = din("rw_t", [D, 8])
    ldn = din("ldn", [D, 128])
    lup = din("lup", [128, D])
    c_oh8 = din("c_oh8", [8, 128])
    wg_r = din("wg_r", [24, 8, 128, 128], I32)
    wdkv = din("wdkv", [D, 128])
    wdq = din("wdq", [D, 128])
    wuk = din("wuk", [128, 768])
    wuv = din("wuv", [128, 1024])
    wuq = din("wuq", [128, 768])
    wqr = din("wqr", [128, 256])
    wkr = din("wkr", [D, 256])
    wo1 = din("wo1", [D, D])
    bdkv = din("bdkv", [128, 1])
    bdq = din("bdq", [128, 1])
    bkf = din("bkf", [128, 8])
    bqf = din("bqf", [128, 8])
    bkr_c = din("bkr_c", [128, 2])
    bqr_c = din("bqr_c", [128, 2])
    buv_c = din("buv_c", [128, 8])
    bo1_c = din("bo1_c", [128, 8])
    c_invf = din("c_invf", [128, 1])
    c_sign = din("c_sign", [128, 1])
    c_perm = din("c_perm", [128, 128], BF16)
    c_qoff = din("c_qoff", [1, 1])
    wq0 = din("wq0", [D, 512])
    wk0 = din("wk0", [D, 512])
    wv0 = din("wv0", [D, 512])
    wo0 = din("wo0", [D, D])
    rel0 = din("rel0", [2048, 64])
    bq0 = din("bq0", [128, 4])
    bk0 = din("bk0", [128, 4])
    bvf0 = din("bvf0", [128, 8])
    bo0 = din("bo0", [128, 8])
    c_ident = din("c_ident", [128, 128], BF16)
    cc0_in = nc.dram_tensor("cc0_in", [128, 16384], BF16)
    cc0_out = nc.dram_tensor("cc0_out", [2, 128, 16384], BF16)
    cc2_in = nc.dram_tensor("cc2_in", [512, 4096], BF16)
    cc2_out = nc.dram_tensor("cc2_out", [2, 512, 4096], BF16)
    c_half = din("c_half", [128, 2])
    SLABN = 768 + 32 * 128 * 2048 + 768
    slab_all = nc.dram_tensor("slab_all", [SLABN], BF16)
    cc1_in = nc.dram_tensor("cc1_in", [128, 16384], BF16)
    cc1_out = nc.dram_tensor("cc1_out", [2, 128, 16384], BF16)
    krd = nc.dram_tensor("krd", [2, 128, 4096], BF16)
    aud = nc.dram_tensor("aud", [8, 128, 2048], BF16)
    wu_r = din("wu_r", [24, 8, 128, 128], I32)
    wd_r = din("wd_r", [8, 24, 128, 128], I32)

    y_out = nc.dram_tensor("y_out", [D, T], F32, kind="ExternalOutput")
    import os as _os
    DUMP = _os.environ.get("DEV_DUMP", "") == "1"
    if DUMP:
        dbg = {n: nc.dram_tensor(n, sh, BF16, kind="ExternalOutput")
               for n, sh in [("dck", [128, 4096]), ("dcq", [128, 2048]),
                             ("dkw", [128, 4096]), ("dqw", [128, 2048]),
                             ("dv65", [128, 4160]), ("dau", [128, 16384]),
                             ("dwof", [128, 8192]), ("dp0", [128, 512]),
                             ("de0", [128, 512]), ("df0", [128, 512]),
                             ("dm0", [128, 512]), ("deb", [128, 2048]), ("debs", [128, 1024]),
                             ("dauh", [64, 4096]), ("dq0", [128, 4096]), ("dk0", [128, 4096])]}
        dbg["dxpre"] = nc.dram_tensor("dxpre", [128, 16384], F32, kind="ExternalOutput")
        dbg["dxent"] = nc.dram_tensor("dxent", [128, 16384], F32, kind="ExternalOutput")

    with ExitStack() as ctx:
        E = ctx.enter_context

        x_res = E(nc.sbuf_tensor("x_res", [128, NCK * T], F32))
        ybuf = E(nc.sbuf_tensor("ybuf", [128, NCK * T], BF16))
        sq_sb = E(nc.sbuf_tensor("sq_sb", [128, NCK * TS], BF16))
        dn_sb = E(nc.sbuf_tensor("dn_sb", [128, T], BF16))
        ones_bf = E(nc.sbuf_tensor("ones_bf", [128, 2], BF16))
        ones_f = E(nc.sbuf_tensor("ones_f", [128, 2], F32))
        onesr_bf = E(nc.sbuf_tensor("onesr_bf", [1, 128], BF16))
        onesr_f = E(nc.sbuf_tensor("onesr_f", [1, 128], F32))
        eps_sb = E(nc.sbuf_tensor("eps_sb", [128, 1], F32))
        strow = E(nc.sbuf_tensor("strow", [1, 3 * TS], F32))
        x0_sb = E(nc.sbuf_tensor("x0_sb", [128, 8], F32))
        x0sq = E(nc.sbuf_tensor("x0sq", [128, 8], F32))
        ln0f = E(nc.sbuf_tensor("ln0f", [128, 8], F32))
        ln0b = E(nc.sbuf_tensor("ln0b", [128, 8], BF16))
        gsc = E(nc.sbuf_tensor("gsc", [1, 16], F32))
        grow = E(nc.sbuf_tensor("grow", [1, 8], F32))
        grow2 = E(nc.sbuf_tensor("grow2", [1, 8], F32))
        g8_sb = E(nc.sbuf_tensor("g8_sb", [8, 1], F32))
        g8b_sb = E(nc.sbuf_tensor("g8b_sb", [8, 1], F32))
        gb_sb = E(nc.sbuf_tensor("gb_sb", [8, 4], F32))
        gexp_sb = E(nc.sbuf_tensor("gexp_sb", [128, 1], F32))
        rw_sb = E(nc.sbuf_tensor("rw_sb", [128, 64], BF16))
        ldn_sb = E(nc.sbuf_tensor("ldn_sb", [128, 1024], BF16))
        lup_sb = E(nc.sbuf_tensor("lup_sb", [128, 1024], BF16))
        oh8_sb = E(nc.sbuf_tensor("oh8_sb", [8, 128], F32))
        invf_sb = E(nc.sbuf_tensor("invf_sb", [128, 1], F32))
        sign_sb = E(nc.sbuf_tensor("sign_sb", [128, 1], F32))
        perm_sb = E(nc.sbuf_tensor("perm_sb", [128, 128], BF16))
        zrow_sb = E(nc.sbuf_tensor("zrow_sb", [1, 2 * TS], BF16))
        zrowf_sb = E(nc.sbuf_tensor("zrowf_sb", [1, 2 * TS], F32))
        bvec_sb = E(nc.sbuf_tensor("bvec_sb", [128, 32], F32))
        bvec2_sb = E(nc.sbuf_tensor("bvec2_sb", [128, 8], F32))
        iot_sb = E(nc.sbuf_tensor("iot_sb", [128, TS], F32))
        qoffc_sb = E(nc.sbuf_tensor("qoffc_sb", [128, 1], F32))
        qoffr_sb = E(nc.sbuf_tensor("qoffr_sb", [1, 1], F32))
        qoffi_sb = E(nc.sbuf_tensor("qoffi_sb", [1, 1], mybir.dt.int32))
        f0_sb = E(nc.sbuf_tensor("f0_sb", [128, 32], F32))
        zpt_sb = E(nc.sbuf_tensor("zpt_sb", [128, 10], F32))
        zqt_sb = E(nc.sbuf_tensor("zqt_sb", [128, 1], F32))
        bvb_sb = E(nc.sbuf_tensor("bvb_sb", [128, 8], BF16))
        au0_sb = E(nc.sbuf_tensor("au0_sb", [128, 8], BF16))
        d0g_sb = E(nc.sbuf_tensor("d0g_sb", [128, 1], BF16))
        maski_sb = E(nc.sbuf_tensor("maski_sb", [128, 4 * 512], mybir.dt.int8))
        AB = E(nc.sbuf_tensor("AB", [128, 36864], BF16))

        pb = [E(nc.psum_tensor(f"pb{i}", [128, 512], F32)) for i in range(7)]
        pbt = E(nc.psum_tensor("pbt", [128, 1024], BF16))

        for sname in ("dS", "dA", "dG", "t", "v", "s", "g", "c", "st",
                      "w0", "w1", "w2", "u0", "u1", "f0", "f1",
                      "e0", "e1", "e2", "e3", "r0", "a0"):
            p.sems[sname] = E(nc.semaphore(sname))

        p.dma("sync", x_res[:], bass.AP(x_in, 0, [[T, 128], [128 * T, NCK], [1, T]]), "dS")
        p.dma("sync", x0_sb[:], x0row.ap(), "dS")
        p.dma("sync", oh8_sb[:], c_oh8.ap(), "dS")
        p.dma("gpsimd", rw_sb[:], bass.AP(rw_t, 0, [[8, 128], [8 * 128, NCK], [1, 8]]), "dG")
        p.dma("gpsimd", ldn_sb[:], bass.AP(ldn, 0, [[128, 128], [128 * 128, NCK], [1, 128]]), "dG")
        p.dma("gpsimd", lup_sb[:], lup.ap(), "dG")
        dG_w = p.val("dG")
        dS_x = 16
        dS_all = p.val("dS")

        p.emit("vector", lambda e: e.memset(ones_bf[:], 1.0))
        p.emit("vector", lambda e: e.memset(ones_f[:], 1.0))
        p.emit("vector", lambda e: e.memset(onesr_bf[:], 1.0))
        p.emit("vector", lambda e: e.memset(onesr_f[:], 1.0))
        p.emit("vector", lambda e: e.memset(eps_sb[:], LN_EPS), inc=p.i1("v"))
        v_const = p.val("v")

        def VW():
            return ("v", p.val("v"))

        def SW():
            return ("s", p.val("s"))

        def TW():
            return ("t", p.val("t"))

        def emit_layernorm(ready):
            for ts in range(NSLICE):
                for ck in range(NCK):
                    co = ck * T + ts * TS
                    p.emit("scalar",
                           lambda e, co=co: e.copy(ybuf[:, co:co + TS], x_res[:, co:co + TS]),
                           wait=(ready if (ts == 0 and ck == 0) else ()),
                           inc=(p.i1("s") if ck == NCK - 1 else ()))
                s_cast = p.val("s")
                for ck in range(NCK):
                    co = ck * T + ts * TS
                    sl = ck * TS
                    p.emit("scalar",
                           lambda e, co=co, sl=sl: e.activation(
                               sq_sb[:, sl:sl + TS], x_res[:, co:co + TS], AF.Square),
                           wait=([TW()] if ck == 0 and ts > 0 else ()),
                           inc=(p.i1("s") if ck == NCK - 1 else ()))
                s_sq = p.val("s")
                for ck in range(NCK):
                    co = ck * T + ts * TS
                    p.emit("tensor",
                           lambda e, co=co, ck=ck: e.matmul(
                               pb[4][0:1, :], ones_bf[:, 0:1], ybuf[:, co:co + TS],
                               start=(ck == 0), stop=(ck == NCK - 1)),
                           wait=([("s", s_cast), VW()] if ck == 0 else ()))
                for ck in range(NCK):
                    sl = ck * TS
                    p.emit("tensor",
                           lambda e, sl=sl, ck=ck: e.matmul(
                               pb[5][0:1, :], ones_bf[:, 0:1], sq_sb[:, sl:sl + TS],
                               start=(ck == 0), stop=(ck == NCK - 1)),
                           wait=([("s", s_sq)] if ck == 0 else ()),
                           inc=(p.i1("t") if ck == NCK - 1 else ()))
                t_stats = p.val("t")

                mrow = strow[0:1, 0:TS]
                rrow = strow[0:1, TS:2 * TS]
                vrow = strow[0:1, 2 * TS:3 * TS]
                p.emit("vector",
                       lambda e: e.tensor_scalar(mrow, pb[4][0:1, :], 1.0 / D, None, ALU.mult),
                       wait=[("t", t_stats)], inc=p.i1("v"))
                p.emit("vector", lambda e: e.tensor_mul(vrow, mrow, mrow),
                       wait=[VW()], inc=p.i1("v"))
                p.emit("vector",
                       lambda e: e.scalar_tensor_tensor(vrow, pb[5][0:1, :], 1.0 / D, vrow,
                                                        ALU.mult, ALU.subtract),
                       wait=[VW()], inc=p.i1("v"))
                v_var = p.val("v")
                p.emit("scalar",
                       lambda e: e.activation(vrow, vrow, AF.Sqrt, bias=eps_sb[0:1, 0:1]),
                       wait=[("v", v_var)], inc=p.i1("s"))
                s_sd = p.val("s")
                p.emit("vector", lambda e: e.reciprocal(rrow, vrow),
                       wait=[("s", s_sd)], inc=p.i1("v"))
                v_r = p.val("v")
                p.emit("tensor",
                       lambda e: e.matmul(pb[6][:, :], onesr_f[0:1, :], mrow,
                                          start=True, stop=True),
                       wait=[("v", v_r)])
                p.emit("tensor",
                       lambda e: e.matmul(pb[4][:, :], onesr_f[0:1, :], rrow,
                                          start=True, stop=True),
                       inc=p.i1("t"))
                t_bc = p.val("t")
                for ck in range(NCK):
                    co = ck * T + ts * TS
                    p.emit("vector",
                           lambda e, co=co: e.scalar_tensor_tensor(
                               ybuf[:, co:co + TS], x_res[:, co:co + TS], 1.0,
                               pb[6][:, :], ALU.mult, ALU.subtract),
                           wait=([("t", t_bc)] if ck == 0 else ()),
                           inc=(p.i1("v") if ck == NCK - 1 else ()))
                v_p1 = p.val("v")
                for ck in range(NCK):
                    co = ck * T + ts * TS
                    p.emit("vector",
                           lambda e, co=co: e.tensor_mul(
                               ybuf[:, co:co + TS], ybuf[:, co:co + TS], pb[4][:, :]),
                           wait=([("v", v_p1)] if ck == 0 else ()),
                           inc=(p.i1("v") if ck == NCK - 1 else ()))
            return p.val("v")

        def emit_gates(ready):
            def ve(f, wait=()):
                p.emit("vector", f, wait=list(wait) + [VW()], inc=p.i1("v"))

            def se(f, wait=()):
                p.emit("scalar", f, wait=list(wait) + [SW()], inc=p.i1("s"))

            se(lambda e: e.activation(x0sq[:], x0_sb[:], AF.Square), wait=ready)
            s_sq = p.val("s")
            p.emit("tensor",
                   lambda e: e.matmul(pb[0][0:1, 0:8], ones_f[:, 0:1], x0_sb[:],
                                      start=True, stop=True),
                   wait=[("s", s_sq), VW()])
            p.emit("tensor",
                   lambda e: e.matmul(pb[0][0:1, 8:16], ones_f[:, 0:1], x0sq[:],
                                      start=True, stop=True),
                   inc=p.i1("t"))
            t_s = p.val("t")
            ve(lambda e: e.reduce_sum(gsc[0:1, 0:1], pb[0][0:1, 0:8],
                                      axis=mybir.AxisListType.X), wait=[("t", t_s)])
            ve(lambda e: e.reduce_sum(gsc[0:1, 1:2], pb[0][0:1, 8:16],
                                      axis=mybir.AxisListType.X))
            ve(lambda e: e.tensor_scalar(gsc[0:1, 0:1], gsc[0:1, 0:1], 1.0 / D, None, ALU.mult))
            ve(lambda e: e.tensor_mul(gsc[0:1, 2:3], gsc[0:1, 0:1], gsc[0:1, 0:1]))
            ve(lambda e: e.scalar_tensor_tensor(gsc[0:1, 2:3], gsc[0:1, 1:2], 1.0 / D,
                                                gsc[0:1, 2:3], ALU.mult, ALU.subtract))
            se(lambda e: e.activation(gsc[0:1, 2:3], gsc[0:1, 2:3], AF.Sqrt,
                                      bias=eps_sb[0:1, 0:1]), wait=[VW()])
            ve(lambda e: e.reciprocal(gsc[0:1, 3:4], gsc[0:1, 2:3]), wait=[SW()])
            v_r0 = p.val("v")
            p.emit("tensor", lambda e: e.matmul(pb[1][:, 0:1], onesr_f[0:1, :],
                                                gsc[0:1, 0:1], start=True, stop=True),
                   wait=[("v", v_r0)])
            p.emit("tensor", lambda e: e.matmul(pb[1][:, 1:2], onesr_f[0:1, :],
                                                gsc[0:1, 3:4], start=True, stop=True),
                   inc=p.i1("t"))
            t_bc = p.val("t")
            ve(lambda e: e.tensor_scalar(ln0f[:], x0_sb[:], pb[1][:, 0:1],
                                         pb[1][:, 1:2], ALU.subtract, ALU.mult),
               wait=[("t", t_bc)])
            se(lambda e: e.copy(ln0b[:], ln0f[:]), wait=[VW()])
            s_ln0 = p.val("s")
            for ck in range(NCK):
                p.emit("tensor",
                       lambda e, ck=ck: e.matmul(pb[2][0:8, 0:1],
                                                 rw_sb[:, ck * 8:(ck + 1) * 8],
                                                 ln0b[:, ck:ck + 1],
                                                 start=(ck == 0), stop=(ck == NCK - 1)),
                       wait=([("s", s_ln0), ("dG", dG_w)] if ck == 0 else ()))
            for ck in range(NCK):
                p.emit("tensor",
                       lambda e, ck=ck: e.matmul(pb[3][0:1, 0:8],
                                                 ln0b[:, ck:ck + 1],
                                                 rw_sb[:, ck * 8:(ck + 1) * 8],
                                                 start=(ck == 0), stop=(ck == NCK - 1)),
                       inc=(p.i1("t") if ck == NCK - 1 else ()))
            t_log = p.val("t")
            ve(lambda e: e.reduce_max(gsc[0:1, 4:5], pb[3][0:1, 0:8],
                                      axis=mybir.AxisListType.X), wait=[("t", t_log)])
            ve(lambda e: e.tensor_scalar(gsc[0:1, 5:6], gsc[0:1, 4:5], -1.0, None, ALU.mult))
            ve(lambda e: e.tensor_scalar(grow[0:1, :], pb[3][0:1, 0:8],
                                         gsc[0:1, 4:5], None, ALU.is_equal))
            ve(lambda e: e.scalar_tensor_tensor(grow[0:1, :], grow[0:1, :], -1e30,
                                                pb[3][0:1, 0:8], ALU.mult, ALU.add))
            ve(lambda e: e.reduce_max(gsc[0:1, 6:7], grow[0:1, :],
                                      axis=mybir.AxisListType.X))
            se(lambda e: e.activation(grow[0:1, :], pb[3][0:1, 0:8], AF.Exp,
                                      bias=gsc[0:1, 5:6]), wait=[VW()])
            ve(lambda e: e.tensor_scalar(grow2[0:1, :], pb[3][0:1, 0:8],
                                         gsc[0:1, 6:7], None, ALU.is_ge), wait=[SW()])
            ve(lambda e: e.tensor_mul(grow[0:1, :], grow[0:1, :], grow2[0:1, :]))
            ve(lambda e: e.reduce_sum(gsc[0:1, 7:8], grow[0:1, :],
                                      axis=mybir.AxisListType.X))
            ve(lambda e: e.tensor_scalar(gsc[0:1, 9:10], gsc[0:1, 7:8], 2.0, None, ALU.mult))
            ve(lambda e: e.reciprocal(gsc[0:1, 8:9], gsc[0:1, 9:10]))
            v_rz = p.val("v")
            p.emit("tensor", lambda e: e.matmul(pb[1][0:8, 2:3], onesr_f[0:1, 0:8],
                                                gsc[0:1, 5:6], start=True, stop=True),
                   wait=[("v", v_rz)])
            p.emit("tensor", lambda e: e.matmul(pb[1][0:8, 3:4], onesr_f[0:1, 0:8],
                                                gsc[0:1, 6:7], start=True, stop=True))
            p.emit("tensor", lambda e: e.matmul(pb[1][0:8, 4:5], onesr_f[0:1, 0:8],
                                                gsc[0:1, 8:9], start=True, stop=True),
                   inc=p.i1("t"))
            t_bc2 = p.val("t")
            ve(lambda e: e.tensor_copy(gb_sb[:, 0:3], pb[1][0:8, 2:5]), wait=[("t", t_bc2)])
            se(lambda e: e.activation(g8_sb[:], pb[2][0:8, 0:1], AF.Exp,
                                      bias=gb_sb[0:8, 0:1]), wait=[VW()])
            ve(lambda e: e.tensor_scalar(g8b_sb[:], pb[2][0:8, 0:1],
                                         gb_sb[0:8, 1:2], None, ALU.is_ge), wait=[SW()])
            ve(lambda e: e.tensor_mul(g8_sb[:], g8_sb[:], g8b_sb[:]))
            ve(lambda e: e.tensor_scalar(g8_sb[:], g8_sb[:], gb_sb[0:8, 2:3], None, ALU.mult))
            v_g8 = p.val("v")
            p.emit("tensor", lambda e: e.matmul(pb[1][:, 5:6], oh8_sb[:], g8_sb[:],
                                                start=True, stop=True),
                   wait=[("v", v_g8)], inc=p.i1("t"))
            t_ge = p.val("t")
            ve(lambda e: e.tensor_copy(gexp_sb[:], pb[1][:, 5:6]), wait=[("t", t_ge)])
            return p.val("v")

        def emit_x0_track():
            for ck in range(NCK):
                p.emit("tensor",
                       lambda e, ck=ck: e.matmul(
                           pb[0][:, 0:1], ldn_sb[:, ck * 128:(ck + 1) * 128],
                           ln0b[:, ck:ck + 1], start=(ck == 0), stop=(ck == NCK - 1)),
                       wait=([VW(), SW()] if ck == 0 else ()),
                       inc=(p.i1("t") if ck == NCK - 1 else ()))
            t_d0 = p.val("t")
            p.emit("vector",
                   lambda e: e.tensor_scalar(d0g_sb[:], pb[0][:, 0:1], gexp_sb[:, 0:1],
                                             None, ALU.mult),
                   wait=[("t", t_d0)], inc=p.i1("v"))
            v_d0 = p.val("v")
            for f in range(NCK):
                p.emit("tensor",
                       lambda e, f=f: e.matmul(
                           pb[1][:, f:f + 1], lup_sb[:, f * 128:(f + 1) * 128],
                           d0g_sb[:], start=True, stop=True),
                       wait=([("v", v_d0)] if f == 0 else ()),
                       inc=(p.i1("t") if f == NCK - 1 else ()))
            t_w0 = p.val("t")
            p.emit("vector",
                   lambda e: e.scalar_tensor_tensor(x0_sb[:], ln0f[:], 0.5, x0_sb[:],
                                                    ALU.mult, ALU.add),
                   wait=[("t", t_w0)], inc=p.i1("v"))
            p.emit("vector",
                   lambda e: e.tensor_add(x0_sb[:], x0_sb[:], pb[1][:, 0:8]),
                   wait=[VW()], inc=p.i1("v"))

        def emit_l0_ffn():
            for ts in range(NSLICE):
                bank = pb[ts % 2]
                for ck in range(NCK):
                    co = ck * T + ts * TS
                    p.emit("tensor",
                           lambda e, co=co, ck=ck, bank=bank: e.matmul(
                               bank[:, :], ldn_sb[:, ck * 128:(ck + 1) * 128],
                               ybuf[:, co:co + TS],
                               start=(ck == 0), stop=(ck == NCK - 1)),
                           wait=([VW()] if ck == 0 else ()),
                           inc=(p.i1("t") if ck == NCK - 1 else ()))
                t_dn = p.val("t")
                p.emit("vector",
                       lambda e, ts=ts, bank=bank: e.tensor_scalar(
                           dn_sb[:, ts * TS:ts * TS + TS], bank[:, :], gexp_sb[:, 0:1],
                           None, ALU.mult),
                       wait=[("t", t_dn)], inc=p.i1("v"))
            v_dn = p.val("v")
            for ck in range(NCK):
                for ts in range(NSLICE):
                    co = ck * T + ts * TS
                    last = (ck == NCK - 1 and ts == NSLICE - 1)
                    p.emit("vector",
                           lambda e, co=co: e.scalar_tensor_tensor(
                               x_res[:, co:co + TS], ybuf[:, co:co + TS], 0.5,
                               x_res[:, co:co + TS], ALU.mult, ALU.add),
                           inc=(p.i1("v") if last else ()))
            v_pa = p.val("v")
            for ts in range(NSLICE):
                for f in range(NCK):
                    bank = pb[2 + (f % 2)]
                    co = f * T + ts * TS
                    p.emit("tensor",
                           lambda e, f=f, ts=ts, bank=bank: e.matmul(
                               bank[:, :], lup_sb[:, f * 128:(f + 1) * 128],
                               dn_sb[:, ts * TS:ts * TS + TS],
                               start=True, stop=True),
                           wait=[VW()],
                           inc=p.i1("t"))
                    t_up = p.val("t")
                    p.emit("vector",
                           lambda e, co=co, bank=bank: e.scalar_tensor_tensor(
                               x_res[:, co:co + TS], bank[:, :], 1.0,
                               x_res[:, co:co + TS], ALU.mult, ALU.add),
                           wait=[("t", t_up), ("v", v_pa)],
                           inc=p.i1("v"))

        def emit_l1_ffn():
            DSC = 0.5 / (127.0 * 127.0)
            ar = Arena(AB, 36864)
            wgc = [ar.take(1024) for _ in range(3)]
            wuc = [ar.take(1024) for _ in range(3)]
            wdc = [ar.take(12 * 128) for _ in range(2)]
            slu = [ar.take(TS) for _ in range(2)]
            hp_sb = ar.take(24 * 1024)
            last_wt = {0: p.val("t"), 1: p.val("t"), 2: p.val("t")}
            for hb in range(2):
                _iw = [("s", p.val("s")), ("v", p.val("v"))]
                gfree = {0: _iw, 1: _iw, 4: _iw, 5: _iw}
                ufree = {2: _iw, 3: _iw, 6: _iw}
                slufree = [p.val("v")] * 2
                for m in range(24):
                    r = m % 3
                    wsem = f"w{r}"
                    p.dma("gpsimd", wgc[r][:],
                          bass.AP(wg_r, m * 8 * 128 * 128,
                                  [[128, 128], [128 * 128, 8], [1, 128]]),
                          wsem, wait=[("t", last_wt[r])])
                    p.dma("gpsimd", wuc[r][:],
                          bass.AP(wu_r, m * 8 * 128 * 128,
                                  [[128, 128], [128 * 128, 8], [1, 128]]),
                          wsem)
                    w_tot = p.val(wsem)
                    for tsl in range(2):
                        ts = 2 * hb + tsl
                        gb = (4 + tsl) if m % 2 == 0 else (0 + tsl)
                        ub = (2 + tsl) if m % 2 == 0 else (6 if tsl == 0 else 2)
                        gbank = pb[gb]
                        ubank = pb[ub]
                        for ck in range(NCK):
                            co = ck * T + ts * TS
                            p.emit("tensor",
                                   lambda e, co=co, ck=ck, gbank=gbank, r=r: e.matmul(
                                       gbank[:, :], wgc[r][:, ck * 128:(ck + 1) * 128],
                                       ybuf[:, co:co + TS],
                                       start=(ck == 0), stop=(ck == NCK - 1)),
                                   wait=(([(wsem, w_tot)] + list(gfree[gb]))
                                         if ck == 0 else ()))
                        for ck in range(NCK):
                            co = ck * T + ts * TS
                            p.emit("tensor",
                                   lambda e, co=co, ck=ck, ubank=ubank, r=r: e.matmul(
                                       ubank[:, :], wuc[r][:, ck * 128:(ck + 1) * 128],
                                       ybuf[:, co:co + TS],
                                       start=(ck == 0), stop=(ck == NCK - 1)),
                                   wait=(list(ufree[ub]) if ck == 0 else ()),
                                   inc=(p.i1("t") if ck == NCK - 1 else ()))
                        t_gu = p.val("t")
                        p.emit("scalar",
                               lambda e, tsl=tsl, gbank=gbank: e.activation(
                                   slu[tsl][:], gbank[:, :], AF.Silu, scale=1.0 / 127.0),
                               wait=[("t", t_gu), ("v", slufree[tsl])], inc=p.i1("s"))
                        s_sl = p.val("s")
                        gfree[gb] = [("s", s_sl)]
                        hco = m * 1024 + tsl * TS
                        p.emit("vector",
                               lambda e, hco=hco, tsl=tsl, ubank=ubank: e.tensor_mul(
                                   hp_sb[:, hco:hco + TS], slu[tsl][:], ubank[:, :]),
                               wait=[("s", s_sl)], inc=p.i1("v"))
                        ufree[ub] = [("v", p.val("v"))]
                        slufree[tsl] = p.val("v")
                    last_wt[r] = p.val("t")
                for fh in range(2 * NCK):
                    f, half = fh // 2, fh % 2
                    r2 = fh % 2
                    usem = f"u{r2}"
                    p.dma("gpsimd", wdc[r2][:],
                          bass.AP(wd_r, (f * 24 + half * 12) * 128 * 128,
                                  [[128, 128], [128 * 128, 12], [1, 128]]),
                          usem, wait=[TW()])
                    u_tot = p.val(usem)
                    for tsl in range(2):
                        ts = 2 * hb + tsl
                        dbank = pb[2 * (f % 2) + tsl]
                        for kh in range(12):
                            khg = half * 12 + kh
                            hco = khg * 1024 + tsl * TS
                            p.emit("tensor",
                                   lambda e, hco=hco, kh=kh, dbank=dbank, r2=r2, half=half: e.matmul(
                                       dbank[:, :], wdc[r2][:, kh * 128:(kh + 1) * 128],
                                       hp_sb[:, hco:hco + TS],
                                       start=(kh == 0 and half == 0),
                                       stop=(kh == 11 and half == 1)),
                                   wait=([(usem, u_tot)] + ([VW(), SW()] if half == 0 else [])
                                         if kh == 0 else ()),
                                   inc=(p.i1("t") if (kh == 11 and half == 1) else ()))
                        if half == 1:
                            t_dn = p.val("t")
                            co = f * T + ts * TS
                            p.emit("vector",
                                   lambda e, co=co, dbank=dbank: e.scalar_tensor_tensor(
                                       x_res[:, co:co + TS], dbank[:, :], DSC,
                                       x_res[:, co:co + TS], ALU.mult, ALU.add),
                                   wait=[("t", t_dn), VW()], inc=p.i1("v"))

        def emit_l1_attn():
            if DUMP:
                p.dma("sync", dbg["dxent"].ap(), x_res[:], "dS",
                      wait=[VW(), SW(), TW()])
            SC8 = 1.0 / 8.0
            TWOPI = 6.283185307179586
            PI = 3.141592653589793
            HPI = 1.5707963267948966
            ar = Arena(AB, 36864)
            buvb = ar.take(8)
            ckv = ar.take(4096)
            cqv = ar.take(2048)
            COSq = ar.take(2048)
            RBq = ar.take(2048)
            COS1 = ar.take(512)
            RB1 = ar.take(512)
            rt1 = ar.take(512)
            rt2 = ar.take(512)
            wuk_sb = ar.take(768)
            wuv_sb = ar.take(1024)
            wuq_sb = ar.take(768)
            wqr_sb = ar.take(256)
            mark_pers = ar.off
            ya = ar.take(4096)
            rraw = ar.take(4096)
            wdkv_sb = ar.take(1024)
            wdq_sb = ar.take(1024)
            wkr_sb = ar.take(1024)

            p.dma("gpsimd", wdkv_sb[:], bass.AP(wdkv, 0, [[128, 128], [128 * 128, 8], [1, 128]]), "dG", wait=[TW(), VW()])
            p.dma("gpsimd", wdq_sb[:], bass.AP(wdq, 0, [[128, 128], [128 * 128, 8], [1, 128]]), "dG")
            p.dma("gpsimd", wuk_sb[:], wuk.ap(), "dG")
            p.dma("gpsimd", wuv_sb[:], wuv.ap(), "dG")
            p.dma("gpsimd", wuq_sb[:], wuq.ap(), "dG")
            p.dma("gpsimd", wqr_sb[:], wqr.ap(), "dG")
            p.dma("gpsimd", perm_sb[:], c_perm.ap(), "dG")
            p.dma("gpsimd", buvb[0:128, 0:8], buv_c.ap(), "dG")
            dG_w1 = p.val("dG")
            p.dma("sync", bvec_sb[:, 0:1], bdkv.ap(), "dS", wait=[VW(), SW()])
            p.dma("sync", bvec_sb[:, 1:2], bdq.ap(), "dS")
            p.dma("sync", bvec_sb[:, 2:10], bkf.ap(), "dS")
            p.dma("sync", bvec_sb[:, 10:18], bqf.ap(), "dS")
            p.dma("sync", bvec_sb[:, 18:20], bkr_c.ap(), "dS")
            p.dma("sync", bvec_sb[:, 20:22], bqr_c.ap(), "dS")
            p.dma("sync", bvec_sb[:, 22:30], bo1_c.ap(), "dS")
            p.dma("sync", invf_sb[:], c_invf.ap(), "dS")
            p.dma("sync", sign_sb[:], c_sign.ap(), "dS")
            p.dma("sync", qoffr_sb[:], c_qoff.ap(), "dS")
            dS_b1 = p.val("dS")
            p.emit("tensor", lambda e: e.matmul(pb[2][:, 0:1], onesr_f[0:1, :],
                                                qoffr_sb[0:1, 0:1], start=True, stop=True),
                   wait=[("dS", dS_b1), VW()], inc=p.i1("t"))
            t_qo = p.val("t")
            p.emit("vector", lambda e: e.tensor_copy(qoffc_sb[:], pb[2][:, 0:1]),
                   wait=[("t", t_qo)], inc=p.i1("v"))

            v_ln = p.val("v")
            p.dma("gpsimd", cc1_in.ap(), ybuf[:], "u0", wait=[("v", v_ln)])
            dG_cc = p.val("u0")
            p.emit("gpsimd",
                   lambda e: e.collective_compute(
                       "AllGather", ALU.bypass,
                       replica_groups=[[0, 1], [2, 3], [4, 5], [6, 7]],
                       ins=[cc1_in.ap().opt()], outs=[cc1_out.ap().opt()]),
                   wait=[("u0", dG_cc)], inc=p.i1("c"))
            c_ag = p.val("c")

            def build_rope_slice(base, add_qoff, cosv, coff, rbv, roff):
                ang = iot_sb[:, 0:TS]
                p.emit("gpsimd",
                       lambda e, base=base: e.iota(ang, [[1, TS]], base=base,
                                                   channel_multiplier=0,
                                                   allow_small_or_imprecise_dtypes=True),
                       wait=[VW(), SW(), ("dS", dS_b1)], inc=p.i1("g"))
                g_i = p.val("g")
                if add_qoff:
                    p.emit("vector", lambda e: e.tensor_scalar(ang, ang, qoffc_sb[:, 0:1], None, ALU.add),
                           wait=[("g", g_i)], inc=p.i1("v"))
                else:
                    p.emit("vector", lambda e: e.tensor_copy(ang, ang),
                           wait=[("g", g_i)], inc=p.i1("v"))
                p.emit("vector", lambda e: e.tensor_scalar(ang, ang, invf_sb[:, 0:1], None, ALU.mult),
                       wait=[VW()], inc=p.i1("v"))
                for KK in (16 * TWOPI, 8 * TWOPI, 4 * TWOPI, 2 * TWOPI, TWOPI):
                    p.emit("vector", lambda e, KK=KK: e.tensor_scalar(rt1[:, 0:TS], ang, KK, None, ALU.is_ge),
                           wait=[VW()], inc=p.i1("v"))
                    p.emit("vector", lambda e, KK=KK: e.scalar_tensor_tensor(ang, rt1[:, 0:TS], -KK, ang,
                                                                             ALU.mult, ALU.add),
                           wait=[VW()], inc=p.i1("v"))
                p.emit("vector", lambda e: e.tensor_scalar(rt1[:, 0:TS], ang, PI, None, ALU.is_ge),
                       wait=[VW()], inc=p.i1("v"))
                p.emit("vector", lambda e: e.scalar_tensor_tensor(ang, rt1[:, 0:TS], -TWOPI, ang,
                                                                  ALU.mult, ALU.add),
                       wait=[VW()], inc=p.i1("v"))
                v_r = p.val("v")
                p.emit("scalar", lambda e: e.activation(rt2[:, 0:TS], ang, AF.Sin),
                       wait=[("v", v_r)], inc=p.i1("s"))
                p.emit("vector",
                       lambda e, rbv=rbv, roff=roff: e.tensor_scalar(
                           rbv[:, roff:roff + TS], rt2[:, 0:TS], sign_sb[:, 0:1], None, ALU.mult),
                       wait=[SW()], inc=p.i1("v"))
                p.emit("vector", lambda e: e.tensor_scalar(ang, ang, HPI, None, ALU.add),
                       wait=[VW()], inc=p.i1("v"))
                p.emit("vector", lambda e: e.tensor_scalar(rt1[:, 0:TS], ang, PI, None, ALU.is_ge),
                       wait=[VW()], inc=p.i1("v"))
                p.emit("vector", lambda e: e.scalar_tensor_tensor(ang, rt1[:, 0:TS], -TWOPI, ang,
                                                                  ALU.mult, ALU.add),
                       wait=[VW()], inc=p.i1("v"))
                v_c = p.val("v")
                p.emit("scalar",
                       lambda e, cosv=cosv, coff=coff: e.activation(
                           cosv[:, coff:coff + TS], ang, AF.Sin),
                       wait=[("v", v_c)], inc=p.i1("s"))

            def apply_rope_slice(buf, boff, nr, cosv, coff, rbv, roff):
                p.emit("tensor",
                       lambda e, boff=boff, nr=nr: e.matmul(
                           pb[3][0:nr, 0:TS], perm_sb[0:nr, 0:nr],
                           buf[0:nr, boff:boff + TS], start=True, stop=True),
                       wait=[VW(), SW(), ("dG", dG_w1)], inc=p.i1("t"))
                t_sh = p.val("t")
                p.emit("vector",
                       lambda e, nr=nr, rbv=rbv, roff=roff: e.tensor_mul(
                           rt1[0:nr, 0:TS], pb[3][0:nr, 0:TS], rbv[0:nr, roff:roff + TS]),
                       wait=[("t", t_sh)], inc=p.i1("v"))
                p.emit("vector",
                       lambda e, boff=boff, nr=nr, cosv=cosv, coff=coff: e.tensor_mul(
                           rt2[0:nr, 0:TS], buf[0:nr, boff:boff + TS], cosv[0:nr, coff:coff + TS]),
                       wait=[VW()], inc=p.i1("v"))
                p.emit("vector",
                       lambda e, boff=boff, nr=nr: e.tensor_add(
                           buf[0:nr, boff:boff + TS], rt1[0:nr, 0:TS], rt2[0:nr, 0:TS]),
                       wait=[VW()], inc=p.i1("v"))

            for ts2 in range(8):
                r, tsl = ts2 // 4, ts2 % 4
                p.dma("scalar", ya[:],
                      bass.AP(cc1_out, r * 128 * 16384 + tsl * TS,
                              [[16384, 128], [T, NCK], [1, TS]]),
                      "dA", wait=[("c", c_ag), TW()])
                dA_y = p.val("dA")
                bank = pb[ts2 % 2]
                for ck in range(NCK):
                    p.emit("tensor",
                           lambda e, ck=ck, bank=bank: e.matmul(
                               bank[:, :], wdkv_sb[:, ck * 128:(ck + 1) * 128],
                               ya[:, ck * TS:(ck + 1) * TS],
                               start=(ck == 0), stop=(ck == NCK - 1)),
                           wait=([("dA", dA_y), ("dG", dG_w1), VW()] if ck == 0 else ()),
                           inc=(p.i1("t") if ck == NCK - 1 else ()))
                t_ck = p.val("t")
                p.emit("vector",
                       lambda e, ts2=ts2, bank=bank: e.tensor_scalar(
                           ckv[:, ts2 * TS:(ts2 + 1) * TS], bank[:, :],
                           bvec_sb[:, 0:1], None, ALU.add),
                       wait=[("t", t_ck), ("dS", dS_b1)], inc=p.i1("v"))
            for tsl in range(4):
                bank = pb[2 + (tsl % 2)]
                for ck in range(NCK):
                    p.emit("tensor",
                           lambda e, ck=ck, tsl=tsl, bank=bank: e.matmul(
                               bank[:, :], wdq_sb[:, ck * 128:(ck + 1) * 128],
                               ybuf[:, ck * T + tsl * TS:ck * T + tsl * TS + TS],
                               start=(ck == 0), stop=(ck == NCK - 1)),
                           wait=([VW()] if ck == 0 else ()),
                           inc=(p.i1("t") if ck == NCK - 1 else ()))
                t_cq = p.val("t")
                p.emit("vector",
                       lambda e, tsl=tsl, bank=bank: e.tensor_scalar(
                           cqv[:, tsl * TS:(tsl + 1) * TS], bank[:, :],
                           bvec_sb[:, 1:2], None, ALU.add),
                       wait=[("t", t_cq)], inc=p.i1("v"))

            for rc in range(2):
                p.dma("gpsimd", wkr_sb[:],
                      bass.AP(wkr, rc * 128, [[256, 128], [256 * 128, 8], [1, 128]]),
                      "dG", wait=[TW()])
                dG_kr = p.val("dG")
                for ts2 in range(8):
                    r, tsl = ts2 // 4, ts2 % 4
                    p.dma("scalar", ya[:],
                          bass.AP(cc1_out, r * 128 * 16384 + tsl * TS,
                                  [[16384, 128], [T, NCK], [1, TS]]),
                          "dA", wait=[TW()])
                    dA_y = p.val("dA")
                    bank = pb[ts2 % 2]
                    for ck in range(NCK):
                        p.emit("tensor",
                               lambda e, ck=ck, bank=bank: e.matmul(
                                   bank[:, :], wkr_sb[:, ck * 128:(ck + 1) * 128],
                                   ya[:, ck * TS:(ck + 1) * TS],
                                   start=(ck == 0), stop=(ck == NCK - 1)),
                               wait=([("dA", dA_y), ("dG", dG_kr), VW()] if ck == 0 else ()),
                               inc=(p.i1("t") if ck == NCK - 1 else ()))
                    t_kr = p.val("t")
                    p.emit("vector",
                           lambda e, ts2=ts2, rc=rc, bank=bank: e.tensor_scalar(
                               rraw[:, ts2 * TS:(ts2 + 1) * TS], bank[:, :],
                               bvec_sb[:, 18 + rc:19 + rc], None, ALU.add),
                           wait=([("t", t_kr)] + ([("dS", p.val("dS"))] if ts2 == 0 else [])),
                           inc=p.i1("v"))
                for sl in range(8):
                    build_rope_slice(sl * TS, False, COS1, 0, RB1, 0)
                    apply_rope_slice(rraw, sl * TS, 128, COS1, 0, RB1, 0)
                v_rope = p.val("v")
                p.dma("sync", bass.AP(krd, rc * 128 * 4096, [[4096, 128], [1, 4096]]),
                      rraw[:], "dS", wait=[("v", v_rope)])
            dS_krd = p.val("dS")

            for sl in range(4):
                build_rope_slice(sl * TS, True, COSq, sl * TS, RBq, sl * TS)

            if DUMP:
                p.dma("sync", dbg["dck"].ap(), ckv[:], "dS", wait=[VW()])
                p.dma("sync", dbg["dcq"].ap(), cqv[:], "dS")
            ar.reset(mark_pers)
            kwv = ar.take(4096)
            qwv = ar.take(2048)
            v65 = ar.take(32 * 130)
            qraw = ar.take(2048)
            auwA = ar.take(2048)
            auwB = ar.take(2048)
            eT = [ar.take(512) for _ in range(4)]
            zbs = ar.take(512)
            ABC = 36864

            for w in range(8):
                hA, hB = 2 * w, 2 * w + 1
                p.emit("vector",
                       lambda e: e.memset(
                           bass.AP(AB, v65.off + 64, [[ABC, 128], [130, 32], [65, 2], [1, 1]]),
                           1.0),
                       wait=([TW()] + ([("dS", p.val("dS")), ("dA", p.val("dA")),
                                        ("dG", p.val("dG")), SW()] if (w == 0 or DUMP) else [])),
                       inc=p.i1("v"))
                for tt in range(32):
                    bank = pb[tt % 2]
                    p.emit("tensor",
                           lambda e, tt=tt, w=w, bank=bank: e.matmul(
                               bank[:, 0:128], ckv[:, tt * 128:(tt + 1) * 128],
                               wuv_sb[:, w * 128:(w + 1) * 128],
                               start=True, stop=True),
                           wait=[VW()], inc=p.i1("t"))
                    t_v = p.val("t")
                    p.emit("vector",
                           lambda e, tt=tt, bank=bank: e.tensor_copy(
                               bass.AP(AB, v65.off + tt * 130, [[ABC, 128], [65, 2], [1, 64]]),
                               bank[:, 0:128]),
                           wait=[("t", t_v)], inc=p.i1("v"))
                for ts2 in range(8):
                    bank = pb[2 + ts2 % 2]
                    for hh in range(2):
                        h = 2 * w + hh
                        p.emit("tensor",
                               lambda e, ts2=ts2, h=h, hh=hh, bank=bank: e.matmul(
                                   bank[64 * hh:64 * hh + 48, :],
                                   wuk_sb[:, h * 48:h * 48 + 48],
                                   ckv[:, ts2 * TS:(ts2 + 1) * TS],
                                   start=True, stop=True,
                                   tile_position=(0, 64 * hh)),
                               wait=([VW()] if hh == 0 else ()),
                               inc=(p.i1("t") if hh == 1 else ()))
                    t_kn = p.val("t")
                    for hh in range(2):
                        p.emit("vector",
                               lambda e, ts2=ts2, hh=hh, w=w, bank=bank: e.tensor_scalar(
                                   kwv[64 * hh:64 * hh + 48, ts2 * TS:(ts2 + 1) * TS],
                                   bank[64 * hh:64 * hh + 48, :],
                                   bvec_sb[64 * hh:64 * hh + 48, 2 + w:3 + w],
                                   None, ALU.add),
                               wait=([("t", t_kn)] if hh == 0 else ()),
                               inc=(p.i1("v") if hh == 1 else ()))
                v_kn = p.val("v")
                for hh in range(2):
                    h = 2 * w + hh
                    rc, rr = h // 8, (h % 8) * 16
                    p.dma("sync", kwv[48 + 64 * hh:64 + 64 * hh, 0:4096],
                          bass.AP(krd, rc * 128 * 4096 + rr * 4096, [[4096, 16], [1, 4096]]),
                          "dS", wait=[("v", v_kn), TW(), ("dS", p.val("dS"))])
                dS_krot = p.val("dS")
                for tsl in range(4):
                    bank = pb[4 + tsl % 2]
                    for hh in range(2):
                        h = 2 * w + hh
                        p.emit("tensor",
                               lambda e, tsl=tsl, h=h, hh=hh, bank=bank: e.matmul(
                                   bank[64 * hh:64 * hh + 48, :],
                                   wuq_sb[:, h * 48:h * 48 + 48],
                                   cqv[:, tsl * TS:(tsl + 1) * TS],
                                   start=True, stop=True,
                                   tile_position=(0, 64 * hh)),
                               wait=([VW()] if hh == 0 else ()),
                               inc=(p.i1("t") if hh == 1 else ()))
                    t_qn = p.val("t")
                    for hh in range(2):
                        p.emit("vector",
                               lambda e, tsl=tsl, hh=hh, w=w, bank=bank: e.tensor_scalar(
                                   qwv[64 * hh:64 * hh + 48, tsl * TS:(tsl + 1) * TS],
                                   bank[64 * hh:64 * hh + 48, :],
                                   bvec_sb[64 * hh:64 * hh + 48, 10 + w:11 + w],
                                   None, ALU.add),
                               wait=([("t", t_qn)] if hh == 0 else ()),
                               inc=(p.i1("v") if hh == 1 else ()))
                for tsl in range(4):
                    bank = pb[4 + tsl % 2]
                    p.emit("tensor",
                           lambda e, tsl=tsl, w=w, bank=bank: e.matmul(
                               bank[0:32, :], wqr_sb[:, w * 32:(w + 1) * 32],
                               cqv[:, tsl * TS:(tsl + 1) * TS],
                               start=True, stop=True),
                           wait=[VW()], inc=p.i1("t"))
                    t_qr = p.val("t")
                    p.emit("vector",
                           lambda e, tsl=tsl, w=w, bank=bank: e.tensor_scalar(
                               qraw[0:32, tsl * TS:(tsl + 1) * TS], bank[0:32, :],
                               bvec_sb[(w % 4) * 32:(w % 4) * 32 + 32, 20 + w // 4:21 + w // 4],
                               None, ALU.add),
                           wait=([("t", t_qr)] + ([("dS", p.val("dS"))] if tsl == 0 else [])),
                           inc=p.i1("v"))
                for sl in range(4):
                    apply_rope_slice(qraw, sl * TS, 32, COSq, sl * TS, RBq, sl * TS)
                v_qrope = p.val("v")
                for hh in range(2):
                    p.dma("sync", qwv[48 + 64 * hh:64 + 64 * hh, 0:2048],
                          qraw[16 * hh:16 * hh + 16, 0:2048],
                          "dS", wait=[("v", v_qrope), TW(), ("dS", p.val("dS"))])
                dS_qrot = p.val("dS")

                if DUMP and w == 0:
                    p.dma("sync", dbg["dkw"].ap(), kwv[:], "dS", wait=[VW(), ("dS", p.val("dS"))])
                    p.dma("sync", dbg["dqw"].ap(), qwv[:], "dS")
                    p.dma("sync", dbg["dv65"].ap(), v65[:], "dS")
                for qp in range(2):
                    avb = [pb[2 + 2 * j + hh] for hh in range(2) for j in range(2)]
                    for kt in range(32):
                        for j in range(2):
                            qs = 2 * qp + j
                            for hh in range(2):
                                sb = pb[hh]
                                p.emit("tensor",
                                       lambda e, kt=kt, hh=hh, qs=qs, sb=sb: e.matmul(
                                           sb[:, :], kwv[64 * hh:64 * hh + 64, kt * 128:(kt + 1) * 128],
                                           qwv[64 * hh:64 * hh + 64, qs * TS:(qs + 1) * TS],
                                           start=True, stop=True),
                                       wait=(([SW(), ("dS", dS_qrot)] + ([VW()] if kt == 0 else []))
                                             if hh == 0 else ()),
                                       inc=(p.i1("t") if hh == 1 else ()))
                            t_sc = p.val("t")
                            for hh in range(2):
                                sb = pb[hh]
                                p.emit("scalar",
                                       lambda e, hh=hh, j=j, sb=sb: e.activation(
                                           eT[2 * hh + j][:], sb[:, :], AF.Exp, scale=SC8),
                                       wait=([("t", t_sc)] if hh == 0 else ()),
                                       inc=(p.i1("s") if hh == 1 else ()))
                            s_e = p.val("s")
                            for hh in range(2):
                                p.emit("tensor",
                                       lambda e, kt=kt, hh=hh, j=j: e.matmul(
                                           avb[2 * hh + j][0:65, :],
                                           bass.AP(AB, v65.off + kt * 130 + 65 * hh,
                                                   [[ABC, 128], [1, 65]]),
                                           eT[2 * hh + j][:],
                                           start=(kt == 0), stop=(kt == 31)),
                                       wait=([("s", s_e)] if hh == 0 else ()),
                                       inc=(p.i1("t") if (kt == 31 and j == 1 and hh == 1) else ()))
                    t_av = p.val("t")
                    for hh in range(2):
                        auw = auwA if hh == 0 else auwB
                        for j in range(2):
                            qs = 2 * qp + j
                            av = avb[2 * hh + j]
                            p.emit("vector",
                                   lambda e, av=av, j=j: e.reciprocal(
                                       zrowf_sb[0:1, j * TS:(j + 1) * TS], av[64:65, :]),
                                   wait=[("t", t_av), VW(), ("dA", p.val("dA"))], inc=p.i1("v"))
                            p.emit("vector",
                                   lambda e, j=j: e.tensor_copy(
                                       zrow_sb[0:1, j * TS:(j + 1) * TS],
                                       zrowf_sb[0:1, j * TS:(j + 1) * TS]),
                                   wait=[VW()], inc=p.i1("v"))
                            v_z = p.val("v")
                            zb = pb[hh]
                            p.emit("tensor",
                                   lambda e, zb=zb, j=j: e.matmul(
                                       zb[0:64, :], onesr_bf[0:1, 0:64],
                                       zrow_sb[0:1, j * TS:(j + 1) * TS],
                                       start=True, stop=True),
                                   wait=[("v", v_z)], inc=p.i1("t"))
                            t_zb = p.val("t")
                            p.emit("vector",
                                   lambda e, zb=zb: e.tensor_copy(zbs[0:64, 0:TS], zb[0:64, :]),
                                   wait=[("t", t_zb)], inc=p.i1("v"))
                            v_zc = p.val("v")
                            p.emit("vector",
                                   lambda e, auw=auw, av=av, qs=qs: e.tensor_mul(
                                       auw[0:64, qs * TS:(qs + 1) * TS],
                                       av[0:64, :], zbs[0:64, 0:TS]),
                                   wait=[("v", v_zc)], inc=p.i1("v"))
                v_au = p.val("v")
                for hh in range(2):
                    auw = auwA if hh == 0 else auwB
                    h = 2 * w + hh
                    p.dma("scalar", bass.AP(aud, 0, [[1, 1]]) if False else
                          bass.AP(aud, h * 64 * 2048, [[2048, 64], [1, 2048]]),
                          auw[0:64, 0:2048],
                          "dA", wait=[("v", v_au), ("dA", p.val("dA"))])
            dA_au = p.val("dA")

            ar.reset(16)
            auf = ar.take(16384)
            wof = ar.take(8192)
            bcorb = ar.take(16)
            p.dma("scalar", auf[:],
                  bass.AP(aud, 0, [[2048, 128], [128 * 2048, 8], [1, 2048]]),
                  "dA", wait=[("dA", dA_au), TW(), VW()])
            dA_auf = p.val("dA")
            if DUMP:
                p.dma("sync", dbg["dau"].ap(), auf[:], "dS", wait=[("dA", dA_auf)])
                p.dma("sync", dbg["dxpre"].ap(), x_res[:], "dS", wait=[VW()])
            p.dma("gpsimd", wof[:],
                  bass.AP(wo1, 0, [[1024, 128], [128 * 1024, 8], [1, 1024]]),
                  "dG", wait=[TW()])
            dG_wo = p.val("dG")
            for oc in range(NCK):
                for ck in range(NCK):
                    p.emit("tensor",
                           lambda e, oc=oc, ck=ck: e.matmul(
                               pb[6][:, oc:oc + 1],
                               wof[:, ck * 1024 + oc * 128:ck * 1024 + oc * 128 + 128],
                               buvb[:, ck:ck + 1],
                               start=(ck == 0), stop=(ck == NCK - 1)),
                           wait=([("dG", dG_wo), ("dG", dG_w1), VW()] if (oc == 0 and ck == 0) else ()),
                           inc=(p.i1("t") if (oc == NCK - 1 and ck == NCK - 1) else ()))
            t_bc = p.val("t")
            if DUMP:
                p.dma("sync", dbg["dwof"].ap(), wof[:], "dS", wait=[("dG", p.val("dG"))])
            p.emit("vector",
                   lambda e: e.tensor_add(bvec2_sb[:, 0:8], pb[6][:, 0:8], bvec_sb[:, 22:30]),
                   wait=[("t", t_bc)], inc=p.i1("v"))
            v_bc = p.val("v")
            for tsl in range(4):
                for oc in range(NCK):
                    bank = pb[oc % 2]
                    for ck in range(NCK):
                        p.emit("tensor",
                               lambda e, tsl=tsl, oc=oc, ck=ck, bank=bank: e.matmul(
                                   bank[:, :],
                                   wof[:, ck * 1024 + oc * 128:ck * 1024 + oc * 128 + 128],
                                   auf[:, ck * 2048 + tsl * TS:ck * 2048 + tsl * TS + TS],
                                   start=(ck == 0), stop=(ck == NCK - 1)),
                               wait=(([("dA", dA_auf), ("v", v_bc), VW()] +
                                      ([("dS", p.val("dS"))] if DUMP else []))
                                     if ck == 0 else ()),
                               inc=(p.i1("t") if ck == NCK - 1 else ()))
                    t_o = p.val("t")
                    co = oc * T + tsl * TS
                    p.emit("vector",
                           lambda e, co=co, oc=oc, bank=bank: e.scalar_tensor_tensor(
                               x_res[:, co:co + TS], bank[:, :],
                               bvec2_sb[:, oc:oc + 1],
                               x_res[:, co:co + TS], ALU.add, ALU.add),
                           wait=[("t", t_o)], inc=p.i1("v"))

        def emit_l0_attn():
            SC8 = 1.0 / 8.0
            ar = Arena(AB, 36864)
            rk_fm = ar.take(2048)
            masks = [ar.take(512) for _ in range(4)]
            zt = ar.take(512)
            idn = ar.take(128)
            mark0 = ar.off
            qwv = ar.take(4096)
            kwv = ar.take(4096)
            v65 = ar.take(32 * 130)
            wqc = ar.take(1024)
            wkc = ar.take(1024)
            wvc = ar.take(1024)
            ebs = ar.take(2048)
            fsl = [ar.take(512) for _ in range(4)]
            esl = [ar.take(512) for _ in range(2)]
            psl = [ar.take(512) for _ in range(2)]
            ptl = [ar.take(512) for _ in range(2)]
            auh = ar.take(4096)
            zbs0 = ar.take(512)
            ya0 = ar.take(4096)


            p.dma("gpsimd", idn[:], c_ident.ap(), "dG", wait=[TW(), VW(), SW()])
            p.dma("sync", bvec_sb[:, 0:4], bq0.ap(), "dS", wait=[VW(), SW()])
            p.dma("sync", bvec_sb[:, 4:8], bk0.ap(), "dS")
            p.dma("sync", bvec_sb[:, 8:16], bvf0.ap(), "dS")
            p.dma("sync", bvec_sb[:, 16:24], bo0.ap(), "dS")
            dS_b0 = p.val("dS")
            p.dma("gpsimd", ya0[0:128, 0:1024],
                  bass.AP(rel0, 0, [[64, 128], [128 * 64, 16], [1, 64]]), "dG")
            dG_rel = p.val("dG")
            p.emit("vector", lambda e: e.memset(zt[:], 0.0), wait=[TW()], inc=p.i1("v"))
            for mi in range(4):
                p.emit("gpsimd",
                       lambda e, mi=mi: e.memset(masks[mi][:], 1.0),
                       inc=p.i1("g"))
                g_m = p.val("g")
                p.emit("gpsimd",
                       lambda e, mi=mi: e.affine_select(
                           masks[mi][:], masks[mi][:], [[-1, 512]], ALU.is_ge, 0.0,
                           base=mi * 128, channel_multiplier=1),
                       wait=[("g", g_m)], inc=p.i1("g"))
                p.emit("gpsimd",
                       lambda e, mi=mi: e.memset(maski_sb[:, mi * 512:(mi + 1) * 512], 1),
                       inc=p.i1("g"))
                g_m3 = p.val("g")
                p.emit("gpsimd",
                       lambda e, mi=mi: e.affine_select(
                           maski_sb[:, mi * 512:(mi + 1) * 512],
                           maski_sb[:, mi * 512:(mi + 1) * 512],
                           [[-1, 512]], ALU.is_ge, 0,
                           base=mi * 128, channel_multiplier=1),
                       wait=[("g", g_m3)], inc=p.i1("g"))
            g_masks = p.val("g")
            for zq in range(12):
                p.dma("scalar",
                      bass.AP(slab_all, 768 + zq * 128 * 2048, [[2048, 128], [1, 512]]),
                      zt[:], "dA", wait=[VW()])
            dA_pad = p.val("dA")
            for mt in range(16):
                p.emit("tensor",
                       lambda e, mt=mt: e.transpose(
                           pbt[0:64, (mt % 8) * 128:(mt % 8) * 128 + 128],
                           ya0[0:128, mt * 64:(mt + 1) * 64], idn[0:128, 0:128]),
                       wait=([("dG", dG_rel), VW()] if mt == 0 else
                             ([TW(), VW()] if mt == 8 else ())),
                       inc=(p.i1("t") if mt % 8 == 7 else ()))
                if mt % 8 == 7:
                    t_rk = p.val("t")
                    p.emit("vector",
                           lambda e, mt=mt: e.tensor_copy(
                               rk_fm[0:64, (mt - 7) * 128:(mt + 1) * 128],
                               pbt[0:64, 0:1024]),
                           wait=[("t", t_rk)], inc=p.i1("v"))
            v_rk = p.val("v")
            p.dma("sync", rk_fm[64:128, 0:2048], rk_fm[0:64, 0:2048], "r0",
                  wait=[("v", v_rk)])
            w1_dup = p.val("r0")

            v_ln = p.val("v")
            p.dma("gpsimd", cc0_in.ap(), ybuf[:], "u0", wait=[("v", v_ln)])
            u_cc = p.val("u0")
            p.emit("gpsimd",
                   lambda e: e.collective_compute(
                       "AllGather", ALU.bypass,
                       replica_groups=[[0, 1], [2, 3], [4, 5], [6, 7]],
                       ins=[cc0_in.ap().opt()], outs=[cc0_out.ap().opt()]),
                   wait=[("u0", u_cc)], inc=p.i1("c"))
            c_ag = p.val("c")

            _l0stop = int(_os.environ.get("DEV_L0STOP", "0"))
            if DUMP and _l0stop >= 1:
                p.dma("sync", dbg["dm0"].ap(), masks[0][:], "st",
                      wait=[("g", p.val("g")), ("dA", p.val("dA")), VW()])
                p.dma("sync", dbg["deb"].ap(),
                      bass.AP(slab_all, 768, [[2048, 128], [1, 2048]]), "st")
            FS = ("f0", "f1", "r0", "st")
            _cv, _cs, _ct = p.val("v"), p.val("s"), p.val("t")
            bankfree = {b: [("v", _cv), ("s", _cs)] for b in (0, 1, 3, 4, 6)}
            avfree = {5: [("v", _cv)], 2: [("v", _cv)]}
            eslfree = [_cv, _cv]
            fslfree = [_cv] * 4
            f0free = _cv
            ebsfree = [0] * 4
            pslfree = [_ct, _ct]
            pbt_free = _cv
            ptlfree = [_ct, _ct]
            for w in range(0 if _l0stop >= 1 else 4):
                p.dma("gpsimd", wqc[:],
                      bass.AP(wq0, w * 128, [[512, 128], [512 * 128, 8], [1, 128]]),
                      "w0", wait=[TW()])
                p.dma("gpsimd", wkc[:],
                      bass.AP(wk0, w * 128, [[512, 128], [512 * 128, 8], [1, 128]]),
                      "w0")
                p.dma("gpsimd", wvc[:],
                      bass.AP(wv0, w * 128, [[512, 128], [512 * 128, 8], [1, 128]]),
                      "w0")
                w_w = p.val("w0")
                for ts2 in range(8):
                    r, tsl = ts2 // 4, ts2 % 4
                    p.dma("scalar", ya0[:],
                          bass.AP(cc0_out, r * 128 * 16384 + tsl * TS,
                                  [[16384, 128], [T, NCK], [1, TS]]),
                          "dA", wait=[("c", c_ag), TW()])
                    dA_y = p.val("dA")
                    for mm_i, (wc, dst, bcol) in enumerate(
                            ((wqc, qwv, 0), (wkc, kwv, 4))):
                        bank = pb[mm_i]
                        for ck in range(NCK):
                            p.emit("tensor",
                                   lambda e, ck=ck, wc=wc, bank=bank: e.matmul(
                                       bank[:, :], wc[:, ck * 128:(ck + 1) * 128],
                                       ya0[:, ck * TS:(ck + 1) * TS],
                                       start=(ck == 0), stop=(ck == NCK - 1)),
                                   wait=([("dA", dA_y), ("w0", w_w), VW(), SW()] if ck == 0 else ()),
                                   inc=(p.i1("t") if ck == NCK - 1 else ()))
                        t_p = p.val("t")
                        p.emit("vector",
                               lambda e, ts2=ts2, dst=dst, bank=bank, bcol=bcol, w=w: e.tensor_scalar(
                                   dst[:, ts2 * TS:(ts2 + 1) * TS], bank[:, :],
                                   bvec_sb[:, bcol + w:bcol + w + 1], None, ALU.add),
                               wait=[("t", t_p), ("dS", dS_b0)], inc=p.i1("v"))
                    for tt4 in range(4):
                        tt = ts2 * 4 + tt4
                        bank = pb[2 + tt4 % 2]
                        for ck in range(NCK):
                            p.emit("tensor",
                                   lambda e, ck=ck, tt4=tt4, bank=bank: e.matmul(
                                       bank[:, 0:128],
                                       ya0[:, ck * TS + tt4 * 128:ck * TS + tt4 * 128 + 128],
                                       wvc[:, ck * 128:(ck + 1) * 128],
                                       start=(ck == 0), stop=(ck == NCK - 1)),
                                   wait=([VW(), SW()] if ck == 0 else ()),
                                   inc=(p.i1("t") if ck == NCK - 1 else ()))
                        t_v = p.val("t")
                        p.emit("vector",
                               lambda e, tt=tt, bank=bank: e.tensor_copy(
                                   bass.AP(AB, v65.off + tt * 130, [[36864, 128], [65, 2], [1, 64]]),
                                   bank[:, 0:128]),
                               wait=[("t", t_v)], inc=p.i1("v"))
                p.emit("vector",
                       lambda e: e.memset(
                           bass.AP(AB, v65.off + 64, [[36864, 128], [130, 32], [65, 2], [1, 1]]),
                           1.0),
                       inc=p.i1("v"))
                v_wave = p.val("v")

                bankfree[0] = [("v", v_wave)]
                bankfree[1] = [("v", v_wave)]
                bankfree[3] = [("v", v_wave)]
                avfree[2] = [("v", v_wave)]
                for hh in range(2):
                    h = 2 * w + hh
                    steps = []
                    for qt in range(32):
                        span = qt * 128 + 128
                        Wn = min((span + 511) // 512, 4) * 512
                        for bs in range(Wn // 512):
                            steps.append((qt, bs, Wn))
                    NBS = len(steps)
                    t_bmm = {}
                    dA_q = {}
                    e0w = {}
                    for n in range(NBS + 1):
                        if n < NBS:
                            qt, bs, Wn = steps[n]
                            bB = 4 if n % 2 == 0 else 0
                            p.emit("tensor",
                                   lambda e, qt=qt, bs=bs, Wn=Wn, hh=hh, bB=bB: e.matmul(
                                       pb[bB][:, :],
                                       qwv[64 * hh:64 * hh + 64, qt * 128:(qt + 1) * 128],
                                       rk_fm[64 * hh:64 * hh + 64,
                                             (2048 - Wn) + bs * 512:(2048 - Wn) + bs * 512 + 512],
                                       start=True, stop=True),
                                   wait=(list(bankfree[bB]) +
                                         ([("r0", w1_dup)] if n == 0 else [])),
                                   inc=p.i1("t"))
                            t_bmm[n] = p.val("t")
                        m = n - 1
                        if m >= 0:
                            qt, bs, Wn = steps[m]
                            bB = 4 if m % 2 == 0 else 0
                            slot = bs
                            p.emit("scalar",
                                   lambda e, slot=slot, bB=bB: e.activation(
                                       ebs[:, slot * TS:slot * TS + TS], pb[bB][:, :],
                                       AF.Exp, scale=SC8),
                                   wait=[("t", t_bmm[m]), (f"e{slot}", ebsfree[slot])] +
                                        ([("v", f0free)] if slot == 0 else []),
                                   inc=p.i1("s"))
                            s_exp = p.val("s")
                            bankfree[bB] = [("s", s_exp)]
                            if bs == 0 and Wn == 2048:
                                p.emit("vector",
                                       lambda e, qt=qt: e.tensor_copy(
                                           f0_sb[:, qt:qt + 1], ebs[:, 0:1]),
                                       wait=[("s", s_exp)], inc=p.i1("v"))
                                f0free = p.val("v")
                            p.dma("sync",
                                  bass.AP(slab_all,
                                          768 + qt * 262144 + (2048 - Wn) + bs * 512,
                                          [[2048, 128], [1, 512]]),
                                  ebs[:, slot * TS:slot * TS + TS],
                                  f"e{slot}",
                                  wait=([("s", s_exp)] +
                                        ([(fs, p.val(fs)) for fs in FS] if m == 0 else [])))
                            ebsfree[slot] = p.val(f"e{slot}")
                            if bs == 0:
                                e0w[qt] = p.val("e0")
                            if bs == Wn // 512 - 1:
                                dA_q[qt] = [(f"e{sl}", p.val(f"e{sl}"))
                                            for sl in range(Wn // 512)]
                    for qt in range(32):
                        I0 = qt * 128
                        span = I0 + 128
                        nkt = (span + 511) // 512
                        Wn = min(nkt, 4) * 512
                        qpar = qt % 2
                        bSs = (6, 3) if qpar == 0 else (1, 3)
                        bAV = 5 if qpar == 0 else 2
                        dA_slab = dA_q[qt]
                        kinfo = []
                        for kt in range(nkt):
                            ktw = min(512, span - kt * 512)
                            c2 = I0 - kt * 512 - 2048
                            kinfo.append(dict(
                                ktw=ktw, c2=c2,
                                fullsat=(c2 >= 511), nosat=(c2 <= -128),
                                diag=(kt == nkt - 1), ntp=(ktw + 127) // 128))
                        st_exp = {}
                        st_p = {}
                        st_tp = {}
                        st_ev = {}
                        st_av = {}
                        st_f = {}
                        LAG1, LAG2 = 1, 2
                        for i in range(nkt + LAG2):
                            kt = i
                            for kf in ([0, 1] if i == 0 else [i + 1]):
                                if kf >= nkt:
                                    continue
                                kif = kinfo[kf]
                                if kif["fullsat"]:
                                    continue
                                fs = FS[kf % 4]
                                xw = (list(dA_slab) +
                                      [("v", fslfree[kf % 4])] +
                                      ([("e0", e0w[qt + 1])]
                                       if (kif["diag"] and qt + 1 <= 31
                                           and qt + 1 >= 12) else []))
                                p.dma("sync", fsl[kf % 4][:],
                                      bass.AP(slab_all,
                                              768 + qt * 262144 + 2047 - I0 + kf * 512,
                                              [[2047, 128], [1, 512]]),
                                      fs, wait=xw)
                                st_f[kf] = (fs, p.val(fs))
                            if kt < nkt:
                                ki = kinfo[kt]
                                bk = bSs[kt % 2]
                                p.emit("tensor",
                                       lambda e, hh=hh, qt=qt, kt=kt, bk=bk: e.matmul(
                                           pb[bk][:, :], qwv[64 * hh:64 * hh + 64, qt * 128:(qt + 1) * 128],
                                           kwv[64 * hh:64 * hh + 64, kt * 512:kt * 512 + 512],
                                           start=True, stop=True),
                                       wait=list(bankfree[bk]), inc=p.i1("t"))
                                t_s = p.val("t")
                                p.emit("scalar",
                                       lambda e, sl2=kt % 2, bk=bk: e.activation(
                                           esl[sl2][:], pb[bk][:, :], AF.Exp, scale=SC8),
                                       wait=[("t", t_s), ("v", eslfree[kt % 2])],
                                       inc=p.i1("s"))
                                st_exp[kt] = p.val("s")
                                bankfree[bk] = [("s", st_exp[kt])]
                            k2 = i - LAG2
                            if 0 <= k2 < nkt:
                                ki = kinfo[k2]
                                po = (k2 % 2) * 512
                                p.emit("vector",
                                       lambda e, sl2=k2 % 2, po=po, ntp=ki["ntp"]: e.tensor_copy(
                                           ptl[sl2][0:128, 0:ntp * 128],
                                           pbt[:, po:po + ntp * 128]),
                                       wait=[("t", max(st_tp[k2], ptlfree[k2 % 2]))],
                                       inc=p.i1("v"))
                                st_ev[k2] = p.val("v")
                                pbt_free = st_ev[k2]
                                for tp in range(ki["ntp"]):
                                    ktt = k2 * 4 + tp
                                    p.emit("tensor",
                                           lambda e, ktt=ktt, hh=hh, sl2=k2 % 2, tp=tp,
                                                  span=span, bAV=bAV: e.matmul(
                                               pb[bAV][0:65, 0:128],
                                               bass.AP(AB, v65.off + ktt * 130 + 65 * hh,
                                                       [[36864, 128], [1, 65]]),
                                               ptl[sl2][0:128, tp * 128:(tp + 1) * 128],
                                               start=(ktt == 0), stop=(ktt == (span // 128) - 1)),
                                           wait=([("v", st_ev[k2])] +
                                                 (list(avfree[bAV]) if k2 == 0 else [])
                                                 if tp == 0 else ()),
                                           inc=(p.i1("t") if tp == ki["ntp"] - 1 else ()))
                                st_av[k2] = p.val("t")
                                ptlfree[k2 % 2] = st_av[k2]
                            k1 = i - LAG1
                            if 0 <= k1 < nkt:
                                ki = kinfo[k1]
                                ps = psl[k1 % 2]
                                pw = [("s", st_exp[k1]), ("t", pslfree[k1 % 2])]
                                if k1 in st_f:
                                    pw.append(st_f[k1])
                                if ki["fullsat"]:
                                    p.emit("vector",
                                           lambda e, ps=ps, sl2=k1 % 2, qt=qt: e.tensor_scalar(
                                               ps[:], esl[sl2][:], f0_sb[:, qt:qt + 1],
                                               None, ALU.mult),
                                           wait=pw, inc=p.i1("v"))
                                else:
                                    p.emit("vector",
                                           lambda e, ps=ps, sl2=k1 % 2, sl4=k1 % 4: e.tensor_mul(
                                               ps[:], esl[sl2][:], fsl[sl4][:]),
                                           wait=pw, inc=p.i1("v"))
                                    if not ki["nosat"]:
                                        mi = ki["c2"] // 128
                                        assert 0 <= mi <= 3
                                        p.emit("vector",
                                               lambda e, sl2=k1 % 2, qt=qt: e.tensor_scalar(
                                                   esl[sl2][:], esl[sl2][:],
                                                   f0_sb[:, qt:qt + 1], None, ALU.mult),
                                               wait=[VW()], inc=p.i1("v"))
                                        p.emit("vector",
                                               lambda e, ps=ps, sl2=k1 % 2, mi=mi: e.copy_predicated(
                                                   ps[:], maski_sb[:, mi * 512:(mi + 1) * 512],
                                                   esl[sl2][:]),
                                               wait=[VW()], inc=p.i1("v"))
                                    if ki["diag"]:
                                        mi2 = (I0 - (nkt - 1) * 512) // 128
                                        assert 0 <= mi2 <= 3
                                        p.emit("vector",
                                               lambda e, ps=ps, mi2=mi2: e.tensor_mul(
                                                   ps[:], ps[:], masks[mi2][:]),
                                               wait=[VW()], inc=p.i1("v"))
                                st_p[k1] = p.val("v")
                                eslfree[k1 % 2] = st_p[k1]
                                if k1 in st_f:
                                    fslfree[k1 % 4] = st_p[k1]
                                po = (k1 % 2) * 512
                                for tp in range(ki["ntp"]):
                                    p.emit("tensor",
                                           lambda e, ps=ps, tp=tp, po=po: e.transpose(
                                               pbt[:, po + tp * 128:po + (tp + 1) * 128],
                                               ps[0:128, tp * 128:tp * 128 + 128],
                                               idn[0:128, 0:128]),
                                           wait=([("v", max(st_p[k1], pbt_free))]
                                                 if tp == 0 else ()),
                                           inc=(p.i1("t") if tp == ki["ntp"] - 1 else ()))
                                st_tp[k1] = p.val("t")
                                pslfree[k1 % 2] = st_tp[k1]
                        t_av = st_av[nkt - 1]
                        zro = qpar * 128
                        p.emit("vector",
                               lambda e, zro=zro, bAV=bAV: e.reciprocal(
                                   zrowf_sb[0:1, zro:zro + 128], pb[bAV][64:65, 0:128]),
                               wait=[("t", t_av)], inc=p.i1("v"))
                        p.emit("vector",
                               lambda e, zro=zro: e.tensor_copy(
                                   zrow_sb[0:1, zro:zro + 128], zrowf_sb[0:1, zro:zro + 128]),
                               wait=[VW()], inc=p.i1("v"))
                        v_z = p.val("v")
                        bZ = bSs[0]
                        p.emit("tensor",
                               lambda e, zro=zro, bZ=bZ: e.matmul(
                                   pb[bZ][0:64, 0:128], onesr_bf[0:1, 0:64],
                                   zrow_sb[0:1, zro:zro + 128], start=True, stop=True),
                               wait=[("v", v_z)] + list(bankfree[bZ]), inc=p.i1("t"))
                        t_zb = p.val("t")
                        p.emit("vector",
                               lambda e, zro=zro, bZ=bZ: e.tensor_copy(
                                   zbs0[0:64, zro:zro + 128], pb[bZ][0:64, 0:128]),
                               wait=[("t", t_zb)], inc=p.i1("v"))
                        v_zc = p.val("v")
                        bankfree[bZ] = [("v", v_zc)]
                        p.emit("vector",
                               lambda e, qt=qt, zro=zro, bAV=bAV: e.tensor_mul(
                                   auh[0:64, qt * 128:(qt + 1) * 128],
                                   pb[bAV][0:64, 0:128], zbs0[0:64, zro:zro + 128]),
                               wait=[("v", v_zc)] +
                                    ([("a0", p.val("a0"))] if qt <= 1 else []),
                               inc=p.i1("v"))
                        avfree[bAV] = [("v", p.val("v"))]
                    v_au = p.val("v")
                    p.dma("scalar",
                          bass.AP(cc2_in, (h % 8) * 64 * 4096, [[4096, 64], [1, 4096]]),
                          auh[0:64, 0:4096], "a0",
                          wait=[("v", v_au), ("a0", p.val("a0"))])
            u_au = p.val("a0")

            p.emit("gpsimd",
                   lambda e: e.collective_compute(
                       "AllGather", ALU.bypass,
                       replica_groups=[[0, 1], [2, 3], [4, 5], [6, 7]],
                       ins=[cc2_in.ap().opt()], outs=[cc2_out.ap().opt()]),
                   wait=[("a0", u_au)], inc=p.i1("c"))
            c_au = p.val("c")

            ar.reset(mark0)
            auo = ar.take(4096)
            auoB = ar.take(4096)
            wvv = ar.take(8192)
            p.dma("gpsimd", wvv[0:128, 0:8192],
                  bass.AP(wo0, 0, [[1024, 128], [128 * 1024, 8], [1, 1024]]), "dG",
                  wait=[TW(), VW(), SW()])
            dG_w0 = p.val("dG")
            p.emit("scalar", lambda e: e.copy(bvb_sb[:, 0:8], bvec_sb[:, 8:16]),
                   wait=[("dS", dS_b0), SW()], inc=p.i1("s"))
            s_bv = p.val("s")
            for oc in range(NCK):
                for ck in range(NCK):
                    p.emit("tensor",
                           lambda e, oc=oc, ck=ck: e.matmul(
                               pb[6][:, oc:oc + 1],
                               wvv[0:128, ck * 1024 + oc * 128:ck * 1024 + oc * 128 + 128],
                               bvb_sb[:, ck:ck + 1],
                               start=(ck == 0), stop=(ck == NCK - 1)),
                           wait=([("dG", dG_w0), ("s", s_bv), VW()] if (oc == 0 and ck == 0) else ()),
                           inc=(p.i1("t") if (oc == NCK - 1 and ck == NCK - 1) else ()))
            t_bc0 = p.val("t")
            p.emit("vector",
                   lambda e: e.tensor_add(bvec2_sb[:, 0:8], pb[6][:, 0:8], bvec_sb[:, 16:24]),
                   wait=[("t", t_bc0)], inc=p.i1("v"))
            v_bc0 = p.val("v")
            p.dma("sync", bvec_sb[:, 30:32], c_half.ap(), "dS",
                  wait=[("dS", p.val("dS"))])
            dS_hm = p.val("dS")
            for tsl in range(4):
                p.dma("scalar", auoB[:],
                      bass.AP(cc2_out, tsl * TS, [[4096, 128], [128 * 4096, 8], [1, TS]]),
                      "dA", wait=[("c", c_au), TW(), VW(), ("dA", p.val("dA"))])
                p.dma("scalar", auo[:],
                      bass.AP(cc2_out, 2048 + tsl * TS, [[4096, 128], [128 * 4096, 8], [1, TS]]),
                      "dA", wait=[("dA", p.val("dA"))])
                dA_both = p.val("dA")
                p.emit("vector",
                       lambda e: e.tensor_scalar(auo[:], auo[:], bvec_sb[:, 31:32],
                                                 None, ALU.mult),
                       wait=[("dA", dA_both), ("dS", dS_hm), VW()], inc=p.i1("v"))
                p.emit("vector",
                       lambda e: e.scalar_tensor_tensor(auo[:], auoB[:],
                                                        bvec_sb[:, 30:31], auo[:],
                                                        ALU.mult, ALU.add),
                       wait=[VW()], inc=p.i1("v"))
                g_au = ("v", p.val("v"))
                for oc in range(NCK):
                    bank = pb[oc % 2]
                    for ck in range(NCK):
                        p.emit("tensor",
                               lambda e, tsl=tsl, oc=oc, ck=ck, bank=bank: e.matmul(
                                   bank[:, :],
                                   wvv[0:128, ck * 1024 + oc * 128:ck * 1024 + oc * 128 + 128],
                                   auo[:, ck * TS:(ck + 1) * TS],
                                   start=(ck == 0), stop=(ck == NCK - 1)),
                               wait=([g_au, ("v", v_bc0), VW()] if ck == 0 else ()),
                               inc=(p.i1("t") if ck == NCK - 1 else ()))
                    t_o = p.val("t")
                    co = oc * T + tsl * TS
                    p.emit("vector",
                           lambda e, co=co, oc=oc, bank=bank: e.scalar_tensor_tensor(
                               x_res[:, co:co + TS], bank[:, :],
                               bvec2_sb[:, oc:oc + 1],
                               x_res[:, co:co + TS], ALU.add, ALU.add),
                           wait=[("t", t_o)], inc=p.i1("v"))
            p.dma("scalar", au0_sb[0:128, 0:8],
                  bass.AP(cc2_out, 0, [[4096, 128], [128 * 4096, 8], [1, 1]]),
                  "dA", wait=[("c", c_au), ("dA", p.val("dA"))],
                  allow_slow_non_contiguous=True)
            dA_au0 = p.val("dA")
            for oc in range(NCK):
                for ck in range(NCK):
                    p.emit("tensor",
                           lambda e, oc=oc, ck=ck: e.matmul(
                               pb[6][:, 8 + oc:9 + oc],
                               wvv[0:128, ck * 1024 + oc * 128:ck * 1024 + oc * 128 + 128],
                               au0_sb[:, ck:ck + 1],
                               start=(ck == 0), stop=(ck == NCK - 1)),
                           wait=([("dA", dA_au0), VW()] if (oc == 0 and ck == 0) else ()),
                           inc=(p.i1("t") if (oc == NCK - 1 and ck == NCK - 1) else ()))
            t_x0 = p.val("t")
            p.emit("vector",
                   lambda e: e.tensor_add(x0_sb[:], x0_sb[:], pb[6][:, 8:16]),
                   wait=[("t", t_x0), VW()], inc=p.i1("v"))
            p.emit("vector",
                   lambda e: e.tensor_add(x0_sb[:], x0_sb[:], bvec2_sb[:, 0:8]),
                   wait=[VW()], inc=p.i1("v"))

        def phase_l0ffn(track_x0=False):
            emit_gates(ready=[("dS", dS_all), ("v", p.val("v")), ("s", p.val("s")),
                              ("t", p.val("t"))])
            emit_layernorm(ready=[("dS", dS_all), ("v", p.val("v"))])
            emit_l0_ffn()
            if track_x0:
                emit_x0_track()

        def phase_l0attn():
            emit_layernorm(ready=[("v", p.val("v")), ("s", p.val("s")),
                                  ("t", p.val("t")), ("dS", dS_all)])
            emit_l0_attn()

        def phase_l1ffn():
            emit_layernorm(ready=[("v", p.val("v")), ("s", p.val("s")),
                                  ("t", p.val("t")), ("dS", dS_all)])
            emit_l1_ffn()

        def phase_l1attn():
            emit_layernorm(ready=[("v", p.val("v")), ("s", p.val("s")),
                                  ("t", p.val("t")), ("dS", dS_all)])
            emit_l1_attn()

        phase_fns = {"l0ffn": phase_l0ffn, "l1ffn": phase_l1ffn,
                     "l1attn": phase_l1attn, "l0attn": phase_l0attn}
        phases_used = list(PHASES[:n_sub])
        for pi, ph in enumerate(phases_used):
            if ph == "l0ffn" and "l0attn" in phases_used[pi + 1:]:
                phase_fns[ph](track_x0=True)
            else:
                phase_fns[ph]()

        v_done = p.val("v")
        s_done = p.val("s")
        t_done = p.val("t")
        for ck in range(NCK):
            p.dma("sync", bass.AP(y_out, ck * 128 * T, [[T, 128], [1, T]]),
                  x_res[:, ck * T:(ck + 1) * T], "st",
                  wait=[("v", v_done), ("s", s_done), ("t", t_done)] if ck == 0 else ())
        st_all = p.val("st")
        p.emit("sync", lambda e: e.nop(), wait=[("st", st_all)])

        with nc.Block() as block:
            @block.sync
            def _(e):
                for f in p.ops["sync"]:
                    f(e)

            @block.vector
            def _(e):
                for f in p.ops["vector"]:
                    f(e)

            @block.scalar
            def _(e):
                for f in p.ops["scalar"]:
                    f(e)

            @block.tensor
            def _(e):
                for f in p.ops["tensor"]:
                    f(e)

            @block.gpsimd
            def _(e):
                for f in p.ops["gpsimd"]:
                    f(e)

    return nc


def _prep_inputs(ins):
    x = np.asarray(ins["L0_attn_x"], dtype=np.float32)
    maps = []
    oh8 = np.zeros((8, 128), dtype=np.float32)
    for e in range(8):
        oh8[e, e * 16:(e + 1) * 16] = 1.0
    wg = np.asarray(ins["L1_ffn_w_gate"])
    wu = np.asarray(ins["L1_ffn_w_up"])
    wd = np.asarray(ins["L1_ffn_w_down"])
    wg_r = np.ascontiguousarray(wg.reshape(24, 128, 8, 128).transpose(0, 2, 3, 1))
    wu_r = np.ascontiguousarray(wu.reshape(24, 128, 8, 128).transpose(0, 2, 3, 1))
    wd_r = np.ascontiguousarray(wd.reshape(8, 128, 24, 128).transpose(0, 2, 3, 1))
    rw_t = np.ascontiguousarray(np.asarray(ins["L0_ffn_router_w"]).T)
    ldn = np.ascontiguousarray(
        np.asarray(ins["L0_ffn_lora_down"]).transpose(2, 0, 1).reshape(D, 128))
    lup = np.ascontiguousarray(
        np.asarray(ins["L0_ffn_lora_up"]).transpose(0, 2, 1).reshape(128, D))
    g = {k: np.asarray(v) for k, v in ins.items()}
    H, DH, DR, SD = 16, 64, 16, 48
    bkf_flat = np.zeros(1024, np.float32)
    bqf_flat = np.zeros(1024, np.float32)
    for h in range(H):
        bkf_flat[h * 64:h * 64 + 48] = g["L1_attn_b_uk"][h * 48:(h + 1) * 48]
        bqf_flat[h * 64:h * 64 + 48] = g["L1_attn_b_uq"][h * 48:(h + 1) * 48]
    invf = (1.0 / 10000.0 ** (np.arange(0, 4, dtype=np.float64) / 4.0))
    invf_row = np.zeros(128, np.float32)
    sign_row = np.zeros(128, np.float32)
    for p_ in range(128):
        d = p_ % 16
        if d < 8:
            invf_row[p_] = np.float32(invf[d % 4] / 40.0)
            sign_row[p_] = -1.0 if d < 4 else 1.0
    perm = np.zeros((128, 128), np.float32)
    for r in range(128):
        d = r % 16
        if d < 4:
            perm[r, r + 4] = 1.0
        elif d < 8:
            perm[r, r - 4] = 1.0
    perm_t = np.ascontiguousarray(perm.T).astype(ml_dtypes.bfloat16)
    l0c = {
        "wo0": g["L0_attn_Wo"],
        "rel0": g["L0_attn_rel_key"],
        "bvf0": np.ascontiguousarray(g["L0_attn_bv"].reshape(8, 128).T),
        "bo0": np.ascontiguousarray(g["L0_attn_bo"].reshape(8, 128).T),
        "c_ident": np.eye(128, dtype=np.float32).astype(ml_dtypes.bfloat16),
    }
    l1c = {
        "wdkv": g["L1_attn_W_dkv"], "wdq": g["L1_attn_W_dq"],
        "wuk": g["L1_attn_W_uk"], "wuv": g["L1_attn_W_uv"],
        "wuq": g["L1_attn_W_uq"], "wqr": g["L1_attn_W_qr"],
        "wkr": g["L1_attn_W_kr"], "wo1": g["L1_attn_W_o"],
        "bdkv": np.ascontiguousarray(g["L1_attn_b_dkv"].reshape(128, 1)),
        "bdq": np.ascontiguousarray(g["L1_attn_b_dq"].reshape(128, 1)),
        "bkf": np.ascontiguousarray(bkf_flat.reshape(8, 128).T),
        "bqf": np.ascontiguousarray(bqf_flat.reshape(8, 128).T),
        "bkr_c": np.ascontiguousarray(g["L1_attn_b_kr"].reshape(2, 128).T),
        "bqr_c": np.ascontiguousarray(g["L1_attn_b_qr"].reshape(2, 128).T),
        "buv_c": np.ascontiguousarray(g["L1_attn_b_uv"].reshape(8, 128).T),
        "bo1_c": np.ascontiguousarray(g["L1_attn_b_o"].reshape(8, 128).T),
        "c_invf": invf_row.reshape(128, 1),
        "c_sign": sign_row.reshape(128, 1),
        "c_perm": perm_t,
    }
    for c in range(NC):
        b, s = c // 2, c % 2
        im = {
            "x_in": np.ascontiguousarray(x[b, s * T:(s + 1) * T, :].T),
            "x0row": np.ascontiguousarray(x[b, 0, :].reshape(8, 128).T),
            "rw_t": rw_t,
            "ldn": ldn,
            "lup": lup,
            "c_oh8": oh8,
            "wg_r": wg_r,
            "wu_r": wu_r,
            "wd_r": wd_r,
            "c_qoff": np.array([[s * 2048]], dtype=np.float32),
            "c_half": np.ascontiguousarray(
                np.stack([np.full(128, 1.0 - s, np.float32),
                          np.full(128, float(s), np.float32)], 1)),
            "wq0": np.ascontiguousarray(g["L0_attn_Wq"][:, s * 512:(s + 1) * 512]),
            "wk0": np.ascontiguousarray(g["L0_attn_Wk"][:, s * 512:(s + 1) * 512]),
            "wv0": np.ascontiguousarray(g["L0_attn_Wv"][:, s * 512:(s + 1) * 512]),
            "bq0": np.ascontiguousarray(g["L0_attn_bq"][s * 512:(s + 1) * 512].reshape(4, 128).T),
            "bk0": np.ascontiguousarray(g["L0_attn_bk"][s * 512:(s + 1) * 512].reshape(4, 128).T),
        }
        im.update(l1c)
        im.update(l0c)
        maps.append(im)
    return maps


def kernel(**ins):
    maps = _prep_inputs(ins)
    nc = build_program(N_SUB)
    res = run_bass_kernel_spmd(nc, maps, core_ids=list(range(NC)))
    out = np.zeros((BATCH, S, D), dtype=np.float32)
    g = {k: np.asarray(v) for k, v in ins.items()}
    H, DH, DR, SD = 16, 64, 16, 48
    bkf_flat = np.zeros(1024, np.float32)
    bqf_flat = np.zeros(1024, np.float32)
    for h in range(H):
        bkf_flat[h * 64:h * 64 + 48] = g["L1_attn_b_uk"][h * 48:(h + 1) * 48]
        bqf_flat[h * 64:h * 64 + 48] = g["L1_attn_b_uq"][h * 48:(h + 1) * 48]
    invf = (1.0 / 10000.0 ** (np.arange(0, 4, dtype=np.float64) / 4.0))
    invf_row = np.zeros(128, np.float32)
    sign_row = np.zeros(128, np.float32)
    for p_ in range(128):
        d = p_ % 16
        if d < 8:
            invf_row[p_] = np.float32(invf[d % 4] / 40.0)
            sign_row[p_] = -1.0 if d < 4 else 1.0
    perm = np.zeros((128, 128), np.float32)
    for r in range(128):
        d = r % 16
        if d < 4:
            perm[r, r + 4] = 1.0
        elif d < 8:
            perm[r, r - 4] = 1.0
    perm_t = np.ascontiguousarray(perm.T).astype(ml_dtypes.bfloat16)
    l0c = {
        "wo0": g["L0_attn_Wo"],
        "rel0": g["L0_attn_rel_key"],
        "bvf0": np.ascontiguousarray(g["L0_attn_bv"].reshape(8, 128).T),
        "bo0": np.ascontiguousarray(g["L0_attn_bo"].reshape(8, 128).T),
        "c_ident": np.eye(128, dtype=np.float32).astype(ml_dtypes.bfloat16),
    }
    l1c = {
        "wdkv": g["L1_attn_W_dkv"], "wdq": g["L1_attn_W_dq"],
        "wuk": g["L1_attn_W_uk"], "wuv": g["L1_attn_W_uv"],
        "wuq": g["L1_attn_W_uq"], "wqr": g["L1_attn_W_qr"],
        "wkr": g["L1_attn_W_kr"], "wo1": g["L1_attn_W_o"],
        "bdkv": np.ascontiguousarray(g["L1_attn_b_dkv"].reshape(128, 1)),
        "bdq": np.ascontiguousarray(g["L1_attn_b_dq"].reshape(128, 1)),
        "bkf": np.ascontiguousarray(bkf_flat.reshape(8, 128).T),
        "bqf": np.ascontiguousarray(bqf_flat.reshape(8, 128).T),
        "bkr_c": np.ascontiguousarray(g["L1_attn_b_kr"].reshape(2, 128).T),
        "bqr_c": np.ascontiguousarray(g["L1_attn_b_qr"].reshape(2, 128).T),
        "buv_c": np.ascontiguousarray(g["L1_attn_b_uv"].reshape(8, 128).T),
        "bo1_c": np.ascontiguousarray(g["L1_attn_b_o"].reshape(8, 128).T),
        "c_invf": invf_row.reshape(128, 1),
        "c_sign": sign_row.reshape(128, 1),
        "c_perm": perm_t,
    }
    for c in range(NC):
        b, s = c // 2, c % 2
        out[b, s * T:(s + 1) * T, :] = res.results[c]["y_out"].T
    return out
